# version 02
# baseline (speedup 1.0000x reference)
import contextlib
import math

import numpy as np
import ml_dtypes

import concourse.bass as bass
import concourse.mybir as mybir
from concourse.bass_utils import run_bass_kernel_spmd

F32 = mybir.dt.float32
BF16 = mybir.dt.bfloat16
I32 = mybir.dt.int32
AF = mybir.ActivationFunctionType
OP = mybir.AluOpType

NCORES = 8
T = 1024
D = 2048
NDC = 16
NH = 16
EQKV = 6144
INTER = 5504
NIC = 43
SCALE_ATT = 1.0 / math.sqrt(128.0)
EXP_BIAS = 40.0
EPS = 1e-5

GSH_GATE = 256 * INTER
GSH_DOWN = INTER * 256
GH_BLK = 2 * GSH_GATE + GSH_DOWN
KV_BLK = 2 * D * T + 16

GJOBS_V = ([("gate", s) for s in range(16)] + [("up", s) for s in range(16)]
           + [("down", s) for s in range(11)])
NGV = len(GJOBS_V)


class Ch:
    def __init__(self, ctx, nc, name):
        self.sem = ctx.enter_context(nc.semaphore(name))
        self.n = 0

    def inc(self, instr, amt=1):
        instr.then_inc(self.sem, amt)
        self.n += amt
        return self.n

    def dinc(self, instr):
        return self.inc(instr, 16)

    def wait(self, eng, n=None):
        eng.wait_ge(self.sem, self.n if n is None else n)


class Drained:
    _COMPUTE = {"tensor_copy", "tensor_tensor", "tensor_scalar",
                "scalar_tensor_tensor", "memset", "reciprocal", "activation",
                "square", "sqrt", "copy", "mul", "add", "select", "iota",
                "tensor_reduce", "reduce_max", "reduce_sum", "dma_start"}

    def __init__(self, eng):
        self._e = eng

    def __getattr__(self, name):
        f = getattr(self._e, name)
        if name in Drained._COMPUTE:
            def wrap(*a, **k):
                self._e.drain()
                return f(*a, **k)
            return wrap
        return f


def build_nc(stage=5):
    nc = bass.Bass("TRN2", target_bir_lowering=False,
                   detect_race_conditions=False)

    xT = nc.dram_tensor("xT", [D, T], F32, kind="ExternalInput").ap()
    wqkvT = nc.dram_tensor("wqkvT", [D, EQKV], F32, kind="ExternalInput").ap()
    wprojT = nc.dram_tensor("wprojT", [D, D], F32, kind="ExternalInput").ap()
    lutb = nc.dram_tensor("lutb", [128, 16], F32, kind="ExternalInput").ap()
    gidxT = nc.dram_tensor("gidxT", [256, INTER], I32, kind="ExternalInput").ap()
    uidxT = nc.dram_tensor("uidxT", [256, INTER], I32, kind="ExternalInput").ap()
    didxT = nc.dram_tensor("didxT", [INTER, 256], I32, kind="ExternalInput").ap()
    outT = nc.dram_tensor("outT", [D, T], F32, kind="ExternalOutput").ap()

    dbg = {}
    if stage < 5:
        def _dt(name, shape, dtyp):
            dbg[name] = nc.dram_tensor(name, shape, dtyp, kind="ExternalOutput").ap()
        _dt("d_q", [D, T], BF16)
        _dt("d_k", [D, T], BF16)
        _dt("d_v", [T, D], BF16)
        _dt("d_stats", [2, T], F32)
        _dt("d_o", [D, T], BF16)
        if stage >= 3:
            _dt("d_x1", [D, T], F32)
            _dt("d_ln2", [D, T], BF16)
            _dt("d_proj", [D, T], BF16)
            _dt("d_rows", [2, T], F32)
        if stage >= 4:
            _dt("d_gw", [GH_BLK], BF16)
            _dt("d_h", [INTER, T], BF16)

    kv_in = nc.dram_tensor("kv_in", [KV_BLK], BF16, kind="Internal").ap()
    kv_out = nc.dram_tensor("kv_out", [NCORES * KV_BLK], BF16, kind="Internal").ap()
    gh_in = nc.dram_tensor("gh_in", [GH_BLK], BF16, kind="Internal").ap()
    gh_out = nc.dram_tensor("gh_out", [NCORES * GH_BLK], BF16, kind="Internal").ap()
    st_bnc = nc.dram_tensor("st_bnc", [2048], F32, kind="Internal").ap()
    x1_sc = nc.dram_tensor("x1_sc", [D, T], F32, kind="Internal").ap()
    proj_sc = nc.dram_tensor("proj_sc", [D, T], BF16, kind="Internal").ap()
    down_sc = nc.dram_tensor("down_sc", [D, T], BF16, kind="Internal").ap()

    kv_in_K = kv_in[0:D * T].rearrange("(e t) -> e t", t=T)
    kv_in_V = kv_in[D * T:2 * D * T].rearrange("(t e) -> t e", e=D)
    kv_in_N = kv_in[2 * D * T:2 * D * T + 16]
    kv_out_K = lambda r: kv_out[r * KV_BLK: r * KV_BLK + D * T].rearrange("(e t) -> e t", t=T)
    kv_out_V = lambda r: kv_out[r * KV_BLK + D * T: r * KV_BLK + 2 * D * T].rearrange("(t e) -> t e", e=D)
    gh_in_g = gh_in[0:GSH_GATE].rearrange("(d i) -> d i", i=INTER)
    gh_in_u = gh_in[GSH_GATE:2 * GSH_GATE].rearrange("(d i) -> d i", i=INTER)
    gh_in_d = gh_in[2 * GSH_GATE:].rearrange("(i e) -> i e", e=256)
    gh_out_g = lambda r: gh_out[r * GH_BLK: r * GH_BLK + GSH_GATE].rearrange("(d i) -> d i", i=INTER)
    gh_out_u = lambda r: gh_out[r * GH_BLK + GSH_GATE: r * GH_BLK + 2 * GSH_GATE].rearrange("(d i) -> d i", i=INTER)
    gh_out_d = lambda r: gh_out[r * GH_BLK + 2 * GSH_GATE: (r + 1) * GH_BLK].rearrange("(i e) -> i e", e=256)

    def ghost_job(kind, sl):
        if kind == "gate":
            return gidxT[:, sl * 344:(sl + 1) * 344], gh_in_g[:, sl * 344:(sl + 1) * 344], 2, 344
        if kind == "up":
            return uidxT[:, sl * 344:(sl + 1) * 344], gh_in_u[:, sl * 344:(sl + 1) * 344], 2, 344
        r0 = sl * 4 * 128
        nrc = 4 if sl < 10 else 3
        return didxT[r0:r0 + nrc * 128, :], gh_in_d[r0:r0 + nrc * 128, :], nrc, 256

    ctx = contextlib.ExitStack()
    with ctx:
        sbuf = lambda name, shape, dtp: ctx.enter_context(nc.sbuf_tensor(name, shape, dtp))
        consts = sbuf("consts", [128, 4], F32)
        ones_bf = sbuf("ones_bf", [128, 1], BF16)
        bc_lhs = sbuf("bc_lhs", [34, 128], F32)
        bc_rhs = sbuf("bc_rhs", [34, 1024], F32)
        lut_s = sbuf("lut_s", [128, 16], F32)
        x_bf = sbuf("x_bf", [128, NDC, T], BF16)
        xf = sbuf("xf", [128, 2, T], F32)
        sqc = sbuf("sqc", [128, 2, T], BF16)
        rows = sbuf("rows", [33, 1024], F32)
        rows2 = sbuf("rows2", [33, 1024], F32)
        bc_a = sbuf("bc_a", [128, T], F32)
        bc_b = sbuf("bc_b", [128, T], F32)
        rz_bc = sbuf("rz_bc", [128, T], F32)
        statsT = sbuf("statsT", [128, 16], F32)
        s_neg = sbuf("s_neg", [128, 48], F32)
        tmp_f = sbuf("tmp_f", [128, T], F32)
        ones2_bf = sbuf("ones2_bf", [2, 128], BF16)
        qnorm_s = sbuf("qnorm_s", [33, 8192], BF16)
        kmax_s = sbuf("kmax_s", [33, 16], F32)
        knrow = sbuf("knrow", [1, 16], BF16)
        kn8 = sbuf("kn8", [1, 128], BF16)

        ch = lambda name: Ch(ctx, nc, name)
        chp = lambda name, n: [Ch(ctx, nc, f"{name}{i}") for i in range(n)]
        d_lut = ch("d_lut")
        d_xp = chp("d_x", 2); mk_x = {}
        d_wp = chp("d_w", 2); mk_w = {}
        d_wv = ch("d_wv"); mk_wv = {}
        d_kvp = chp("d_kv", 2); mk_kv = {}
        d_gip = chp("d_gi", 3); mk_gi = {}
        d_bnc = ch("d_bnc")
        d_ksp = chp("d_ks", 2); mk_ks = {}
        d_vsp = chp("d_vs", 2); mk_vs = {}
        d_knt = ch("d_knt"); d_kn = ch("d_kn")
        d_ghp = chp("d_gh", 3); mk_gh = {}
        d_pjp = chp("d_pj", 2); mk_pj = {}
        d_x1p = chp("d_x1", 2); mk_x1 = {}
        d_dnp = chp("d_dn", 2); mk_dn = {}
        d_ldtp = chp("d_ldt", 2); mk_ldt = {}
        d_gwp = chp("d_gw", 2); mk_gw = {}
        mk_bc = {}
        d_lxp = chp("d_lx", 2); mk_lx = {}
        d_l5p = chp("d_l5", 2); mk_l5 = {}
        d_outp = chp("d_out", 2); mk_out = {}
        d_dbg = ch("d_dbg")
        v_cc = ch("v_cc"); v_xc = ch("v_xc"); v_wq = ch("v_wq"); v_wv = ch("v_wv")
        v_sx = ch("v_sx"); v_rowpre = ch("v_rowpre"); v_rowb = ch("v_rowb")
        v_bca = ch("v_bca"); v_bcb = ch("v_bcb")
        v_svst = ch("v_svst"); v_svbc = ch("v_svbc")
        v_epi = ch("v_epi"); v_vepi = ch("v_vepi")
        v_gc = ch("v_gc"); v_ghd = ch("v_ghd")
        s_k2 = ch("s_k2"); s_q2 = ch("s_q2"); s_qn = ch("s_qn")
        t_kn = ch("t_kn"); t_qn = ch("t_qn"); v_kn = ch("v_kn")
        v_mn = ch("v_mn")
        v_z = ch("v_z"); v_rz = ch("v_rz"); v_rzb = ch("v_rzb"); v_o = ch("v_o")
        v_w3 = ch("v_w3"); v_pe = ch("v_pe")
        v_x1 = ch("v_x1"); v_x1b = ch("v_x1b"); v_l2 = ch("v_l2")
        v_h = ch("v_h"); v_de = ch("v_de"); v_out = ch("v_out")
        s_sq = ch("s_sq"); s_exp = ch("s_exp"); s_si = ch("s_si")
        s_sqr = ch("s_sqr")
        s_sq2 = ch("s_sq2"); s_sq3 = ch("s_sq3"); s_sq4 = ch("s_sq4")
        t_stq = ch("t_stq"); t_qk = ch("t_qk"); t_s = ch("t_s")
        t_bc = ch("t_bc"); t_v = ch("t_v"); t_sv = ch("t_sv"); t_svb = ch("t_svb")
        t_smm = ch("t_smm"); t_pv = ch("t_pv"); t_z = ch("t_z"); t_rzb = ch("t_rzb")
        t_pj = ch("t_pj"); t_stq2 = ch("t_stq2")
        t_stq3 = ch("t_stq3")
        t_guu = ch("t_guu")
        t_dnu = ch("t_dnu")
        t_stq4 = ch("t_stq4")
        g_cc = ch("g_cc")
        bar = ch("bar")

        blk = ctx.enter_context(nc.Block())

        def bar_end(eng):
            eng.drain()
            eng.sem_inc(bar.sem, 1)

        def bar_wait(eng, phase):
            eng.wait_ge(bar.sem, 4 * phase)

        scA = contextlib.ExitStack()
        o_bf = scA.enter_context(nc.sbuf_tensor("o_bf", [128, NDC, T], BF16))
        gvi = scA.enter_context(nc.sbuf_tensor("gvi", [128, 3, 1024], I32))
        gvb = scA.enter_context(nc.sbuf_tensor("gvb", [128, 1024], BF16))
        gvm = scA.enter_context(nc.sbuf_tensor("gvm", [128, 1024], BF16))
        gvo = scA.enter_context(nc.sbuf_tensor("gvo", [128, 3, 1024], BF16))
        scQ = contextlib.ExitStack()
        qs = scQ.enter_context(nc.sbuf_tensor("qs", [128, NH, T], BF16))

        def ghost_spill_ap(j):
            kind, sl = GJOBS_V[j]
            _, dst, a, b = ghost_job(kind, sl)
            n = a * b
            oap = gvo[:, j % 3, 0:n].rearrange("p (a b) -> p a b", a=a)
            return dst.rearrange("(c p) i -> p c i", p=128), oap

        def ghost_build(eng, j):
            kind, sl = GJOBS_V[j]
            _, dst, a, b = ghost_job(kind, sl)
            n = a * b
            d_gip[j % 3].wait(eng, mk_gi[j])
            d_lut.wait(eng, 16)
            iap = gvi[:, j % 3, 0:n].rearrange("p (a b) -> p a b", a=a)
            bap = gvb[:, 0:n].rearrange("p (a b) -> p a b", a=a)
            map_ = gvm[:, 0:n].rearrange("p (a b) -> p a b", a=a)
            oap = gvo[:, j % 3, 0:n].rearrange("p (a b) -> p a b", a=a)
            if j >= 3:
                d_ghp[j % 3].wait(eng, mk_gh[j - 3])
            v_gc.inc(eng.tensor_copy(out=bap, in_=iap))
            for k in range(16):
                eng.tensor_scalar(out=map_, in0=bap, scalar1=float(k),
                                  scalar2=lut_s[:, k:k + 1], op0=OP.is_equal,
                                  op1=OP.mult)
                if k == 0:
                    eng.tensor_copy(out=oap, in_=map_)
                elif k < 15:
                    eng.scalar_tensor_tensor(out=oap, in0=map_, scalar=1.0,
                                             in1=oap, op0=OP.mult, op1=OP.add)
                else:
                    i = eng.scalar_tensor_tensor(out=oap, in0=map_, scalar=1.0,
                                                 in1=oap, op0=OP.mult, op1=OP.add)
            v_ghd.inc(i)

        p1q = contextlib.ExitStack()
        wst = p1q.enter_context(nc.sbuf_tensor("wst", [128, 2, NDC, 128], F32))
        ktmp = p1q.enter_context(nc.sbuf_tensor("ktmp", [128, 2, T], BF16))
        wbt = p1q.enter_context(nc.sbuf_tensor("wbt", [128, 2, NDC, 128], BF16))
        p1p = contextlib.ExitStack()
        qk_ps = p1p.enter_context(nc.psum_tensor("qk_ps", [128, 2, 1024], F32))
        st_ps = p1p.enter_context(nc.psum_tensor("st_ps", [128, 1024], F32))
        sA_ps = p1p.enter_context(nc.psum_tensor("sA_ps", [128, 1], F32))
        sB_ps = p1p.enter_context(nc.psum_tensor("sB_ps", [128, 1], F32))

        @blk.sync
        def _(sync):
            d_lut.dinc(sync.dma_start(out=lut_s[:], in_=lutb[:]))
            for dc in range(NDC):
                if dc >= 2:
                    v_xc.wait(sync, dc - 1)
                mk_x[dc] = d_xp[dc % 2].dinc(sync.dma_start(
                    out=xf[:, dc % 2, :],
                    in_=xT[dc * 128:(dc + 1) * 128, :]))
            for ec in range(32):
                if ec >= 2:
                    v_wq.wait(sync, ec - 1)
                for sub in range(4):
                    mk_w[ec] = d_wp[ec % 2].dinc(sync.dma_start(
                        out=wst[:, ec % 2, sub * 4:(sub + 1) * 4, :],
                        in_=wqkvT[sub * 512:(sub + 1) * 512,
                                  ec * 128:(ec + 1) * 128]
                        .rearrange("(c p) e -> p c e", p=128)))

        @blk.scalar
        def _(scalar):
            scalar = Drained(scalar)
            for dc in range(NDC):
                v_xc.wait(scalar, dc + 1)
                if dc >= 2:
                    t_stq.wait(scalar, dc - 1)
                s_sq.inc(scalar.square(out=sqc[:, dc % 2, :], in_=x_bf[:, dc, :]))
            v_rowpre.wait(scalar, 1)
            s_sqr.inc(scalar.sqrt(out=rows[0:1, :], in_=rows2[32:33, :]))
            v_rowb.wait(scalar, 1)
            d_bnc.dinc(scalar.dma_start(
                out=st_bnc[0:1024].rearrange("(a b) -> a b", a=1),
                in_=rows[0:1, :]))
            d_bnc.dinc(scalar.dma_start(
                out=st_bnc[1024:2048].rearrange("(a b) -> a b", a=1),
                in_=rows[32:33, :]))
            d_bnc.wait(scalar, 32)
            d_bnc.dinc(scalar.dma_start(
                out=statsT[:, 0:8],
                in_=st_bnc[0:1024].rearrange("(c p) -> p c", p=128),
                allow_slow_non_contiguous=True))
            d_bnc.dinc(scalar.dma_start(
                out=statsT[:, 8:16],
                in_=st_bnc[1024:2048].rearrange("(c p) -> p c", p=128),
                allow_slow_non_contiguous=True))
            for ec in range(16, 32):
                v_epi.wait(scalar, ec + 1)
                mk_ks[ec] = d_ksp[ec % 2].dinc(scalar.dma_start(
                    out=kv_in_K[(ec - 16) * 128:(ec - 15) * 128, :],
                    in_=ktmp[:, ec % 2, :]))
                if ec == 16:
                    t_stq.wait(scalar, NDC)
                if ec >= 18:
                    t_kn.wait(scalar, ec - 17)
                s_k2.inc(scalar.square(out=sqc[:, ec % 2, :], in_=ktmp[:, ec % 2, :]))
            v_kn.wait(scalar, 16)
            d_knt.dinc(scalar.dma_start(
                out=kv_in_N.rearrange("(a b) -> a b", a=1), in_=knrow[:]))
            for h in range(NH):
                if h == 0:
                    t_kn.wait(scalar, 16)
                if h >= 2:
                    t_qn.wait(scalar, h - 1)
                s_q2.inc(scalar.square(out=sqc[:, h % 2, :], in_=qs[:, h, :]))
                t_qn.wait(scalar, h + 1)
                pr = 32 * (h % 2)
                s_qn.inc(scalar.sqrt(
                    out=qnorm_s[32 * (h // 8):32 * (h // 8) + 1,
                                (h % 8) * 1024:(h % 8) * 1024 + 1024],
                    in_=st_ps[pr:pr + 1, :]))

        @blk.vector
        def _(vector):
            vector = Drained(vector)
            v_cc.inc(vector.memset(consts[:, 0:1], EXP_BIAS))
            v_cc.inc(vector.memset(consts[:, 1:2], SCALE_ATT))
            v_cc.inc(vector.memset(ones_bf[:], 1.0))
            v_cc.inc(vector.memset(bc_lhs[:], 0.0))
            v_cc.inc(vector.memset(bc_lhs[0:1, :], 1.0))
            v_cc.inc(vector.memset(bc_lhs[32:33, :], 1.0))
            v_cc.inc(vector.memset(bc_rhs[:], 0.0))
            v_cc.inc(vector.memset(ones2_bf[:], 0.0))
            v_cc.inc(vector.memset(ones2_bf[0:1, :], 1.0))
            for dc in range(NDC):
                d_xp[dc % 2].wait(vector, mk_x[dc])
                if dc >= 2:
                    t_stq.wait(vector, dc - 1)
                v_xc.inc(vector.tensor_copy(out=x_bf[:, dc, :], in_=xf[:, dc % 2, :]))
            for ec in range(36):
                if ec == 3:
                    t_stq.wait(vector, NDC)
                    inv = 1.0 / D
                    vector.tensor_scalar(out=rows2[0:1, :], in0=st_ps[0:1, :],
                                         scalar1=inv, scalar2=None, op0=OP.mult)
                    vector.tensor_scalar(out=rows[32:33, :], in0=st_ps[32:33, :],
                                         scalar1=inv, scalar2=EPS,
                                         op0=OP.mult, op1=OP.add)
                    vector.tensor_tensor(out=rows2[32:33, :], in0=rows2[0:1, :],
                                         in1=rows2[0:1, :], op=OP.mult)
                    vector.tensor_tensor(out=rows[32:33, :], in0=rows[32:33, :],
                                         in1=rows2[32:33, :], op=OP.subtract)
                    v_rowpre.inc(vector.reciprocal(out=rows2[32:33, :],
                                                   in_=rows[32:33, :]))
                    s_sqr.wait(vector, 1)
                    vector.tensor_tensor(out=rows[32:33, :], in0=rows2[0:1, :],
                                         in1=rows[0:1, :], op=OP.mult)
                    vector.tensor_copy(out=bc_rhs[0:1, :], in_=rows[0:1, :])
                    v_rowb.inc(vector.tensor_copy(out=bc_rhs[32:33, :],
                                                  in_=rows[32:33, :]))
                    t_bc.wait(vector, 2)
                    v_bca.inc(vector.tensor_copy(out=bc_a[:], in_=st_ps[:]))
                    t_bc.wait(vector, 4)
                    v_bcb.inc(vector.tensor_copy(out=bc_b[:], in_=st_ps[:]))
                if ec >= 4:
                    e1 = ec - 4
                    t_qk.wait(vector, e1 + 1)
                    v_sx.wait(vector, e1 + 1)
                    if e1 >= 18:
                        d_ksp[e1 % 2].wait(vector, mk_ks[e1 - 2])
                    vector.tensor_tensor(out=tmp_f[:], in0=qk_ps[:, e1 % 2, :],
                                         in1=bc_a[:], op=OP.mult)
                    dst = qs[:, e1, :] if e1 < 16 else ktmp[:, e1 % 2, :]
                    v_epi.inc(vector.scalar_tensor_tensor(
                        out=dst, in0=bc_b[:], scalar=s_neg[:, e1:e1 + 1],
                        in1=tmp_f[:], op0=OP.mult, op1=OP.add))
                if ec < 32:
                    d_wp[ec % 2].wait(vector, mk_w[ec])
                    if ec >= 2:
                        t_s.wait(vector, ec - 1)
                    v_wq.inc(vector.tensor_copy(out=wbt[:, ec % 2, :, :],
                                                in_=wst[:, ec % 2, :, :]))
                if 2 <= ec < 34:
                    e0 = ec - 2
                    t_s.wait(vector, e0 + 1)
                    sp = sA_ps if e0 % 2 == 0 else sB_ps
                    v_sx.inc(vector.tensor_scalar(
                        out=s_neg[:, e0:e0 + 1], in0=sp[:], scalar1=-1.0,
                        scalar2=None, op0=OP.mult))
                if 20 <= ec < 36:
                    e2 = ec - 4
                    t_kn.wait(vector, e2 - 15)
                    pr = 32 * (e2 % 2)
                    v_kn.inc(vector.reduce_max(
                        out=knrow[:, e2 - 16:e2 - 15], in_=st_ps[pr:pr + 1, :],
                        axis=mybir.AxisListType.X))
        @blk.tensor
        def _(tensor):
            v_cc.wait(tensor, 9)
            for dc in range(NDC):
                v_xc.wait(tensor, dc + 1)
                for half in range(2):
                    tensor.matmul(st_ps[0:1, half * 512:(half + 1) * 512],
                                  ones_bf[:],
                                  x_bf[:, dc, half * 512:(half + 1) * 512],
                                  start=(dc == 0), stop=(dc == NDC - 1))
                s_sq.wait(tensor, dc + 1)
                for half in range(2):
                    i = tensor.matmul(st_ps[32:33, half * 512:(half + 1) * 512],
                                      ones_bf[:],
                                      sqc[:, dc % 2, half * 512:(half + 1) * 512],
                                      start=(dc == 0), stop=(dc == NDC - 1))
                t_stq.inc(i)
            for ec in range(32):
                v_wq.wait(tensor, ec + 1)
                if ec == 2:
                    v_rowb.wait(tensor, 1)
                    for half in range(2):
                        i = tensor.matmul(
                            st_ps[:, half * 512:(half + 1) * 512], bc_lhs[0:2, :],
                            bc_rhs[0:2, half * 512:(half + 1) * 512],
                            start=True, stop=True)
                    t_bc.inc(i, 2)
                    v_bca.wait(tensor, 1)
                    for half in range(2):
                        i = tensor.matmul(
                            st_ps[:, half * 512:(half + 1) * 512], bc_lhs[32:34, :],
                            bc_rhs[32:34, half * 512:(half + 1) * 512],
                            start=True, stop=True)
                    t_bc.inc(i, 2)
                if ec >= 2:
                    v_epi.wait(tensor, ec - 1)
                    v_sx.wait(tensor, ec - 1)
                sp = sA_ps if ec % 2 == 0 else sB_ps
                for dc in range(NDC):
                    for half in range(2):
                        i = tensor.matmul(
                            qk_ps[:, ec % 2, half * 512:(half + 1) * 512],
                            wbt[:, ec % 2, dc, :],
                            x_bf[:, dc, half * 512:(half + 1) * 512],
                            start=(dc == 0), stop=(dc == NDC - 1))
                    j = tensor.matmul(sp[:], wbt[:, ec % 2, dc, :], ones_bf[:],
                                      start=(dc == 0), stop=(dc == NDC - 1))
                t_qk.inc(i)
                t_s.inc(j)
                if ec >= 18:
                    e = ec - 2
                    if e == 16:
                        v_bcb.wait(tensor, 1)
                    s_k2.wait(tensor, e - 15)
                    v_kn.wait(tensor, max(0, e - 17))
                    pr = 32 * (e % 2)
                    for half in range(2):
                        i = tensor.matmul(
                            st_ps[pr:pr + 1, half * 512:(half + 1) * 512],
                            ones_bf[:], sqc[:, e % 2, half * 512:(half + 1) * 512],
                            start=True, stop=True)
                    t_kn.inc(i)
            for e in (30, 31):
                s_k2.wait(tensor, e - 15)
                v_kn.wait(tensor, e - 17)
                pr = 32 * (e % 2)
                for half in range(2):
                    i = tensor.matmul(
                        st_ps[pr:pr + 1, half * 512:(half + 1) * 512],
                        ones_bf[:], sqc[:, e % 2, half * 512:(half + 1) * 512],
                        start=True, stop=True)
                t_kn.inc(i)
            for h in range(NH):
                s_q2.wait(tensor, h + 1)
                if h >= 2:
                    s_qn.wait(tensor, h - 1)
                pr = 32 * (h % 2)
                for half in range(2):
                    i = tensor.matmul(
                        st_ps[pr:pr + 1, half * 512:(half + 1) * 512],
                        ones_bf[:], sqc[:, h % 2, half * 512:(half + 1) * 512],
                        start=True, stop=True)
                t_qn.inc(i)

        p1q.close()

        p1v = contextlib.ExitStack()
        wstv = p1v.enter_context(nc.sbuf_tensor("wstv", [128, 4, 512], F32))
        vtmp = p1v.enter_context(nc.sbuf_tensor("vtmp", [128, 2, 512], BF16))
        wbv = p1v.enter_context(nc.sbuf_tensor("wbv", [128, NDC, 512], BF16))
        s_bc = p1v.enter_context(nc.sbuf_tensor("s_bc", [128, 2, 512], F32))

        @blk.sync
        def _(sync):
            for u in range(16):
                es, sub = divmod(u, 4)
                if u == 0:
                    v_wq.wait(sync, 32)
                if u >= 1:
                    v_wv.wait(sync, u)
                mk_wv[u] = d_wv.dinc(sync.dma_start(
                    out=wstv[:],
                    in_=wqkvT[sub * 512:(sub + 1) * 512,
                              4096 + es * 512:4096 + (es + 1) * 512]
                    .rearrange("(c p) e -> p c e", p=128)))
            bar_end(sync)

        @blk.scalar
        def _(scalar):
            scalar = Drained(scalar)
            for es in range(4):
                for tt in range(8):
                    un = es * 8 + tt
                    v_vepi.wait(scalar, un + 1)
                    mk_vs[un] = d_vsp[un % 2].dinc(scalar.dma_start(
                        out=kv_in_V[tt * 128:(tt + 1) * 128,
                                    es * 512:(es + 1) * 512],
                        in_=vtmp[:, un % 2, :]))
            bar_end(scalar)

        @blk.vector
        def _(vector):
            vector = Drained(vector)
            d_bnc.wait(vector, 64)
            for es in range(4):
                for sub in range(4):
                    u = es * 4 + sub
                    d_wv.wait(vector, mk_wv[u])
                    if u == 0:
                        t_s.wait(vector, 32)
                    if sub == 0 and es >= 1:
                        t_v.wait(vector, es * 8)
                    v_wv.inc(vector.tensor_copy(
                        out=wbv[:, sub * 4:(sub + 1) * 4, :], in_=wstv[:]))
                t_sv.wait(vector, es + 1)
                if es >= 1:
                    t_svb.wait(vector, es)
                srcp = st_ps[64:65, (es % 2) * 512:(es % 2 + 1) * 512]
                v_svst.inc(vector.tensor_copy(out=bc_rhs[0:1, 0:512], in_=srcp))
                t_svb.wait(vector, es + 1)
                dstb = st_ps[:, (es % 2) * 512:(es % 2 + 1) * 512]
                v_svbc.inc(vector.tensor_copy(
                    out=s_bc[:, es % 2, :], in_=dstb))
                for tt in range(8):
                    un = es * 8 + tt
                    t_v.wait(vector, un + 1)
                    if un >= 2:
                        d_vsp[un % 2].wait(vector, mk_vs[un - 2])
                    vector.tensor_scalar(
                        out=tmp_f[:, 0:512], in0=s_bc[:, es % 2, :],
                        scalar1=statsT[:, 8 + tt:9 + tt], scalar2=None, op0=OP.mult)
                    v_vepi.inc(vector.scalar_tensor_tensor(
                        out=vtmp[:, un % 2, :], in0=qk_ps[:, un % 2, 0:512],
                        scalar=statsT[:, tt:tt + 1], in1=tmp_f[:, 0:512],
                        op0=OP.mult, op1=OP.subtract))
            bar_end(vector)

        @blk.tensor
        def _(tensor):
            for es in range(4):
                v_wv.wait(tensor, es * 4 + 4)
                if es == 0:
                    s_qn.wait(tensor, NH)
                if es >= 2:
                    v_svbc.wait(tensor, es - 1)
                dstp = st_ps[64:65, (es % 2) * 512:(es % 2 + 1) * 512]
                for dc in range(NDC):
                    i = tensor.matmul(dstp, ones_bf[:], wbv[:, dc, :],
                                      start=(dc == 0), stop=(dc == NDC - 1))
                t_sv.inc(i)
                v_svst.wait(tensor, es + 1)
                i = tensor.matmul(st_ps[:, (es % 2) * 512:(es % 2 + 1) * 512],
                                  bc_lhs[0:2, :], bc_rhs[0:2, 0:512],
                                  start=True, stop=True)
                t_svb.inc(i)
                for tt in range(8):
                    un = es * 8 + tt
                    if un >= 2:
                        v_vepi.wait(tensor, un - 1)
                    for dc in range(NDC):
                        i = tensor.matmul(qk_ps[:, un % 2, 0:512],
                                          x_bf[:, dc, tt * 128:(tt + 1) * 128],
                                          wbv[:, dc, :],
                                          start=(dc == 0), stop=(dc == NDC - 1))
                    t_v.inc(i)
            bar_end(tensor)

        @blk.gpsimd
        def _(gpsimd):
            d_ksp[0].wait(gpsimd)
            d_ksp[1].wait(gpsimd)
            d_vsp[0].wait(gpsimd)
            d_vsp[1].wait(gpsimd)
            d_knt.wait(gpsimd)
            g_cc.inc(gpsimd.collective_compute(
                "AllGather", OP.bypass, ins=[kv_in[:]], outs=[kv_out[:]],
                replica_groups=[list(range(NCORES))]))

        p1v.close()
        p1p.close()

        p2s = contextlib.ExitStack()
        kst = p2s.enter_context(nc.sbuf_tensor("kst", [128, 2, T], BF16))
        vst = p2s.enter_context(nc.sbuf_tensor("vst", [128, 2, 8, 128], BF16))
        pst = p2s.enter_context(nc.sbuf_tensor("pst", [128, 2, 1024], BF16))
        mneg = p2s.enter_context(nc.sbuf_tensor("mneg", [2, 2, 1024], BF16))
        zacc = p2s.enter_context(nc.sbuf_tensor("zacc", [128, 2, 1024], BF16))
        p2p = contextlib.ExitStack()
        sat_ps = p2p.enter_context(nc.psum_tensor("sat_ps", [128, 2, 1024], F32))
        pv_ps = p2p.enter_context(nc.psum_tensor("pv_ps", [128, 1024], F32))
        z_ps = p2p.enter_context(nc.psum_tensor("z_ps", [128, 1024], F32))

        def _vghost_done_after_tail(h):
            return min(NGV, 3 * (h + 1) if h < 11 else 33 + 2 * (h - 10))

        @blk.sync
        def _(sync):
            bar_wait(sync, 1)
            gvj = 0
            for h in range(NH):
                for r in range(NCORES):
                    u = h * NCORES + r
                    if u == 0:
                        g_cc.wait(sync, 1)
                        d_kn.dinc(sync.dma_start(
                            out=kn8[:].rearrange("a (r h) -> a r h", r=8),
                            in_=kv_out[0:8 * KV_BLK].rearrange(
                                "(r x) -> r x", r=8)[:, 2 * D * T:2 * D * T + 16]))
                    if u >= 2:
                        t_pv.wait(sync, (u - 1) * 8)
                    d_kvp[u % 2].dinc(sync.dma_start(
                        out=kst[:, u % 2, :],
                        in_=kv_out_K(r)[h * 128:(h + 1) * 128, :]))
                    mk_kv[u] = d_kvp[u % 2].dinc(sync.dma_start(
                        out=vst[:, u % 2, :, :],
                        in_=kv_out_V(r)[:, h * 128:(h + 1) * 128]
                        .rearrange("(c p) e -> p c e", p=128)))
                    if u % 2 == 0 and gvj < _vghost_done_after_tail(h):
                        if gvj >= 3:
                            v_gc.wait(sync, gvj - 2)
                        kind, sl = GJOBS_V[gvj]
                        srcg, _, a, b = ghost_job(kind, sl)
                        mk_gi[gvj] = d_gip[gvj % 3].dinc(sync.dma_start(
                            out=gvi[:, gvj % 3, 0:a * b].rearrange(
                                "p (a b) -> p a b", a=a),
                            in_=srcg.rearrange("(c p) i -> p c i", p=128)))
                        gvj += 1
            bar_end(sync)

        @blk.tensor
        def _(tensor):
            bar_wait(tensor, 1)
            for h in range(NH):
                for j in range(65):
                    gj = h * 64 + j
                    if j < 64:
                        r, jj = divmod(j, 8)
                        u = h * NCORES + r
                        if jj == 0:
                            d_kvp[u % 2].wait(tensor, mk_kv[u])
                        if gj >= 2:
                            s_exp.wait(tensor, gj - 1)
                        if j == 0:
                            v_rzb.wait(tensor, h)
                            v_o.wait(tensor, h)
                        for half in range(2):
                            tensor.matmul(
                                sat_ps[:, gj % 2, half * 512:(half + 1) * 512],
                                kst[:, u % 2, jj * 128:(jj + 1) * 128],
                                qs[:, h, half * 512:(half + 1) * 512],
                                start=True, stop=False)
                        if j == 0:
                            v_mn.wait(tensor, max(2, h + 1))
                        for half in range(2):
                            i = tensor.matmul(
                                sat_ps[:, gj % 2, half * 512:(half + 1) * 512],
                                ones2_bf[:], mneg[0:2, h % 2, half * 512:(half + 1) * 512],
                                start=False, stop=True)
                        t_smm.inc(i)
                    if j >= 1:
                        jp = j - 1
                        gp = h * 64 + jp
                        rp, jjp = divmod(jp, 8)
                        up = h * NCORES + rp
                        s_exp.wait(tensor, gp + 1)
                        for half in range(2):
                            i = tensor.matmul(
                                pv_ps[:, half * 512:(half + 1) * 512],
                                vst[:, up % 2, jjp, :],
                                pst[:, gp % 2, half * 512:(half + 1) * 512],
                                start=(jp == 0), stop=(jp == 63))
                        t_pv.inc(i)
                v_z.wait(tensor, (h + 1) * 64)
                v_rz.wait(tensor, h)
                for half in range(2):
                    for p in range(2):
                        i = tensor.matmul(
                            z_ps[0:1, half * 512:(half + 1) * 512], ones_bf[:],
                            zacc[:, p, half * 512:(half + 1) * 512],
                            start=(p == 0), stop=(p == 1))
                t_z.inc(i)
                v_rz.wait(tensor, h + 1)
                for half in range(2):
                    i = tensor.matmul(sat_ps[:, 0, half * 512:(half + 1) * 512],
                                      bc_lhs[0:2, :],
                                      bc_rhs[0:2, half * 512:(half + 1) * 512],
                                      start=True, stop=True)
                t_rzb.inc(i)
            bar_end(tensor)

        @blk.scalar
        def _(scalar):
            scalar = Drained(scalar)
            bar_wait(scalar, 1)
            v_kn.wait(scalar, 17)
            scalar.sqrt(out=kmax_s[0:1, :], in_=kmax_s[0:1, :])
            s_qn.inc(scalar.mul(out=kmax_s[0:1, :], in_=kmax_s[0:1, :], mul=-1.0))
            spilled = 0
            for h in range(NH):
                if h >= 1:
                    tgt = _vghost_done_after_tail(h - 1)
                    while spilled < tgt:
                        v_ghd.wait(scalar, spilled + 1)
                        dsta, oap = ghost_spill_ap(spilled)
                        mk_gh[spilled] = d_ghp[spilled % 3].dinc(
                            scalar.dma_start(out=dsta, in_=oap))
                        spilled += 1
                for j in range(64):
                    gj = h * 64 + j
                    t_smm.wait(scalar, gj + 1)
                    if gj >= 2:
                        t_pv.wait(scalar, gj - 1)
                        v_z.wait(scalar, gj - 1)
                    s_exp.inc(scalar.activation(
                        out=pst[:, gj % 2, :], in_=sat_ps[:, gj % 2, :],
                        func=AF.Exp, bias=consts[:, 0:1], scale=consts[:, 1:2]))
            while spilled < NGV:
                v_ghd.wait(scalar, spilled + 1)
                dsta, oap = ghost_spill_ap(spilled)
                mk_gh[spilled] = d_ghp[spilled % 3].dinc(
                    scalar.dma_start(out=dsta, in_=oap))
                spilled += 1
            bar_end(scalar)

        @blk.vector
        def _(vector):
            vector = Drained(vector)
            bar_wait(vector, 1)
            d_kn.wait(vector, 16)
            for h in range(NH):
                vector.reduce_max(
                    out=kmax_s[0:1, h:h + 1],
                    in_=kn8[:].rearrange("a (r h) -> a r h", r=8)[:, :, h],
                    axis=mybir.AxisListType.X)
            v_kn.inc(vector.tensor_copy(out=kmax_s[0:1, :], in_=kmax_s[0:1, :]))
            s_qn.wait(vector, NH + 1)
            vector.tensor_copy(out=kmax_s[32:33, :], in_=kmax_s[0:1, :])
            vector.memset(mneg[0:2, :, :], 0.0)
            for hh in range(2):
                vector.tensor_scalar(
                    out=mneg[0:1, hh, :],
                    in0=qnorm_s[32 * (hh // 8):32 * (hh // 8) + 1,
                                (hh % 8) * 1024:(hh % 8) * 1024 + 1024],
                    scalar1=kmax_s[32 * (hh // 8):32 * (hh // 8) + 1, hh:hh + 1],
                    scalar2=None, op0=OP.mult)
            v_mn.inc(vector.engine_nop() if False else vector.tensor_copy(
                out=mneg[0:1, 0, 0:1], in_=mneg[0:1, 0, 0:1]), 2)
            gvj = 0
            for h in range(NH):
                for j in range(64):
                    gj = h * 64 + j
                    s_exp.wait(vector, gj + 1)
                    if j < 2:
                        if h >= 1 and j == 0:
                            t_z.wait(vector, h)
                        v_z.inc(vector.tensor_copy(out=zacc[:, j, :],
                                                   in_=pst[:, gj % 2, :]))
                    else:
                        v_z.inc(vector.tensor_tensor(
                            out=zacc[:, j % 2, :], in0=pst[:, gj % 2, :],
                            in1=zacc[:, j % 2, :], op=OP.add))
                t_z.wait(vector, h + 1)
                v_rz.inc(vector.reciprocal(out=bc_rhs[0:1, :],
                                           in_=z_ps[0:1, :]))
                t_rzb.wait(vector, h + 1)
                v_rzb.inc(vector.tensor_copy(out=rz_bc[:], in_=sat_ps[:, 0, :]))
                v_o.inc(vector.tensor_tensor(out=o_bf[:, h, :], in0=pv_ps[:],
                                             in1=rz_bc[:], op=OP.mult))
                if h + 2 < NH:
                    hh = h + 2
                    t_smm.wait(vector, h * 64 + 64)
                    v_mn.inc(vector.tensor_scalar(
                        out=mneg[0:1, hh % 2, :],
                        in0=qnorm_s[32 * (hh // 8):32 * (hh // 8) + 1,
                                    (hh % 8) * 1024:(hh % 8) * 1024 + 1024],
                        scalar1=kmax_s[32 * (hh // 8):32 * (hh // 8) + 1, hh:hh + 1],
                        scalar2=None, op0=OP.mult))
                nj = 3 if h < 11 else 2
                for _ in range(nj):
                    if gvj < NGV:
                        ghost_build(vector, gvj)
                        gvj += 1
            while gvj < NGV:
                ghost_build(vector, gvj)
                gvj += 1
            bar_end(vector)

        @blk.gpsimd
        def _(gpsimd):
            d_ghp[0].wait(gpsimd)
            d_ghp[1].wait(gpsimd)
            d_ghp[2].wait(gpsimd)
            g_cc.inc(gpsimd.collective_compute(
                "AllGather", OP.bypass, ins=[gh_in[:]], outs=[gh_out[:]],
                replica_groups=[list(range(NCORES))]))

        p2s.close()
        p2p.close()
        scQ.close()

        if stage <= 2:
            @blk.sync
            def _(sync):
                bar_wait(sync, 2)
                g_cc.wait(sync, 2)
                d_dbg.dinc(sync.dma_start(out=dbg["d_q"].rearrange(
                    "(c p) t -> p c t", p=128), in_=qs[:]))
                d_dbg.dinc(sync.dma_start(out=dbg["d_k"][:], in_=kv_in_K[:]))
                d_dbg.dinc(sync.dma_start(out=dbg["d_v"][:], in_=kv_in_V[:]))
                d_dbg.dinc(sync.dma_start(out=dbg["d_stats"][0:1, :],
                                          in_=rows[0:1, :]))
                d_dbg.dinc(sync.dma_start(out=dbg["d_stats"][1:2, :],
                                          in_=rows[32:33, :]))
                d_dbg.dinc(sync.dma_start(out=dbg["d_o"].rearrange(
                    "(c p) t -> p c t", p=128), in_=o_bf[:]))
                d_outp[0].dinc(sync.dma_start(out=outT[0:128, :], in_=xf[:, 0, :]))
                d_dbg.wait(sync)
                d_outp[0].wait(sync)
            scA.close()
            return nc, dbg, ctx

        p3s = contextlib.ExitStack()
        wst3 = p3s.enter_context(nc.sbuf_tensor("wst3", [128, 2, NDC, 128], F32))
        wbt3 = p3s.enter_context(nc.sbuf_tensor("wbt3", [128, 2, NDC, 128], BF16))
        pjt = p3s.enter_context(nc.sbuf_tensor("pjt", [128, 2, T], BF16))
        x1t = p3s.enter_context(nc.sbuf_tensor("x1t", [128, 2, T], F32))
        x1b = p3s.enter_context(nc.sbuf_tensor("x1b", [128, 2, T], BF16))
        ldt = p3s.enter_context(nc.sbuf_tensor("ldt", [128, 2, T], BF16))
        p3p = contextlib.ExitStack()
        st2_ps = p3p.enter_context(nc.psum_tensor("st2_ps", [128, 1024], F32))
        pj_ps = p3p.enter_context(nc.psum_tensor("pj_ps", [128, 2, 1024], F32))

        @blk.scalar
        def _(scalar):
            scalar = Drained(scalar)
            bar_wait(scalar, 2)
            for ec in range(NDC):
                v_pe.wait(scalar, ec + 1)
                if ec >= 2:
                    t_stq2.wait(scalar, ec - 1)
                s_sq2.inc(scalar.square(out=sqc[:, ec % 2, :], in_=pjt[:, ec % 2, :]))
                mk_pj[ec] = d_pjp[ec % 2].dinc(scalar.dma_start(
                    out=proj_sc[ec * 128:(ec + 1) * 128, :], in_=pjt[:, ec % 2, :]))
            v_rowpre.wait(scalar, 2)
            s_sqr.inc(scalar.sqrt(out=rows[0:1, :], in_=rows2[32:33, :]))
            for ec in range(NDC):
                v_x1.wait(scalar, ec + 1)
                mk_x1[ec] = d_x1p[ec % 2].dinc(scalar.dma_start(
                    out=x1_sc[ec * 128:(ec + 1) * 128, :], in_=x1t[:, ec % 2, :]))
                v_x1b.wait(scalar, ec + 1)
                if ec >= 2:
                    t_stq3.wait(scalar, ec - 1)
                s_sq3.inc(scalar.square(out=sqc[:, ec % 2, :], in_=x1b[:, ec % 2, :]))
            v_rowpre.wait(scalar, 3)
            s_sqr.inc(scalar.sqrt(out=rows[0:1, :], in_=rows2[32:33, :]))
            bar_end(scalar)

        @blk.sync
        def _(sync):
            bar_wait(sync, 2)
            for ec in range(NDC):
                if ec >= 2:
                    v_w3.wait(sync, ec - 1)
                for sub in range(4):
                    mk_w[("p3", ec)] = d_wp[ec % 2].dinc(sync.dma_start(
                        out=wst3[:, ec % 2, sub * 4:(sub + 1) * 4, :],
                        in_=wprojT[sub * 512:(sub + 1) * 512,
                                   ec * 128:(ec + 1) * 128]
                        .rearrange("(c p) e -> p c e", p=128)))
            for ec in range(NDC):
                if ec >= 2:
                    v_x1.wait(sync, ec - 1)
                mk_x[("p3", ec)] = d_xp[ec % 2].dinc(sync.dma_start(
                    out=xf[:, ec % 2, :],
                    in_=xT[ec * 128:(ec + 1) * 128, :]))
                d_pjp[ec % 2].wait(sync, mk_pj[ec])
                mk_ldt[ec] = d_ldtp[ec % 2].dinc(sync.dma_start(
                    out=ldt[:, ec % 2, :],
                    in_=proj_sc[ec * 128:(ec + 1) * 128, :]))
            d_x1p[0].wait(sync)
            d_x1p[1].wait(sync)
            for ec in range(NDC):
                if ec >= 2:
                    v_l2.wait(sync, ec - 1)
                mk_lx[ec] = d_lxp[ec % 2].dinc(sync.dma_start(
                    out=xf[:, ec % 2, :],
                    in_=x1_sc[ec * 128:(ec + 1) * 128, :]))
            bar_end(sync)

        @blk.tensor
        def _(tensor):
            bar_wait(tensor, 2)
            for ec in range(NDC):
                v_w3.wait(tensor, ec + 1)
                if ec >= 2:
                    v_pe.wait(tensor, ec - 1)
                for dc in range(NDC):
                    for half in range(2):
                        i = tensor.matmul(
                            pj_ps[:, ec % 2, half * 512:(half + 1) * 512],
                            wbt3[:, ec % 2, dc, :],
                            o_bf[:, dc, half * 512:(half + 1) * 512],
                            start=(dc == 0), stop=(dc == NDC - 1))
                t_pj.inc(i)
                v_pe.wait(tensor, ec + 1)
                for half in range(2):
                    tensor.matmul(st2_ps[0:1, half * 512:(half + 1) * 512],
                                  ones_bf[:],
                                  pjt[:, ec % 2, half * 512:(half + 1) * 512],
                                  start=(ec == 0), stop=(ec == NDC - 1))
                s_sq2.wait(tensor, ec + 1)
                for half in range(2):
                    i = tensor.matmul(st2_ps[32:33, half * 512:(half + 1) * 512],
                                      ones_bf[:],
                                      sqc[:, ec % 2, half * 512:(half + 1) * 512],
                                      start=(ec == 0), stop=(ec == NDC - 1))
                t_stq2.inc(i)
            v_rowb.wait(tensor, 2)
            for half in range(2):
                i = tensor.matmul(st2_ps[:, half * 512:(half + 1) * 512],
                                  bc_lhs[0:2, :], bc_rhs[0:2, half * 512:(half + 1) * 512],
                                  start=True, stop=True)
            t_bc.inc(i, 2)
            mk_bc[("pa", 0)] = t_bc.n
            v_bca.wait(tensor, 2)
            for half in range(2):
                i = tensor.matmul(st2_ps[:, half * 512:(half + 1) * 512],
                                  bc_lhs[32:34, :],
                                  bc_rhs[32:34, half * 512:(half + 1) * 512],
                                  start=True, stop=True)
            t_bc.inc(i, 2)
            mk_bc[("pa", 1)] = t_bc.n
            for ec in range(NDC):
                v_x1b.wait(tensor, ec + 1)
                for half in range(2):
                    tensor.matmul(st2_ps[0:1, half * 512:(half + 1) * 512],
                                  ones_bf[:],
                                  x1b[:, ec % 2, half * 512:(half + 1) * 512],
                                  start=(ec == 0), stop=(ec == NDC - 1))
                s_sq3.wait(tensor, ec + 1)
                for half in range(2):
                    i = tensor.matmul(st2_ps[32:33, half * 512:(half + 1) * 512],
                                      ones_bf[:],
                                      sqc[:, ec % 2, half * 512:(half + 1) * 512],
                                      start=(ec == 0), stop=(ec == NDC - 1))
                t_stq3.inc(i)
            v_rowb.wait(tensor, 3)
            for half in range(2):
                i = tensor.matmul(st2_ps[:, half * 512:(half + 1) * 512],
                                  bc_lhs[0:2, :], bc_rhs[0:2, half * 512:(half + 1) * 512],
                                  start=True, stop=True)
            t_bc.inc(i, 2)
            mk_bc[("l2", 0)] = t_bc.n
            v_bca.wait(tensor, 3)
            for half in range(2):
                i = tensor.matmul(st2_ps[:, half * 512:(half + 1) * 512],
                                  bc_lhs[32:34, :],
                                  bc_rhs[32:34, half * 512:(half + 1) * 512],
                                  start=True, stop=True)
            t_bc.inc(i, 2)
            mk_bc[("l2", 1)] = t_bc.n
            bar_end(tensor)

        def rows_finalize(vector, st_psrc, pre_ch, sqr_ch, sqr_target):
            inv = 1.0 / D
            vector.tensor_scalar(out=rows2[0:1, :], in0=st_psrc[0:1, :],
                                 scalar1=inv, scalar2=None, op0=OP.mult)
            vector.tensor_scalar(out=rows[32:33, :], in0=st_psrc[32:33, :],
                                 scalar1=inv, scalar2=EPS, op0=OP.mult, op1=OP.add)
            vector.tensor_tensor(out=rows2[32:33, :], in0=rows2[0:1, :],
                                 in1=rows2[0:1, :], op=OP.mult)
            vector.tensor_tensor(out=rows[32:33, :], in0=rows[32:33, :],
                                 in1=rows2[32:33, :], op=OP.subtract)
            pre_ch.inc(vector.reciprocal(out=rows2[32:33, :],
                                         in_=rows[32:33, :]))
            sqr_ch.wait(vector, sqr_target)
            vector.tensor_tensor(out=rows[32:33, :], in0=rows2[0:1, :],
                                 in1=rows[0:1, :], op=OP.mult)
            vector.tensor_copy(out=bc_rhs[0:1, :], in_=rows[0:1, :])
            return vector.tensor_copy(out=bc_rhs[32:33, :],
                                      in_=rows[32:33, :])

        @blk.vector
        def _(vector):
            vector = Drained(vector)
            bar_wait(vector, 2)
            for ec in range(NDC + 2):
                if ec < NDC:
                    d_wp[ec % 2].wait(vector, mk_w[("p3", ec)])
                    if ec >= 2:
                        t_pj.wait(vector, ec - 1)
                    v_w3.inc(vector.tensor_copy(out=wbt3[:, ec % 2, :, :],
                                                in_=wst3[:, ec % 2, :, :]))
                if ec >= 2:
                    e0 = ec - 2
                    t_pj.wait(vector, e0 + 1)
                    if e0 >= 2:
                        d_pjp[e0 % 2].wait(vector, mk_pj[e0 - 2])
                        t_stq2.wait(vector, e0 - 1)
                    v_pe.inc(vector.tensor_copy(out=pjt[:, e0 % 2, :],
                                                in_=pj_ps[:, e0 % 2, :]))
            t_stq2.wait(vector, NDC)
            v_rowb.inc(rows_finalize(vector, st2_ps, v_rowpre, s_sqr, 2))
            t_bc.wait(vector, mk_bc[("pa", 0)])
            v_bca.inc(vector.tensor_copy(out=bc_a[:], in_=st2_ps[:]))
            t_bc.wait(vector, mk_bc[("pa", 1)])
            v_bcb.inc(vector.tensor_copy(out=bc_b[:], in_=st2_ps[:]))
            for ec in range(NDC):
                d_xp[ec % 2].wait(vector, mk_x[("p3", ec)])
                d_ldtp[ec % 2].wait(vector, mk_ldt[ec])
                if ec >= 2:
                    d_x1p[ec % 2].wait(vector, mk_x1[ec - 2])
                    t_stq3.wait(vector, ec - 1)
                vector.tensor_tensor(out=tmp_f[:], in0=ldt[:, ec % 2, :],
                                     in1=bc_a[:], op=OP.mult)
                vector.tensor_tensor(out=tmp_f[:], in0=tmp_f[:],
                                     in1=bc_b[:], op=OP.subtract)
                v_x1.inc(vector.tensor_tensor(out=x1t[:, ec % 2, :], in0=tmp_f[:],
                                              in1=xf[:, ec % 2, :], op=OP.add))
                v_x1b.inc(vector.tensor_copy(out=x1b[:, ec % 2, :],
                                             in_=x1t[:, ec % 2, :]))
            t_stq3.wait(vector, NDC)
            v_rowb.inc(rows_finalize(vector, st2_ps, v_rowpre, s_sqr, 3))
            t_bc.wait(vector, mk_bc[("l2", 0)])
            v_bca.inc(vector.tensor_copy(out=bc_a[:], in_=st2_ps[:]))
            t_bc.wait(vector, mk_bc[("l2", 1)])
            v_bcb.inc(vector.tensor_copy(out=bc_b[:], in_=st2_ps[:]))
            for ec in range(NDC):
                d_lxp[ec % 2].wait(vector, mk_lx[ec])
                vector.tensor_tensor(out=tmp_f[:], in0=xf[:, ec % 2, :],
                                     in1=bc_a[:], op=OP.mult)
                v_l2.inc(vector.tensor_tensor(out=x_bf[:, ec, :], in0=tmp_f[:],
                                              in1=bc_b[:], op=OP.subtract))
            bar_end(vector)

        p3s.close()
        p3p.close()
        scA.close()

        if stage == 3:
            @blk.sync
            def _(sync):
                bar_wait(sync, 3)
                d_dbg.dinc(sync.dma_start(out=dbg["d_x1"][:], in_=x1_sc[:]))
                d_dbg.dinc(sync.dma_start(out=dbg["d_ln2"].rearrange(
                    "(c p) t -> p c t", p=128), in_=x_bf[:]))
                d_dbg.dinc(sync.dma_start(out=dbg["d_proj"][:], in_=proj_sc[:]))
                d_dbg.dinc(sync.dma_start(out=dbg["d_rows"][0:1, :],
                                          in_=bc_a[0:1, :]))
                d_dbg.dinc(sync.dma_start(out=dbg["d_rows"][1:2, :],
                                          in_=bc_b[0:1, :]))
                d_outp[0].dinc(sync.dma_start(out=outT[0:128, :], in_=xf[:, 0, :]))
                d_dbg.wait(sync)
                d_outp[0].wait(sync)
            return nc, dbg, ctx

        p4s = contextlib.ExitStack()
        h_bf = p4s.enter_context(nc.sbuf_tensor("h_bf", [128, NIC, T], BF16))
        gt_st = p4s.enter_context(nc.sbuf_tensor("gt_st", [128, 2, 128], BF16))
        up_st = p4s.enter_context(nc.sbuf_tensor("up_st", [128, 2, 128], BF16))
        sil_t = p4s.enter_context(nc.sbuf_tensor("sil_t", [128, T], BF16))
        p4p = contextlib.ExitStack()
        g_ps = p4p.enter_context(nc.psum_tensor("g_ps", [128, 2, 1024], F32))
        u_ps = p4p.enter_context(nc.psum_tensor("u_ps", [128, 2, 1024], F32))

        @blk.sync
        def _(sync):
            bar_wait(sync, 3)
            g_cc.wait(sync, 2)
            for ic in range(NIC):
                for dc in range(NDC):
                    un = ic * NDC + dc
                    if un >= 2:
                        t_guu.wait(sync, un - 1)
                    r, half = divmod(dc, 2)
                    d_gwp[un % 2].dinc(sync.dma_start(
                        out=gt_st[:, un % 2, :],
                        in_=gh_out_g(r)[half * 128:(half + 1) * 128,
                                        ic * 128:(ic + 1) * 128]))
                    mk_gw[un] = d_gwp[un % 2].dinc(sync.dma_start(
                        out=up_st[:, un % 2, :],
                        in_=gh_out_u(r)[half * 128:(half + 1) * 128,
                                        ic * 128:(ic + 1) * 128]))
            bar_end(sync)

        @blk.tensor
        def _(tensor):
            bar_wait(tensor, 3)
            for ic in range(NIC):
                if ic >= 2:
                    v_h.wait(tensor, ic - 1)
                for dc in range(NDC):
                    un = ic * NDC + dc
                    d_gwp[un % 2].wait(tensor, mk_gw[un])
                    for half in range(2):
                        tensor.matmul(
                            g_ps[:, ic % 2, half * 512:(half + 1) * 512],
                            gt_st[:, un % 2, :],
                            x_bf[:, dc, half * 512:(half + 1) * 512],
                            start=(dc == 0), stop=(dc == NDC - 1))
                    for half in range(2):
                        i = tensor.matmul(
                            u_ps[:, ic % 2, half * 512:(half + 1) * 512],
                            up_st[:, un % 2, :],
                            x_bf[:, dc, half * 512:(half + 1) * 512],
                            start=(dc == 0), stop=(dc == NDC - 1))
                    t_guu.inc(i)
            bar_end(tensor)

        @blk.scalar
        def _(scalar):
            scalar = Drained(scalar)
            bar_wait(scalar, 3)
            for ic in range(NIC):
                t_guu.wait(scalar, (ic + 1) * NDC)
                s_si.inc(scalar.activation(out=sil_t[:], in_=g_ps[:, ic % 2, :],
                                           func=AF.Silu))
            bar_end(scalar)

        @blk.vector
        def _(vector):
            vector = Drained(vector)
            bar_wait(vector, 3)
            for ic in range(NIC):
                s_si.wait(vector, ic + 1)
                v_h.inc(vector.tensor_tensor(out=h_bf[:, ic, :], in0=sil_t[:],
                                             in1=u_ps[:, ic % 2, :], op=OP.mult))
            bar_end(vector)

        p4p.close()

        if stage == 4:
            @blk.sync
            def _(sync):
                bar_wait(sync, 4)
                d_dbg.dinc(sync.dma_start(out=dbg["d_gw"][:], in_=gh_in[:]))
                d_dbg.dinc(sync.dma_start(out=dbg["d_h"].rearrange(
                    "(c p) t -> p c t", p=128), in_=h_bf[:]))
                d_dbg.dinc(sync.dma_start(out=dbg["d_x1"][:], in_=x1_sc[:]))
                d_dbg.dinc(sync.dma_start(out=dbg["d_ln2"].rearrange(
                    "(c p) t -> p c t", p=128), in_=x_bf[:]))
                d_outp[0].dinc(sync.dma_start(out=outT[0:128, :], in_=xf[:, 0, :]))
                d_dbg.wait(sync)
                d_outp[0].wait(sync)
            p4s.close()
            return nc, dbg, ctx

        p5s = contextlib.ExitStack()
        dn_st = p5s.enter_context(nc.sbuf_tensor("dn_st", [128, 2, 256], BF16))
        dnt = p5s.enter_context(nc.sbuf_tensor("dnt", [128, 2, T], BF16))
        outst = p5s.enter_context(nc.sbuf_tensor("outst", [128, 2, T], F32))
        x1t5 = p5s.enter_context(nc.sbuf_tensor("x1t5", [128, 2, T], F32))
        p5p = contextlib.ExitStack()
        dn_ps = p5p.enter_context(nc.psum_tensor("dn_ps", [128, 2, 1024], F32))
        st3_ps = p5p.enter_context(nc.psum_tensor("st3_ps", [128, 1024], F32))

        @blk.scalar
        def _(scalar):
            scalar = Drained(scalar)
            bar_wait(scalar, 4)
            for ec in range(NDC):
                v_de.wait(scalar, ec + 1)
                if ec >= 2:
                    t_stq4.wait(scalar, ec - 1)
                s_sq4.inc(scalar.square(out=sqc[:, ec % 2, :], in_=dnt[:, ec % 2, :]))
                mk_dn[ec] = d_dnp[ec % 2].dinc(scalar.dma_start(
                    out=down_sc[ec * 128:(ec + 1) * 128, :], in_=dnt[:, ec % 2, :]))
            v_rowpre.wait(scalar, 4)
            s_sqr.inc(scalar.sqrt(out=rows[0:1, :], in_=rows2[32:33, :]))

        @blk.sync
        def _(sync):
            bar_wait(sync, 4)
            for ep in range(8):
                for ic in range(NIC):
                    un = ep * NIC + ic
                    if un >= 2:
                        t_dnu.wait(sync, un - 1)
                    mk_gw[("dn", un)] = d_gwp[un % 2].dinc(sync.dma_start(
                        out=dn_st[:, un % 2, :],
                        in_=gh_out_d(ep)[ic * 128:(ic + 1) * 128, :]))
            d_dnp[0].wait(sync)
            d_dnp[1].wait(sync)
            d_x1p[0].wait(sync)
            d_x1p[1].wait(sync)
            t_stq4.wait(sync, 16)
            for ec in range(NDC + 1):
                if ec < NDC:
                    if ec >= 2:
                        v_out.wait(sync, ec - 1)
                    d_l5p[ec % 2].dinc(sync.dma_start(
                        out=x1t5[:, ec % 2, :],
                        in_=x1_sc[ec * 128:(ec + 1) * 128, :]))
                    mk_l5[ec] = d_l5p[ec % 2].dinc(sync.dma_start(
                        out=dnt[:, ec % 2, :],
                        in_=down_sc[ec * 128:(ec + 1) * 128, :]))
                if ec >= 1:
                    e1 = ec - 1
                    v_out.wait(sync, e1 + 1)
                    mk_out[e1] = d_outp[e1 % 2].dinc(sync.dma_start(
                        out=outT[e1 * 128:(e1 + 1) * 128, :],
                        in_=outst[:, e1 % 2, :]))
            d_outp[0].wait(sync)
            d_outp[1].wait(sync)

        @blk.tensor
        def _(tensor):
            bar_wait(tensor, 4)
            for ep in range(8):
                if ep >= 1:
                    v_de.wait(tensor, 2 * ep)
                for ic in range(NIC):
                    un = ep * NIC + ic
                    d_gwp[un % 2].wait(tensor, mk_gw[("dn", un)])
                    for ecl in range(2):
                        for half in range(2):
                            i = tensor.matmul(
                                dn_ps[:, ecl, half * 512:(half + 1) * 512],
                                dn_st[:, un % 2, ecl * 128:(ecl + 1) * 128],
                                h_bf[:, ic, half * 512:(half + 1) * 512],
                                start=(ic == 0), stop=(ic == NIC - 1))
                    t_dnu.inc(i)
                for ecl in range(2):
                    ec = 2 * ep + ecl
                    v_de.wait(tensor, ec + 1)
                    for half in range(2):
                        tensor.matmul(st3_ps[0:1, half * 512:(half + 1) * 512],
                                      ones_bf[:],
                                      dnt[:, ec % 2, half * 512:(half + 1) * 512],
                                      start=(ec == 0), stop=(ec == NDC - 1))
                    s_sq4.wait(tensor, ec + 1)
                    for half in range(2):
                        i = tensor.matmul(st3_ps[32:33, half * 512:(half + 1) * 512],
                                          ones_bf[:],
                                          sqc[:, ec % 2, half * 512:(half + 1) * 512],
                                          start=(ec == 0), stop=(ec == NDC - 1))
                    t_stq4.inc(i)
            v_rowb.wait(tensor, 4)
            for half in range(2):
                i = tensor.matmul(st3_ps[:, half * 512:(half + 1) * 512],
                                  bc_lhs[0:2, :], bc_rhs[0:2, half * 512:(half + 1) * 512],
                                  start=True, stop=True)
            t_bc.inc(i, 2)
            mk_bc[("l3", 0)] = t_bc.n
            v_bca.wait(tensor, 4)
            for half in range(2):
                i = tensor.matmul(st3_ps[:, half * 512:(half + 1) * 512],
                                  bc_lhs[32:34, :],
                                  bc_rhs[32:34, half * 512:(half + 1) * 512],
                                  start=True, stop=True)
            t_bc.inc(i, 2)
            mk_bc[("l3", 1)] = t_bc.n

        @blk.vector
        def _(vector):
            vector = Drained(vector)
            bar_wait(vector, 4)
            for ep in range(8):
                t_dnu.wait(vector, (ep + 1) * NIC)
                for ecl in range(2):
                    ec = 2 * ep + ecl
                    if ec >= 2:
                        t_stq4.wait(vector, ec - 1)
                        d_dnp[ec % 2].wait(vector, mk_dn[ec - 2])
                    v_de.inc(vector.tensor_copy(out=dnt[:, ec % 2, :],
                                                in_=dn_ps[:, ecl, :]))
            t_stq4.wait(vector, NDC)
            v_rowb.inc(rows_finalize(vector, st3_ps, v_rowpre, s_sqr, 4))
            t_bc.wait(vector, mk_bc[("l3", 0)])
            v_bca.inc(vector.tensor_copy(out=bc_a[:], in_=st3_ps[:]))
            t_bc.wait(vector, mk_bc[("l3", 1)])
            v_bcb.inc(vector.tensor_copy(out=bc_b[:], in_=st3_ps[:]))
            for ec in range(NDC):
                d_l5p[ec % 2].wait(vector, mk_l5[ec])
                if ec >= 2:
                    d_outp[ec % 2].wait(vector, mk_out[ec - 2])
                vector.tensor_tensor(out=tmp_f[:], in0=dnt[:, ec % 2, :],
                                     in1=bc_a[:], op=OP.mult)
                vector.tensor_tensor(out=tmp_f[:], in0=tmp_f[:],
                                     in1=bc_b[:], op=OP.subtract)
                v_out.inc(vector.tensor_tensor(out=outst[:, ec % 2, :],
                                               in0=tmp_f[:], in1=x1t5[:, ec % 2, :],
                                               op=OP.add))

        p5p.close()
        p5s.close()
        p4s.close()
        return nc, dbg, ctx

    raise AssertionError("unreachable")


_STAGE = 5


def kernel(L0_attn_x, L0_attn_w_qkv, L0_attn_w_proj, L0_ffn_lut,
           L0_ffn_gate_idx, L0_ffn_up_idx, L0_ffn_down_idx):
    x = np.asarray(L0_attn_x)[0]
    xTf = np.ascontiguousarray(x.T.astype(np.float32, copy=False))
    wqkvT = np.ascontiguousarray(np.asarray(L0_attn_w_qkv).T.astype(np.float32, copy=False))
    wprojT = np.ascontiguousarray(np.asarray(L0_attn_w_proj).T.astype(np.float32, copy=False))
    lutb = np.ascontiguousarray(np.broadcast_to(
        np.asarray(L0_ffn_lut).astype(np.float32, copy=False)[None, :], (128, 16)))
    gidxT = np.ascontiguousarray(np.asarray(L0_ffn_gate_idx).T)
    uidxT = np.ascontiguousarray(np.asarray(L0_ffn_up_idx).T)
    didxT = np.ascontiguousarray(np.asarray(L0_ffn_down_idx).T)

    nc, dbg, ctx = build_nc(stage=_STAGE)
    in_maps = []
    for c in range(NCORES):
        in_maps.append({
            "xT": np.ascontiguousarray(xTf[:, c * T:(c + 1) * T]),
            "wqkvT": wqkvT,
            "wprojT": wprojT,
            "lutb": lutb,
            "gidxT": np.ascontiguousarray(gidxT[c * 256:(c + 1) * 256, :]),
            "uidxT": np.ascontiguousarray(uidxT[c * 256:(c + 1) * 256, :]),
            "didxT": np.ascontiguousarray(didxT[:, c * 256:(c + 1) * 256]),
        })
    res = run_bass_kernel_spmd(nc, in_maps, core_ids=list(range(NCORES)))
    kernel.last_results = res
    outTs = [res.results[c]["outT"] for c in range(NCORES)]
    out = np.concatenate(outTs, axis=1).T[None, :, :]
    return np.ascontiguousarray(out.astype(np.float32, copy=False))


# version 03
# speedup vs baseline: 1.1863x; 1.1863x over previous
import contextlib
import math

import numpy as np
import ml_dtypes

import concourse.bass as bass
import concourse.mybir as mybir
from concourse.bass_utils import run_bass_kernel_spmd

F32 = mybir.dt.float32
BF16 = mybir.dt.bfloat16
I32 = mybir.dt.int32
AF = mybir.ActivationFunctionType
OP = mybir.AluOpType

NCORES = 8
T = 1024
D = 2048
NDC = 16
NH = 16
EQKV = 6144
INTER = 5504
NIC = 43
SCALE_ATT = 1.0 / math.sqrt(128.0)
EXP_BIAS = 40.0
EPS = 1e-5

GSH_GATE = 256 * INTER
GSH_DOWN = INTER * 256
GH_BLK = 2 * GSH_GATE + GSH_DOWN
KV_BLK = 2 * D * T + 16

GJOBS_V = ([("gate", s) for s in range(16)] + [("up", s) for s in range(16)]
           + [("down", s) for s in range(11)])
NGV = len(GJOBS_V)


class Ch:
    def __init__(self, ctx, nc, name):
        self.sem = ctx.enter_context(nc.semaphore(name))
        self.n = 0

    def inc(self, instr, amt=1):
        instr.then_inc(self.sem, amt)
        self.n += amt
        return self.n

    def dinc(self, instr):
        return self.inc(instr, 16)

    def wait(self, eng, n=None):
        eng.wait_ge(self.sem, self.n if n is None else n)


class Drained:
    _COMPUTE = {"tensor_copy", "tensor_tensor", "tensor_scalar",
                "scalar_tensor_tensor", "memset", "reciprocal", "activation",
                "square", "sqrt", "copy", "mul", "add", "select", "iota",
                "tensor_reduce", "reduce_max", "reduce_sum", "dma_start"}

    def __init__(self, eng):
        self._e = eng

    def __getattr__(self, name):
        f = getattr(self._e, name)
        if name in Drained._COMPUTE:
            def wrap(*a, **k):
                self._e.drain()
                return f(*a, **k)
            return wrap
        return f


def build_nc(stage=5):
    nc = bass.Bass("TRN2", target_bir_lowering=False,
                   detect_race_conditions=False)

    xT = nc.dram_tensor("xT", [D, T], F32, kind="ExternalInput").ap()
    wqkvT = nc.dram_tensor("wqkvT", [D, EQKV], F32, kind="ExternalInput").ap()
    wprojT = nc.dram_tensor("wprojT", [D, D], F32, kind="ExternalInput").ap()
    lutb = nc.dram_tensor("lutb", [128, 16], F32, kind="ExternalInput").ap()
    gidxT = nc.dram_tensor("gidxT", [256, INTER], I32, kind="ExternalInput").ap()
    uidxT = nc.dram_tensor("uidxT", [256, INTER], I32, kind="ExternalInput").ap()
    didxT = nc.dram_tensor("didxT", [INTER, 256], I32, kind="ExternalInput").ap()
    outT = nc.dram_tensor("outT", [D, T], F32, kind="ExternalOutput").ap()

    dbg = {}
    if stage < 5:
        def _dt(name, shape, dtyp):
            dbg[name] = nc.dram_tensor(name, shape, dtyp, kind="ExternalOutput").ap()
        _dt("d_q", [D, T], BF16)
        _dt("d_k", [D, T], BF16)
        _dt("d_v", [T, D], BF16)
        _dt("d_stats", [2, T], F32)
        _dt("d_o", [D, T], BF16)
        if stage >= 3:
            _dt("d_x1", [D, T], F32)
            _dt("d_ln2", [D, T], BF16)
            _dt("d_proj", [D, T], BF16)
            _dt("d_rows", [2, T], F32)
        if stage >= 4:
            _dt("d_gw", [GH_BLK], BF16)
            _dt("d_h", [INTER, T], BF16)

    KA = D * T + 16
    kv_in = nc.dram_tensor("kv_in", [KV_BLK], BF16, kind="Internal").ap()
    kvA_out = nc.dram_tensor("kvA_out", [NCORES * KA], BF16, kind="Internal").ap()
    kvB_out = nc.dram_tensor("kvB_out", [NCORES * D * T], BF16, kind="Internal").ap()
    gh_in = nc.dram_tensor("gh_in", [GH_BLK], BF16, kind="Internal").ap()
    gh_out = nc.dram_tensor("gh_out", [NCORES * GH_BLK], BF16, kind="Internal").ap()
    st_bnc = nc.dram_tensor("st_bnc", [2048], F32, kind="Internal").ap()
    x1_sc = nc.dram_tensor("x1_sc", [D, T], F32, kind="Internal").ap()
    proj_sc = nc.dram_tensor("proj_sc", [D, T], BF16, kind="Internal").ap()
    down_sc = nc.dram_tensor("down_sc", [D, T], BF16, kind="Internal").ap()

    kv_in_K = kv_in[0:D * T].rearrange("(e t) -> e t", t=T)
    kv_in_N = kv_in[D * T:D * T + 16]
    kv_in_V = kv_in[KA:KA + D * T].rearrange("(t e) -> t e", e=D)
    kv_out_K = lambda r: kvA_out[r * KA: r * KA + D * T].rearrange("(e t) -> e t", t=T)
    kv_out_V = lambda r: kvB_out[r * D * T: (r + 1) * D * T].rearrange("(t e) -> t e", e=D)
    gh_in_g = gh_in[0:GSH_GATE].rearrange("(d i) -> d i", i=INTER)
    gh_in_u = gh_in[GSH_GATE:2 * GSH_GATE].rearrange("(d i) -> d i", i=INTER)
    gh_in_d = gh_in[2 * GSH_GATE:].rearrange("(i e) -> i e", e=256)
    gh_out_g = lambda r: gh_out[r * GH_BLK: r * GH_BLK + GSH_GATE].rearrange("(d i) -> d i", i=INTER)
    gh_out_u = lambda r: gh_out[r * GH_BLK + GSH_GATE: r * GH_BLK + 2 * GSH_GATE].rearrange("(d i) -> d i", i=INTER)
    gh_out_d = lambda r: gh_out[r * GH_BLK + 2 * GSH_GATE: (r + 1) * GH_BLK].rearrange("(i e) -> i e", e=256)

    def ghost_job(kind, sl):
        if kind == "gate":
            return gidxT[:, sl * 344:(sl + 1) * 344], gh_in_g[:, sl * 344:(sl + 1) * 344], 2, 344
        if kind == "up":
            return uidxT[:, sl * 344:(sl + 1) * 344], gh_in_u[:, sl * 344:(sl + 1) * 344], 2, 344
        r0 = sl * 4 * 128
        nrc = 4 if sl < 10 else 3
        return didxT[r0:r0 + nrc * 128, :], gh_in_d[r0:r0 + nrc * 128, :], nrc, 256

    ctx = contextlib.ExitStack()
    with ctx:
        sbuf = lambda name, shape, dtp: ctx.enter_context(nc.sbuf_tensor(name, shape, dtp))
        consts = sbuf("consts", [128, 4], F32)
        ones_bf = sbuf("ones_bf", [128, 1], BF16)
        bc_lhs = sbuf("bc_lhs", [34, 128], F32)
        bc_rhs = sbuf("bc_rhs", [34, 1024], F32)
        lut_s = sbuf("lut_s", [128, 16], F32)
        x_bf = sbuf("x_bf", [128, NDC, T], BF16)
        xf = sbuf("xf", [128, 2, T], F32)
        sqc = sbuf("sqc", [128, 2, T], BF16)
        rows = sbuf("rows", [33, 1024], F32)
        rows2 = sbuf("rows2", [33, 1024], F32)
        bc_a = sbuf("bc_a", [128, T], F32)
        bc_b = sbuf("bc_b", [128, T], F32)
        statsT = sbuf("statsT", [128, 16], F32)
        s_neg = sbuf("s_neg", [128, 48], F32)
        tmp_f = sbuf("tmp_f", [128, T], F32)
        ones2_bf = sbuf("ones2_bf", [2, 128], BF16)
        qnorm_s = sbuf("qnorm_s", [33, 8192], BF16)
        kmax_s = sbuf("kmax_s", [33, 16], F32)
        knrow = sbuf("knrow", [1, 16], BF16)
        kn8 = sbuf("kn8", [1, 128], BF16)

        ch = lambda name: Ch(ctx, nc, name)
        chp = lambda name, n: [Ch(ctx, nc, f"{name}{i}") for i in range(n)]
        d_lut = ch("d_lut")
        d_xp = chp("d_x", 2); mk_x = {}
        d_wp = chp("d_w", 2); mk_w = {}
        d_wvp = chp("d_wv", 2); mk_wv = {}
        d_kvp = chp("d_kv", 2); mk_kv = {}
        d_gip = chp("d_gi", 3); mk_gi = {}
        d_bnc = ch("d_bnc")
        d_ksp = chp("d_ks", 2); mk_ks = {}
        d_vsp = chp("d_vs", 2); mk_vs = {}
        d_knt = ch("d_knt"); d_kn = ch("d_kn")
        d_ghp = chp("d_gh", 3); mk_gh = {}
        d_pjp = chp("d_pj", 2); mk_pj = {}
        d_x1p = chp("d_x1", 2); mk_x1 = {}
        d_dnp = chp("d_dn", 2); mk_dn = {}
        d_ldtp = chp("d_ldt", 2); mk_ldt = {}
        d_gwp = chp("d_gw", 2); mk_gw = {}
        mk_bc = {}
        d_lxp = chp("d_lx", 2); mk_lx = {}
        d_l5p = chp("d_l5", 2); mk_l5 = {}
        d_outp = chp("d_out", 2); mk_out = {}
        d_dbg = ch("d_dbg")
        v_cc = ch("v_cc"); v_xc = ch("v_xc"); v_wq = ch("v_wq"); v_wv = ch("v_wv")
        v_sx = ch("v_sx"); v_rowpre = ch("v_rowpre"); v_rowb = ch("v_rowb")
        v_bca = ch("v_bca"); v_bcb = ch("v_bcb")
        v_svst = ch("v_svst"); v_svbc = ch("v_svbc")
        v_epi = ch("v_epi"); v_vepi = ch("v_vepi")
        v_gc = ch("v_gc"); v_ghd = ch("v_ghd")
        s_k2 = ch("s_k2"); s_q2 = ch("s_q2"); s_qn = ch("s_qn")
        t_kn = ch("t_kn"); t_qn = ch("t_qn"); v_kn = ch("v_kn")
        v_mn = ch("v_mn")
        v_z = ch("v_z"); v_rz = ch("v_rz"); v_no = ch("v_no"); v_o = ch("v_o")
        v_w3 = ch("v_w3"); v_pe = ch("v_pe")
        v_x1 = ch("v_x1"); v_x1b = ch("v_x1b"); v_l2 = ch("v_l2")
        v_h = ch("v_h"); v_de = ch("v_de"); v_out = ch("v_out")
        s_sq = ch("s_sq"); s_exp = ch("s_exp"); s_si = ch("s_si")
        s_sqr = ch("s_sqr")
        s_sq2 = ch("s_sq2"); s_sq3 = ch("s_sq3"); s_sq4 = ch("s_sq4")
        t_stq = ch("t_stq"); t_qk = ch("t_qk"); t_s = ch("t_s")
        t_bc = ch("t_bc"); t_v = ch("t_v"); t_sv = ch("t_sv"); t_svb = ch("t_svb")
        t_smm = ch("t_smm"); t_pv = ch("t_pv"); t_z = ch("t_z"); t_rzb = ch("t_rzb")
        t_pj = ch("t_pj"); t_stq2 = ch("t_stq2")
        t_stq3 = ch("t_stq3")
        t_guu = ch("t_guu")
        t_dnu = ch("t_dnu")
        t_stq4 = ch("t_stq4")
        g_cc = ch("g_cc")
        bar = ch("bar")

        blk = ctx.enter_context(nc.Block())

        def bar_end(eng):
            eng.drain()
            eng.sem_inc(bar.sem, 1)

        def bar_wait(eng, phase):
            eng.wait_ge(bar.sem, 4 * phase)

        scA = contextlib.ExitStack()
        o_bf = scA.enter_context(nc.sbuf_tensor("o_bf", [128, NDC, T], BF16))
        gvi = scA.enter_context(nc.sbuf_tensor("gvi", [128, 3, 1024], I32))
        gvb = scA.enter_context(nc.sbuf_tensor("gvb", [128, 1024], BF16))
        gvm = scA.enter_context(nc.sbuf_tensor("gvm", [128, 1024], BF16))
        gvo = scA.enter_context(nc.sbuf_tensor("gvo", [128, 3, 1024], BF16))
        scQ = contextlib.ExitStack()
        qs = scQ.enter_context(nc.sbuf_tensor("qs", [128, NH, T], BF16))

        def ghost_spill_ap(j):
            kind, sl = GJOBS_V[j]
            _, dst, a, b = ghost_job(kind, sl)
            n = a * b
            oap = gvo[:, j % 3, 0:n].rearrange("p (a b) -> p a b", a=a)
            return dst.rearrange("(c p) i -> p c i", p=128), oap

        def ghost_units(eng, j):
            kind, sl = GJOBS_V[j]
            _, dst, a, b = ghost_job(kind, sl)
            n = a * b
            iap = gvi[:, j % 3, 0:n].rearrange("p (a b) -> p a b", a=a)
            bap = gvb[:, 0:n].rearrange("p (a b) -> p a b", a=a)
            map_ = gvm[:, 0:n].rearrange("p (a b) -> p a b", a=a)
            oap = gvo[:, j % 3, 0:n].rearrange("p (a b) -> p a b", a=a)

            def u_cast():
                d_gip[j % 3].wait(eng, mk_gi[j])
                d_lut.wait(eng, 16)
                v_gc.inc(eng.tensor_copy(out=bap, in_=iap))

            def u_k(k):
                eng.tensor_scalar(out=map_, in0=bap, scalar1=float(k),
                                  scalar2=lut_s[:, k:k + 1], op0=OP.is_equal,
                                  op1=OP.mult)
                if k == 0:
                    if j >= 3:
                        d_ghp[j % 3].wait(eng, mk_gh[j - 3])
                    eng.tensor_copy(out=oap, in_=map_)
                elif k < 15:
                    eng.scalar_tensor_tensor(out=oap, in0=map_, scalar=1.0,
                                             in1=oap, op0=OP.mult, op1=OP.add)
                else:
                    v_ghd.inc(eng.scalar_tensor_tensor(
                        out=oap, in0=map_, scalar=1.0, in1=oap,
                        op0=OP.mult, op1=OP.add))

            return [u_cast] + [(lambda k=k: u_k(k)) for k in range(16)]

        p1q = contextlib.ExitStack()
        wst = p1q.enter_context(nc.sbuf_tensor("wst", [128, 2, NDC, 128], F32))
        ktmp = p1q.enter_context(nc.sbuf_tensor("ktmp", [128, 2, T], BF16))
        wbt = p1q.enter_context(nc.sbuf_tensor("wbt", [128, 2, NDC, 128], BF16))
        p1p = contextlib.ExitStack()
        qk_ps = p1p.enter_context(nc.psum_tensor("qk_ps", [128, 2, 1024], F32))
        st_ps = p1p.enter_context(nc.psum_tensor("st_ps", [128, 1024], F32))
        sA_ps = p1p.enter_context(nc.psum_tensor("sA_ps", [128, 1], F32))
        sB_ps = p1p.enter_context(nc.psum_tensor("sB_ps", [128, 1], F32))

        @blk.sync
        def _(sync):
            d_lut.dinc(sync.dma_start(out=lut_s[:], in_=lutb[:]))
            for dc in range(NDC):
                if dc >= 2:
                    v_xc.wait(sync, dc - 1)
                mk_x[dc] = d_xp[dc % 2].dinc(sync.dma_start(
                    out=xf[:, dc % 2, :],
                    in_=xT[dc * 128:(dc + 1) * 128, :]))
            for ec in range(32):
                if ec >= 2:
                    v_wq.wait(sync, ec - 1)
                for sub in range(4):
                    mk_w[ec] = d_wp[ec % 2].dinc(sync.dma_start(
                        out=wst[:, ec % 2, sub * 4:(sub + 1) * 4, :],
                        in_=wqkvT[sub * 512:(sub + 1) * 512,
                                  ec * 128:(ec + 1) * 128]
                        .rearrange("(c p) e -> p c e", p=128)))

        @blk.scalar
        def _(scalar):
            scalar = Drained(scalar)
            for dc in range(NDC):
                v_xc.wait(scalar, dc + 1)
                if dc >= 2:
                    t_stq.wait(scalar, dc - 1)
                s_sq.inc(scalar.square(out=sqc[:, dc % 2, :], in_=x_bf[:, dc, :]))
            v_rowpre.wait(scalar, 1)
            s_sqr.inc(scalar.sqrt(out=rows[0:1, :], in_=rows2[32:33, :]))
            v_rowb.wait(scalar, 1)
            d_bnc.dinc(scalar.dma_start(
                out=st_bnc[0:1024].rearrange("(a b) -> a b", a=1),
                in_=rows[0:1, :]))
            d_bnc.dinc(scalar.dma_start(
                out=st_bnc[1024:2048].rearrange("(a b) -> a b", a=1),
                in_=rows[32:33, :]))
            d_bnc.wait(scalar, 32)
            d_bnc.dinc(scalar.dma_start(
                out=statsT[:, 0:8],
                in_=st_bnc[0:1024].rearrange("(c p) -> p c", p=128),
                allow_slow_non_contiguous=True))
            d_bnc.dinc(scalar.dma_start(
                out=statsT[:, 8:16],
                in_=st_bnc[1024:2048].rearrange("(c p) -> p c", p=128),
                allow_slow_non_contiguous=True))
            for ec in range(16, 32):
                v_epi.wait(scalar, ec + 1)
                mk_ks[ec] = d_ksp[ec % 2].dinc(scalar.dma_start(
                    out=kv_in_K[(ec - 16) * 128:(ec - 15) * 128, :],
                    in_=ktmp[:, ec % 2, :]))
                if ec == 16:
                    t_stq.wait(scalar, NDC)
                if ec >= 18:
                    t_kn.wait(scalar, ec - 17)
                s_k2.inc(scalar.square(out=sqc[:, ec % 2, :], in_=ktmp[:, ec % 2, :]))
            v_kn.wait(scalar, 16)
            d_knt.dinc(scalar.dma_start(
                out=kv_in_N.rearrange("(a b) -> a b", a=1), in_=knrow[:]))
            for h in range(NH):
                if h == 0:
                    t_kn.wait(scalar, 16)
                if h >= 2:
                    t_qn.wait(scalar, h - 1)
                s_q2.inc(scalar.square(out=sqc[:, h % 2, :], in_=qs[:, h, :]))
                t_qn.wait(scalar, h + 1)
                pr = 32 * (h % 2)
                s_qn.inc(scalar.sqrt(
                    out=qnorm_s[32 * (h // 8):32 * (h // 8) + 1,
                                (h % 8) * 1024:(h % 8) * 1024 + 1024],
                    in_=st_ps[pr:pr + 1, :]))

        @blk.vector
        def _(vector):
            vector = Drained(vector)
            v_cc.inc(vector.memset(consts[:, 0:1], EXP_BIAS))
            v_cc.inc(vector.memset(consts[:, 1:2], SCALE_ATT))
            v_cc.inc(vector.memset(ones_bf[:], 1.0))
            v_cc.inc(vector.memset(bc_lhs[:], 0.0))
            v_cc.inc(vector.memset(bc_lhs[0:1, :], 1.0))
            v_cc.inc(vector.memset(bc_lhs[32:33, :], 1.0))
            v_cc.inc(vector.memset(bc_rhs[:], 0.0))
            v_cc.inc(vector.memset(ones2_bf[:], 0.0))
            v_cc.inc(vector.memset(ones2_bf[0:1, :], 1.0))
            for dc in range(NDC):
                d_xp[dc % 2].wait(vector, mk_x[dc])
                if dc >= 2:
                    t_stq.wait(vector, dc - 1)
                v_xc.inc(vector.tensor_copy(out=x_bf[:, dc, :], in_=xf[:, dc % 2, :]))
            for ec in range(36):
                if ec == 3:
                    t_stq.wait(vector, NDC)
                    inv = 1.0 / D
                    vector.tensor_scalar(out=rows2[0:1, :], in0=st_ps[0:1, :],
                                         scalar1=inv, scalar2=None, op0=OP.mult)
                    vector.tensor_scalar(out=rows[32:33, :], in0=st_ps[32:33, :],
                                         scalar1=inv, scalar2=EPS,
                                         op0=OP.mult, op1=OP.add)
                    vector.tensor_tensor(out=rows2[32:33, :], in0=rows2[0:1, :],
                                         in1=rows2[0:1, :], op=OP.mult)
                    vector.tensor_tensor(out=rows[32:33, :], in0=rows[32:33, :],
                                         in1=rows2[32:33, :], op=OP.subtract)
                    v_rowpre.inc(vector.reciprocal(out=rows2[32:33, :],
                                                   in_=rows[32:33, :]))
                    s_sqr.wait(vector, 1)
                    vector.tensor_tensor(out=rows[32:33, :], in0=rows2[0:1, :],
                                         in1=rows[0:1, :], op=OP.mult)
                    vector.tensor_copy(out=bc_rhs[0:1, :], in_=rows[0:1, :])
                    v_rowb.inc(vector.tensor_copy(out=bc_rhs[32:33, :],
                                                  in_=rows[32:33, :]))
                    t_bc.wait(vector, 2)
                    v_bca.inc(vector.tensor_copy(out=bc_a[:], in_=st_ps[:]))
                    t_bc.wait(vector, 4)
                    v_bcb.inc(vector.tensor_copy(out=bc_b[:], in_=st_ps[:]))
                if ec >= 4:
                    e1 = ec - 4
                    t_qk.wait(vector, e1 + 1)
                    v_sx.wait(vector, e1 + 1)
                    if e1 >= 18:
                        d_ksp[e1 % 2].wait(vector, mk_ks[e1 - 2])
                    vector.tensor_tensor(out=tmp_f[:], in0=qk_ps[:, e1 % 2, :],
                                         in1=bc_a[:], op=OP.mult)
                    dst = qs[:, e1, :] if e1 < 16 else ktmp[:, e1 % 2, :]
                    v_epi.inc(vector.scalar_tensor_tensor(
                        out=dst, in0=bc_b[:], scalar=s_neg[:, e1:e1 + 1],
                        in1=tmp_f[:], op0=OP.mult, op1=OP.add))
                if ec < 32:
                    d_wp[ec % 2].wait(vector, mk_w[ec])
                    if ec >= 2:
                        t_s.wait(vector, ec - 1)
                    v_wq.inc(vector.tensor_copy(out=wbt[:, ec % 2, :, :],
                                                in_=wst[:, ec % 2, :, :]))
                if 2 <= ec < 34:
                    e0 = ec - 2
                    t_s.wait(vector, e0 + 1)
                    sp = sA_ps if e0 % 2 == 0 else sB_ps
                    v_sx.inc(vector.tensor_scalar(
                        out=s_neg[:, e0:e0 + 1], in0=sp[:], scalar1=-1.0,
                        scalar2=None, op0=OP.mult))
                if 20 <= ec < 36:
                    e2 = ec - 4
                    t_kn.wait(vector, e2 - 15)
                    pr = 32 * (e2 % 2)
                    v_kn.inc(vector.reduce_max(
                        out=knrow[:, e2 - 16:e2 - 15], in_=st_ps[pr:pr + 1, :],
                        axis=mybir.AxisListType.X))
        @blk.tensor
        def _(tensor):
            v_cc.wait(tensor, 9)
            for dc in range(NDC):
                v_xc.wait(tensor, dc + 1)
                for half in range(2):
                    tensor.matmul(st_ps[0:1, half * 512:(half + 1) * 512],
                                  ones_bf[:],
                                  x_bf[:, dc, half * 512:(half + 1) * 512],
                                  start=(dc == 0), stop=(dc == NDC - 1))
                s_sq.wait(tensor, dc + 1)
                for half in range(2):
                    i = tensor.matmul(st_ps[32:33, half * 512:(half + 1) * 512],
                                      ones_bf[:],
                                      sqc[:, dc % 2, half * 512:(half + 1) * 512],
                                      start=(dc == 0), stop=(dc == NDC - 1))
                t_stq.inc(i)
            for ec in range(32):
                v_wq.wait(tensor, ec + 1)
                if ec == 2:
                    v_rowb.wait(tensor, 1)
                    for half in range(2):
                        i = tensor.matmul(
                            st_ps[:, half * 512:(half + 1) * 512], bc_lhs[0:2, :],
                            bc_rhs[0:2, half * 512:(half + 1) * 512],
                            start=True, stop=True)
                    t_bc.inc(i, 2)
                    v_bca.wait(tensor, 1)
                    for half in range(2):
                        i = tensor.matmul(
                            st_ps[:, half * 512:(half + 1) * 512], bc_lhs[32:34, :],
                            bc_rhs[32:34, half * 512:(half + 1) * 512],
                            start=True, stop=True)
                    t_bc.inc(i, 2)
                if ec >= 2:
                    v_epi.wait(tensor, ec - 1)
                    v_sx.wait(tensor, ec - 1)
                sp = sA_ps if ec % 2 == 0 else sB_ps
                for dc in range(NDC):
                    for half in range(2):
                        i = tensor.matmul(
                            qk_ps[:, ec % 2, half * 512:(half + 1) * 512],
                            wbt[:, ec % 2, dc, :],
                            x_bf[:, dc, half * 512:(half + 1) * 512],
                            start=(dc == 0), stop=(dc == NDC - 1))
                    j = tensor.matmul(sp[:], wbt[:, ec % 2, dc, :], ones_bf[:],
                                      start=(dc == 0), stop=(dc == NDC - 1))
                t_qk.inc(i)
                t_s.inc(j)
                if ec >= 18:
                    e = ec - 2
                    if e == 16:
                        v_bcb.wait(tensor, 1)
                    s_k2.wait(tensor, e - 15)
                    v_kn.wait(tensor, max(0, e - 17))
                    pr = 32 * (e % 2)
                    for half in range(2):
                        i = tensor.matmul(
                            st_ps[pr:pr + 1, half * 512:(half + 1) * 512],
                            ones_bf[:], sqc[:, e % 2, half * 512:(half + 1) * 512],
                            start=True, stop=True)
                    t_kn.inc(i)
            for e in (30, 31):
                s_k2.wait(tensor, e - 15)
                v_kn.wait(tensor, e - 17)
                pr = 32 * (e % 2)
                for half in range(2):
                    i = tensor.matmul(
                        st_ps[pr:pr + 1, half * 512:(half + 1) * 512],
                        ones_bf[:], sqc[:, e % 2, half * 512:(half + 1) * 512],
                        start=True, stop=True)
                t_kn.inc(i)
            for h in range(NH):
                s_q2.wait(tensor, h + 1)
                if h >= 2:
                    s_qn.wait(tensor, h - 1)
                pr = 32 * (h % 2)
                for half in range(2):
                    i = tensor.matmul(
                        st_ps[pr:pr + 1, half * 512:(half + 1) * 512],
                        ones_bf[:], sqc[:, h % 2, half * 512:(half + 1) * 512],
                        start=True, stop=True)
                t_qn.inc(i)

        p1q.close()

        p1v = contextlib.ExitStack()
        wstv = p1v.enter_context(nc.sbuf_tensor("wstv", [128, 2, 2, 512], F32))
        vtmp = p1v.enter_context(nc.sbuf_tensor("vtmp", [128, 2, 512], BF16))
        wbv = p1v.enter_context(nc.sbuf_tensor("wbv", [128, NDC, 512], BF16))
        s_bc = p1v.enter_context(nc.sbuf_tensor("s_bc", [128, 2, 512], F32))

        @blk.sync
        def _(sync):
            for u in range(32):
                es, sub = divmod(u, 8)
                if u == 0:
                    v_wq.wait(sync, 32)
                if u >= 2:
                    v_wv.wait(sync, u - 1)
                mk_wv[u] = d_wvp[u % 2].dinc(sync.dma_start(
                    out=wstv[:, u % 2, :, :],
                    in_=wqkvT[sub * 256:(sub + 1) * 256,
                              4096 + es * 512:4096 + (es + 1) * 512]
                    .rearrange("(c p) e -> p c e", p=128)))
            bar_end(sync)

        @blk.scalar
        def _(scalar):
            scalar = Drained(scalar)
            for es in range(4):
                for tt in range(8):
                    un = es * 8 + tt
                    v_vepi.wait(scalar, un + 1)
                    mk_vs[un] = d_vsp[un % 2].dinc(scalar.dma_start(
                        out=kv_in_V[tt * 128:(tt + 1) * 128,
                                    es * 512:(es + 1) * 512],
                        in_=vtmp[:, un % 2, :]))
            bar_end(scalar)

        @blk.vector
        def _(vector):
            vector = Drained(vector)
            d_bnc.wait(vector, 64)
            for es in range(4):
                for sub in range(8):
                    u = es * 8 + sub
                    d_wvp[u % 2].wait(vector, mk_wv[u])
                    if u == 0:
                        t_s.wait(vector, 32)
                    if sub == 0 and es >= 1:
                        t_v.wait(vector, es * 8)
                    v_wv.inc(vector.tensor_copy(
                        out=wbv[:, sub * 2:(sub + 1) * 2, :],
                        in_=wstv[:, u % 2, :, :]))
                t_sv.wait(vector, es + 1)
                if es >= 1:
                    t_svb.wait(vector, es)
                srcp = st_ps[64:65, (es % 2) * 512:(es % 2 + 1) * 512]
                v_svst.inc(vector.tensor_copy(out=bc_rhs[0:1, 0:512], in_=srcp))
                t_svb.wait(vector, es + 1)
                dstb = st_ps[:, (es % 2) * 512:(es % 2 + 1) * 512]
                v_svbc.inc(vector.tensor_copy(
                    out=s_bc[:, es % 2, :], in_=dstb))
                for tt in range(8):
                    un = es * 8 + tt
                    t_v.wait(vector, un + 1)
                    if un >= 2:
                        d_vsp[un % 2].wait(vector, mk_vs[un - 2])
                    vector.tensor_scalar(
                        out=tmp_f[:, 0:512], in0=s_bc[:, es % 2, :],
                        scalar1=statsT[:, 8 + tt:9 + tt], scalar2=None, op0=OP.mult)
                    v_vepi.inc(vector.scalar_tensor_tensor(
                        out=vtmp[:, un % 2, :], in0=qk_ps[:, un % 2, 0:512],
                        scalar=statsT[:, tt:tt + 1], in1=tmp_f[:, 0:512],
                        op0=OP.mult, op1=OP.subtract))
            bar_end(vector)

        @blk.tensor
        def _(tensor):
            for es in range(4):
                v_wv.wait(tensor, es * 8 + 8)
                if es == 0:
                    s_qn.wait(tensor, NH)
                if es >= 2:
                    v_svbc.wait(tensor, es - 1)
                dstp = st_ps[64:65, (es % 2) * 512:(es % 2 + 1) * 512]
                for dc in range(NDC):
                    i = tensor.matmul(dstp, ones_bf[:], wbv[:, dc, :],
                                      start=(dc == 0), stop=(dc == NDC - 1))
                t_sv.inc(i)
                v_svst.wait(tensor, es + 1)
                i = tensor.matmul(st_ps[:, (es % 2) * 512:(es % 2 + 1) * 512],
                                  bc_lhs[0:2, :], bc_rhs[0:2, 0:512],
                                  start=True, stop=True)
                t_svb.inc(i)
                for tt in range(8):
                    un = es * 8 + tt
                    if un >= 2:
                        v_vepi.wait(tensor, un - 1)
                    for dc in range(NDC):
                        i = tensor.matmul(qk_ps[:, un % 2, 0:512],
                                          x_bf[:, dc, tt * 128:(tt + 1) * 128],
                                          wbv[:, dc, :],
                                          start=(dc == 0), stop=(dc == NDC - 1))
                    t_v.inc(i)
            bar_end(tensor)

        @blk.gpsimd
        def _(gpsimd):
            d_ksp[0].wait(gpsimd)
            d_ksp[1].wait(gpsimd)
            d_knt.wait(gpsimd)
            g_cc.inc(gpsimd.collective_compute(
                "AllGather", OP.bypass, ins=[kv_in[0:KA]], outs=[kvA_out[:]],
                replica_groups=[list(range(NCORES))]))
            d_vsp[0].wait(gpsimd)
            d_vsp[1].wait(gpsimd)
            g_cc.inc(gpsimd.collective_compute(
                "AllGather", OP.bypass, ins=[kv_in[KA:KA + D * T]],
                outs=[kvB_out[:]],
                replica_groups=[list(range(NCORES))]))

        p1v.close()
        p1p.close()

        p2s = contextlib.ExitStack()
        kst = p2s.enter_context(nc.sbuf_tensor("kst", [128, 2, T], BF16))
        vst = p2s.enter_context(nc.sbuf_tensor("vst", [128, 2, 8, 128], BF16))
        pst = p2s.enter_context(nc.sbuf_tensor("pst", [128, 2, 1024], BF16))
        mneg = p2s.enter_context(nc.sbuf_tensor("mneg", [2, 2, 1024], BF16))
        zacc = p2s.enter_context(nc.sbuf_tensor("zacc", [128, 2, 1024], BF16))
        p2p = contextlib.ExitStack()
        sat_ps = p2p.enter_context(nc.psum_tensor("sat_ps", [128, 2, 1024], F32))
        pv_ps = p2p.enter_context(nc.psum_tensor("pv_ps", [128, 1024], F32))
        z_ps = p2p.enter_context(nc.psum_tensor("z_ps", [128, 1024], F32))

        pop_slots = []
        for _h in range(NH):
            for _j in range(64):
                _busy = ((_j == 2 and _h + 1 < NH) or (_j == 3 and _h >= 1)
                         or (_j == 8 and _h >= 1) or (_j == 63))
                if not _busy:
                    pop_slots.append(_h * 64 + _j)
        first_pop = [pop_slots[17 * j] for j in range(NGV)]
        done_pop = [pop_slots[17 * (j + 1) - 1] for j in range(NGV)]

        def gv_allow(h):
            n = sum(1 for f in first_pop if f <= 64 * h - 1)
            return min(NGV, n + 3)

        @blk.sync
        def _(sync):
            bar_wait(sync, 1)
            gvj = 0
            for h in range(NH):
                for r in range(NCORES):
                    u = h * NCORES + r
                    if u == 0:
                        g_cc.wait(sync, 1)
                        d_kn.dinc(sync.dma_start(
                            out=kn8[:].rearrange("a (r h) -> a r h", r=8),
                            in_=kvA_out[0:8 * KA].rearrange(
                                "(r x) -> r x", r=8)[:, D * T:D * T + 16]))
                    if u % 2 == 0 and gvj < gv_allow(h):
                        if gvj >= 3:
                            v_gc.wait(sync, gvj - 2)
                        kind, sl = GJOBS_V[gvj]
                        srcg, _, a, b = ghost_job(kind, sl)
                        mk_gi[gvj] = d_gip[gvj % 3].dinc(sync.dma_start(
                            out=gvi[:, gvj % 3, 0:a * b].rearrange(
                                "p (a b) -> p a b", a=a),
                            in_=srcg.rearrange("(c p) i -> p c i", p=128)))
                        gvj += 1
                    if u >= 2:
                        t_pv.wait(sync, (u - 1) * 8)
                    d_kvp[u % 2].dinc(sync.dma_start(
                        out=kst[:, u % 2, :],
                        in_=kv_out_K(r)[h * 128:(h + 1) * 128, :]))
                    if u == 0:
                        g_cc.wait(sync, 2)
                    mk_kv[u] = d_kvp[u % 2].dinc(sync.dma_start(
                        out=vst[:, u % 2, :, :],
                        in_=kv_out_V(r)[:, h * 128:(h + 1) * 128]
                        .rearrange("(c p) e -> p c e", p=128)))
            bar_end(sync)

        def _emit_colsums(tensor, hz):
            v_z.wait(tensor, (hz + 1) * 64)
            if hz >= 1:
                v_no.wait(tensor, hz)
            for half in range(2):
                for p in range(2):
                    i = tensor.matmul(
                        z_ps[0:1, half * 512:(half + 1) * 512], ones_bf[:],
                        zacc[:, p, half * 512:(half + 1) * 512],
                        start=(p == 0), stop=(p == 1))
            t_z.inc(i)

        def _emit_rzmm(tensor, hz):
            v_rz.wait(tensor, hz + 1)
            for half in range(2):
                i = tensor.matmul(z_ps[:, half * 512:(half + 1) * 512],
                                  bc_lhs[0:2, :],
                                  bc_rhs[0:2, half * 512:(half + 1) * 512],
                                  start=True, stop=True)
            t_rzb.inc(i)

        @blk.tensor
        def _(tensor):
            bar_wait(tensor, 1)
            for h in range(NH):
                for j in range(65):
                    gj = h * 64 + j
                    if h >= 1 and j == 1:
                        _emit_colsums(tensor, h - 1)
                    if h >= 1 and j == 6:
                        _emit_rzmm(tensor, h - 1)
                    if j < 64:
                        r, jj = divmod(j, 8)
                        u = h * NCORES + r
                        if jj == 0:
                            d_kvp[u % 2].wait(tensor, mk_kv[u])
                        if gj >= 2:
                            s_exp.wait(tensor, gj - 1)
                        for half in range(2):
                            tensor.matmul(
                                sat_ps[:, gj % 2, half * 512:(half + 1) * 512],
                                kst[:, u % 2, jj * 128:(jj + 1) * 128],
                                qs[:, h, half * 512:(half + 1) * 512],
                                start=True, stop=False)
                        if j == 0:
                            v_mn.wait(tensor, h + 1)
                        for half in range(2):
                            i = tensor.matmul(
                                sat_ps[:, gj % 2, half * 512:(half + 1) * 512],
                                ones2_bf[:], mneg[0:2, h % 2, half * 512:(half + 1) * 512],
                                start=False, stop=True)
                        t_smm.inc(i)
                    if j >= 1:
                        jp = j - 1
                        gp = h * 64 + jp
                        rp, jjp = divmod(jp, 8)
                        up = h * NCORES + rp
                        if jp == 0:
                            v_o.wait(tensor, h)
                        s_exp.wait(tensor, gp + 1)
                        for half in range(2):
                            i = tensor.matmul(
                                pv_ps[:, half * 512:(half + 1) * 512],
                                vst[:, up % 2, jjp, :],
                                pst[:, gp % 2, half * 512:(half + 1) * 512],
                                start=(jp == 0), stop=(jp == 63))
                        t_pv.inc(i)
            _emit_colsums(tensor, NH - 1)
            _emit_rzmm(tensor, NH - 1)
            bar_end(tensor)

        @blk.scalar
        def _(scalar):
            scalar = Drained(scalar)
            bar_wait(scalar, 1)
            v_kn.wait(scalar, 17)
            scalar.sqrt(out=kmax_s[0:1, :], in_=kmax_s[0:1, :])
            s_qn.inc(scalar.mul(out=kmax_s[0:1, :], in_=kmax_s[0:1, :], mul=-1.0))
            spilled = 0
            for h in range(NH):
                for j in range(64):
                    gj = h * 64 + j
                    t_smm.wait(scalar, gj + 1)
                    if gj >= 2:
                        t_pv.wait(scalar, gj - 1)
                        v_z.wait(scalar, gj - 1)
                    s_exp.inc(scalar.activation(
                        out=pst[:, gj % 2, :], in_=sat_ps[:, gj % 2, :],
                        func=AF.Exp, bias=consts[:, 0:1], scale=consts[:, 1:2]))
                    while spilled < NGV and done_pop[spilled] + 6 <= gj:
                        v_ghd.wait(scalar, spilled + 1)
                        dsta, oap = ghost_spill_ap(spilled)
                        mk_gh[spilled] = d_ghp[spilled % 3].dinc(
                            scalar.dma_start(out=dsta, in_=oap))
                        spilled += 1
            while spilled < NGV:
                v_ghd.wait(scalar, spilled + 1)
                dsta, oap = ghost_spill_ap(spilled)
                mk_gh[spilled] = d_ghp[spilled % 3].dinc(
                    scalar.dma_start(out=dsta, in_=oap))
                spilled += 1
            bar_end(scalar)

        @blk.vector
        def _(vector):
            vector = Drained(vector)
            bar_wait(vector, 1)
            d_kn.wait(vector, 16)
            for h in range(NH):
                vector.reduce_max(
                    out=kmax_s[0:1, h:h + 1],
                    in_=kn8[:].rearrange("a (r h) -> a r h", r=8)[:, :, h],
                    axis=mybir.AxisListType.X)
            v_kn.inc(vector.tensor_copy(out=kmax_s[0:1, :], in_=kmax_s[0:1, :]))
            s_qn.wait(vector, NH + 1)
            vector.tensor_copy(out=kmax_s[32:33, :], in_=kmax_s[0:1, :])
            vector.memset(mneg[0:2, :, :], 0.0)

            def _emit_mneg(hh):
                v_mn.inc(vector.tensor_scalar(
                    out=mneg[0:1, hh % 2, :],
                    in0=qnorm_s[32 * (hh // 8):32 * (hh // 8) + 1,
                                (hh % 8) * 1024:(hh % 8) * 1024 + 1024],
                    scalar1=kmax_s[32 * (hh // 8):32 * (hh // 8) + 1, hh:hh + 1],
                    scalar2=None, op0=OP.mult))

            _emit_mneg(0)
            gq = []
            gnext = 0

            def gpump():
                nonlocal gnext
                if not gq and gnext < NGV:
                    gq.extend(ghost_units(vector, gnext))
                    gnext += 1
                if gq:
                    gq.pop(0)()

            for h in range(NH):
                for j in range(64):
                    gj = h * 64 + j
                    s_exp.wait(vector, gj + 1)
                    if j < 2:
                        if h >= 1 and j == 0:
                            t_z.wait(vector, h)
                        v_z.inc(vector.tensor_copy(out=zacc[:, j, :],
                                                   in_=pst[:, gj % 2, :]))
                    else:
                        v_z.inc(vector.tensor_tensor(
                            out=zacc[:, j % 2, :], in0=pst[:, gj % 2, :],
                            in1=zacc[:, j % 2, :], op=OP.add))
                    busy = True
                    if j == 2 and h + 1 < NH:
                        t_smm.wait(vector, h * 64)
                        _emit_mneg(h + 1)
                    elif j == 3 and h >= 1:
                        t_z.wait(vector, h)
                        v_rz.inc(vector.reciprocal(out=bc_rhs[0:1, :],
                                                   in_=z_ps[0:1, :]))
                    elif j == 8 and h >= 1:
                        t_rzb.wait(vector, h)
                        v_no.inc(vector.tensor_tensor(
                            out=o_bf[:, h - 1, :], in0=o_bf[:, h - 1, :],
                            in1=z_ps[:], op=OP.mult))
                    elif j == 63:
                        t_pv.wait(vector, (h + 1) * 64)
                        v_o.inc(vector.tensor_copy(out=o_bf[:, h, :],
                                                   in_=pv_ps[:]))
                    else:
                        busy = False
                    if not busy:
                        gpump()
            t_z.wait(vector, NH)
            v_rz.inc(vector.reciprocal(out=bc_rhs[0:1, :], in_=z_ps[0:1, :]))
            t_rzb.wait(vector, NH)
            v_no.inc(vector.tensor_tensor(
                out=o_bf[:, NH - 1, :], in0=o_bf[:, NH - 1, :],
                in1=z_ps[:], op=OP.mult))
            while gq or gnext < NGV:
                gpump()
            bar_end(vector)

        @blk.gpsimd
        def _(gpsimd):
            d_ghp[0].wait(gpsimd)
            d_ghp[1].wait(gpsimd)
            d_ghp[2].wait(gpsimd)
            g_cc.inc(gpsimd.collective_compute(
                "AllGather", OP.bypass, ins=[gh_in[:]], outs=[gh_out[:]],
                replica_groups=[list(range(NCORES))]))

        p2s.close()
        p2p.close()
        scQ.close()

        if stage <= 2:
            @blk.sync
            def _(sync):
                bar_wait(sync, 2)
                g_cc.wait(sync, 3)
                d_dbg.dinc(sync.dma_start(out=dbg["d_q"].rearrange(
                    "(c p) t -> p c t", p=128), in_=qs[:]))
                d_dbg.dinc(sync.dma_start(out=dbg["d_k"][:], in_=kv_in_K[:]))
                d_dbg.dinc(sync.dma_start(out=dbg["d_v"][:], in_=kv_in_V[:]))
                d_dbg.dinc(sync.dma_start(out=dbg["d_stats"][0:1, :],
                                          in_=rows[0:1, :]))
                d_dbg.dinc(sync.dma_start(out=dbg["d_stats"][1:2, :],
                                          in_=rows[32:33, :]))
                d_dbg.dinc(sync.dma_start(out=dbg["d_o"].rearrange(
                    "(c p) t -> p c t", p=128), in_=o_bf[:]))
                d_outp[0].dinc(sync.dma_start(out=outT[0:128, :], in_=xf[:, 0, :]))
                d_dbg.wait(sync)
                d_outp[0].wait(sync)
            scA.close()
            return nc, dbg, ctx

        p3s = contextlib.ExitStack()
        wst3 = p3s.enter_context(nc.sbuf_tensor("wst3", [128, 2, NDC, 128], F32))
        wbt3 = p3s.enter_context(nc.sbuf_tensor("wbt3", [128, 2, NDC, 128], BF16))
        pjt = p3s.enter_context(nc.sbuf_tensor("pjt", [128, 2, T], BF16))
        x1t = p3s.enter_context(nc.sbuf_tensor("x1t", [128, 2, T], F32))
        x1b = p3s.enter_context(nc.sbuf_tensor("x1b", [128, 2, T], BF16))
        ldt = p3s.enter_context(nc.sbuf_tensor("ldt", [128, 2, T], BF16))
        p3p = contextlib.ExitStack()
        st2_ps = p3p.enter_context(nc.psum_tensor("st2_ps", [128, 1024], F32))
        pj_ps = p3p.enter_context(nc.psum_tensor("pj_ps", [128, 2, 1024], F32))

        @blk.scalar
        def _(scalar):
            scalar = Drained(scalar)
            bar_wait(scalar, 2)
            for ec in range(NDC):
                v_pe.wait(scalar, ec + 1)
                if ec >= 2:
                    t_stq2.wait(scalar, ec - 1)
                s_sq2.inc(scalar.square(out=sqc[:, ec % 2, :], in_=pjt[:, ec % 2, :]))
                mk_pj[ec] = d_pjp[ec % 2].dinc(scalar.dma_start(
                    out=proj_sc[ec * 128:(ec + 1) * 128, :], in_=pjt[:, ec % 2, :]))
            v_rowpre.wait(scalar, 2)
            s_sqr.inc(scalar.sqrt(out=rows[0:1, :], in_=rows2[32:33, :]))
            for ec in range(NDC):
                v_x1.wait(scalar, ec + 1)
                mk_x1[ec] = d_x1p[ec % 2].dinc(scalar.dma_start(
                    out=x1_sc[ec * 128:(ec + 1) * 128, :], in_=x1t[:, ec % 2, :]))
                v_x1b.wait(scalar, ec + 1)
                if ec >= 2:
                    t_stq3.wait(scalar, ec - 1)
                s_sq3.inc(scalar.square(out=sqc[:, ec % 2, :], in_=x1b[:, ec % 2, :]))
            v_rowpre.wait(scalar, 3)
            s_sqr.inc(scalar.sqrt(out=rows[0:1, :], in_=rows2[32:33, :]))
            bar_end(scalar)

        @blk.sync
        def _(sync):
            bar_wait(sync, 2)
            for ec in range(NDC):
                if ec >= 2:
                    v_w3.wait(sync, ec - 1)
                for sub in range(4):
                    mk_w[("p3", ec)] = d_wp[ec % 2].dinc(sync.dma_start(
                        out=wst3[:, ec % 2, sub * 4:(sub + 1) * 4, :],
                        in_=wprojT[sub * 512:(sub + 1) * 512,
                                   ec * 128:(ec + 1) * 128]
                        .rearrange("(c p) e -> p c e", p=128)))
            for ec in range(NDC):
                if ec >= 2:
                    v_x1.wait(sync, ec - 1)
                mk_x[("p3", ec)] = d_xp[ec % 2].dinc(sync.dma_start(
                    out=xf[:, ec % 2, :],
                    in_=xT[ec * 128:(ec + 1) * 128, :]))
                d_pjp[ec % 2].wait(sync, mk_pj[ec])
                mk_ldt[ec] = d_ldtp[ec % 2].dinc(sync.dma_start(
                    out=ldt[:, ec % 2, :],
                    in_=proj_sc[ec * 128:(ec + 1) * 128, :]))
            d_x1p[0].wait(sync)
            d_x1p[1].wait(sync)
            for ec in range(NDC):
                if ec >= 2:
                    v_l2.wait(sync, ec - 1)
                mk_lx[ec] = d_lxp[ec % 2].dinc(sync.dma_start(
                    out=xf[:, ec % 2, :],
                    in_=x1_sc[ec * 128:(ec + 1) * 128, :]))
            bar_end(sync)

        @blk.tensor
        def _(tensor):
            bar_wait(tensor, 2)
            for ec in range(NDC):
                v_w3.wait(tensor, ec + 1)
                if ec >= 2:
                    v_pe.wait(tensor, ec - 1)
                for dc in range(NDC):
                    for half in range(2):
                        i = tensor.matmul(
                            pj_ps[:, ec % 2, half * 512:(half + 1) * 512],
                            wbt3[:, ec % 2, dc, :],
                            o_bf[:, dc, half * 512:(half + 1) * 512],
                            start=(dc == 0), stop=(dc == NDC - 1))
                t_pj.inc(i)
                v_pe.wait(tensor, ec + 1)
                for half in range(2):
                    tensor.matmul(st2_ps[0:1, half * 512:(half + 1) * 512],
                                  ones_bf[:],
                                  pjt[:, ec % 2, half * 512:(half + 1) * 512],
                                  start=(ec == 0), stop=(ec == NDC - 1))
                s_sq2.wait(tensor, ec + 1)
                for half in range(2):
                    i = tensor.matmul(st2_ps[32:33, half * 512:(half + 1) * 512],
                                      ones_bf[:],
                                      sqc[:, ec % 2, half * 512:(half + 1) * 512],
                                      start=(ec == 0), stop=(ec == NDC - 1))
                t_stq2.inc(i)
            v_rowb.wait(tensor, 2)
            for half in range(2):
                i = tensor.matmul(st2_ps[:, half * 512:(half + 1) * 512],
                                  bc_lhs[0:2, :], bc_rhs[0:2, half * 512:(half + 1) * 512],
                                  start=True, stop=True)
            t_bc.inc(i, 2)
            mk_bc[("pa", 0)] = t_bc.n
            v_bca.wait(tensor, 2)
            for half in range(2):
                i = tensor.matmul(st2_ps[:, half * 512:(half + 1) * 512],
                                  bc_lhs[32:34, :],
                                  bc_rhs[32:34, half * 512:(half + 1) * 512],
                                  start=True, stop=True)
            t_bc.inc(i, 2)
            mk_bc[("pa", 1)] = t_bc.n
            for ec in range(NDC):
                v_x1b.wait(tensor, ec + 1)
                for half in range(2):
                    tensor.matmul(st2_ps[0:1, half * 512:(half + 1) * 512],
                                  ones_bf[:],
                                  x1b[:, ec % 2, half * 512:(half + 1) * 512],
                                  start=(ec == 0), stop=(ec == NDC - 1))
                s_sq3.wait(tensor, ec + 1)
                for half in range(2):
                    i = tensor.matmul(st2_ps[32:33, half * 512:(half + 1) * 512],
                                      ones_bf[:],
                                      sqc[:, ec % 2, half * 512:(half + 1) * 512],
                                      start=(ec == 0), stop=(ec == NDC - 1))
                t_stq3.inc(i)
            v_rowb.wait(tensor, 3)
            for half in range(2):
                i = tensor.matmul(st2_ps[:, half * 512:(half + 1) * 512],
                                  bc_lhs[0:2, :], bc_rhs[0:2, half * 512:(half + 1) * 512],
                                  start=True, stop=True)
            t_bc.inc(i, 2)
            mk_bc[("l2", 0)] = t_bc.n
            v_bca.wait(tensor, 3)
            for half in range(2):
                i = tensor.matmul(st2_ps[:, half * 512:(half + 1) * 512],
                                  bc_lhs[32:34, :],
                                  bc_rhs[32:34, half * 512:(half + 1) * 512],
                                  start=True, stop=True)
            t_bc.inc(i, 2)
            mk_bc[("l2", 1)] = t_bc.n
            bar_end(tensor)

        def rows_finalize(vector, st_psrc, pre_ch, sqr_ch, sqr_target):
            inv = 1.0 / D
            vector.tensor_scalar(out=rows2[0:1, :], in0=st_psrc[0:1, :],
                                 scalar1=inv, scalar2=None, op0=OP.mult)
            vector.tensor_scalar(out=rows[32:33, :], in0=st_psrc[32:33, :],
                                 scalar1=inv, scalar2=EPS, op0=OP.mult, op1=OP.add)
            vector.tensor_tensor(out=rows2[32:33, :], in0=rows2[0:1, :],
                                 in1=rows2[0:1, :], op=OP.mult)
            vector.tensor_tensor(out=rows[32:33, :], in0=rows[32:33, :],
                                 in1=rows2[32:33, :], op=OP.subtract)
            pre_ch.inc(vector.reciprocal(out=rows2[32:33, :],
                                         in_=rows[32:33, :]))
            sqr_ch.wait(vector, sqr_target)
            vector.tensor_tensor(out=rows[32:33, :], in0=rows2[0:1, :],
                                 in1=rows[0:1, :], op=OP.mult)
            vector.tensor_copy(out=bc_rhs[0:1, :], in_=rows[0:1, :])
            return vector.tensor_copy(out=bc_rhs[32:33, :],
                                      in_=rows[32:33, :])

        @blk.vector
        def _(vector):
            vector = Drained(vector)
            bar_wait(vector, 2)
            for ec in range(NDC + 2):
                if ec < NDC:
                    d_wp[ec % 2].wait(vector, mk_w[("p3", ec)])
                    if ec >= 2:
                        t_pj.wait(vector, ec - 1)
                    v_w3.inc(vector.tensor_copy(out=wbt3[:, ec % 2, :, :],
                                                in_=wst3[:, ec % 2, :, :]))
                if ec >= 2:
                    e0 = ec - 2
                    t_pj.wait(vector, e0 + 1)
                    if e0 >= 2:
                        d_pjp[e0 % 2].wait(vector, mk_pj[e0 - 2])
                        t_stq2.wait(vector, e0 - 1)
                    v_pe.inc(vector.tensor_copy(out=pjt[:, e0 % 2, :],
                                                in_=pj_ps[:, e0 % 2, :]))
            t_stq2.wait(vector, NDC)
            v_rowb.inc(rows_finalize(vector, st2_ps, v_rowpre, s_sqr, 2))
            t_bc.wait(vector, mk_bc[("pa", 0)])
            v_bca.inc(vector.tensor_copy(out=bc_a[:], in_=st2_ps[:]))
            t_bc.wait(vector, mk_bc[("pa", 1)])
            v_bcb.inc(vector.tensor_copy(out=bc_b[:], in_=st2_ps[:]))
            for ec in range(NDC):
                d_xp[ec % 2].wait(vector, mk_x[("p3", ec)])
                d_ldtp[ec % 2].wait(vector, mk_ldt[ec])
                if ec >= 2:
                    d_x1p[ec % 2].wait(vector, mk_x1[ec - 2])
                    t_stq3.wait(vector, ec - 1)
                vector.tensor_tensor(out=tmp_f[:], in0=ldt[:, ec % 2, :],
                                     in1=bc_a[:], op=OP.mult)
                vector.tensor_tensor(out=tmp_f[:], in0=tmp_f[:],
                                     in1=bc_b[:], op=OP.subtract)
                v_x1.inc(vector.tensor_tensor(out=x1t[:, ec % 2, :], in0=tmp_f[:],
                                              in1=xf[:, ec % 2, :], op=OP.add))
                v_x1b.inc(vector.tensor_copy(out=x1b[:, ec % 2, :],
                                             in_=x1t[:, ec % 2, :]))
            t_stq3.wait(vector, NDC)
            v_rowb.inc(rows_finalize(vector, st2_ps, v_rowpre, s_sqr, 3))
            t_bc.wait(vector, mk_bc[("l2", 0)])
            v_bca.inc(vector.tensor_copy(out=bc_a[:], in_=st2_ps[:]))
            t_bc.wait(vector, mk_bc[("l2", 1)])
            v_bcb.inc(vector.tensor_copy(out=bc_b[:], in_=st2_ps[:]))
            for ec in range(NDC):
                d_lxp[ec % 2].wait(vector, mk_lx[ec])
                vector.tensor_tensor(out=tmp_f[:], in0=xf[:, ec % 2, :],
                                     in1=bc_a[:], op=OP.mult)
                v_l2.inc(vector.tensor_tensor(out=x_bf[:, ec, :], in0=tmp_f[:],
                                              in1=bc_b[:], op=OP.subtract))
            bar_end(vector)

        p3s.close()
        p3p.close()
        scA.close()

        if stage == 3:
            @blk.sync
            def _(sync):
                bar_wait(sync, 3)
                d_dbg.dinc(sync.dma_start(out=dbg["d_x1"][:], in_=x1_sc[:]))
                d_dbg.dinc(sync.dma_start(out=dbg["d_ln2"].rearrange(
                    "(c p) t -> p c t", p=128), in_=x_bf[:]))
                d_dbg.dinc(sync.dma_start(out=dbg["d_proj"][:], in_=proj_sc[:]))
                d_dbg.dinc(sync.dma_start(out=dbg["d_rows"][0:1, :],
                                          in_=bc_a[0:1, :]))
                d_dbg.dinc(sync.dma_start(out=dbg["d_rows"][1:2, :],
                                          in_=bc_b[0:1, :]))
                d_outp[0].dinc(sync.dma_start(out=outT[0:128, :], in_=xf[:, 0, :]))
                d_dbg.wait(sync)
                d_outp[0].wait(sync)
            return nc, dbg, ctx

        p4s = contextlib.ExitStack()
        h_bf = p4s.enter_context(nc.sbuf_tensor("h_bf", [128, NIC, T], BF16))
        gt_st = p4s.enter_context(nc.sbuf_tensor("gt_st", [128, 2, 128], BF16))
        up_st = p4s.enter_context(nc.sbuf_tensor("up_st", [128, 2, 128], BF16))
        sil_t = p4s.enter_context(nc.sbuf_tensor("sil_t", [128, T], BF16))
        p4p = contextlib.ExitStack()
        g_ps = p4p.enter_context(nc.psum_tensor("g_ps", [128, 2, 1024], F32))
        u_ps = p4p.enter_context(nc.psum_tensor("u_ps", [128, 2, 1024], F32))

        @blk.sync
        def _(sync):
            bar_wait(sync, 3)
            g_cc.wait(sync, 3)
            for ic in range(NIC):
                for dc in range(NDC):
                    un = ic * NDC + dc
                    if un >= 2:
                        t_guu.wait(sync, un - 1)
                    r, half = divmod(dc, 2)
                    d_gwp[un % 2].dinc(sync.dma_start(
                        out=gt_st[:, un % 2, :],
                        in_=gh_out_g(r)[half * 128:(half + 1) * 128,
                                        ic * 128:(ic + 1) * 128]))
                    mk_gw[un] = d_gwp[un % 2].dinc(sync.dma_start(
                        out=up_st[:, un % 2, :],
                        in_=gh_out_u(r)[half * 128:(half + 1) * 128,
                                        ic * 128:(ic + 1) * 128]))
            bar_end(sync)

        @blk.tensor
        def _(tensor):
            bar_wait(tensor, 3)
            for ic in range(NIC):
                if ic >= 2:
                    v_h.wait(tensor, ic - 1)
                for dc in range(NDC):
                    un = ic * NDC + dc
                    d_gwp[un % 2].wait(tensor, mk_gw[un])
                    for half in range(2):
                        tensor.matmul(
                            g_ps[:, ic % 2, half * 512:(half + 1) * 512],
                            gt_st[:, un % 2, :],
                            x_bf[:, dc, half * 512:(half + 1) * 512],
                            start=(dc == 0), stop=(dc == NDC - 1))
                    for half in range(2):
                        i = tensor.matmul(
                            u_ps[:, ic % 2, half * 512:(half + 1) * 512],
                            up_st[:, un % 2, :],
                            x_bf[:, dc, half * 512:(half + 1) * 512],
                            start=(dc == 0), stop=(dc == NDC - 1))
                    t_guu.inc(i)
            bar_end(tensor)

        @blk.scalar
        def _(scalar):
            scalar = Drained(scalar)
            bar_wait(scalar, 3)
            for ic in range(NIC):
                t_guu.wait(scalar, (ic + 1) * NDC)
                s_si.inc(scalar.activation(out=sil_t[:], in_=g_ps[:, ic % 2, :],
                                           func=AF.Silu))
            bar_end(scalar)

        @blk.vector
        def _(vector):
            vector = Drained(vector)
            bar_wait(vector, 3)
            for ic in range(NIC):
                s_si.wait(vector, ic + 1)
                v_h.inc(vector.tensor_tensor(out=h_bf[:, ic, :], in0=sil_t[:],
                                             in1=u_ps[:, ic % 2, :], op=OP.mult))
            bar_end(vector)

        p4p.close()

        if stage == 4:
            @blk.sync
            def _(sync):
                bar_wait(sync, 4)
                d_dbg.dinc(sync.dma_start(out=dbg["d_gw"][:], in_=gh_in[:]))
                d_dbg.dinc(sync.dma_start(out=dbg["d_h"].rearrange(
                    "(c p) t -> p c t", p=128), in_=h_bf[:]))
                d_dbg.dinc(sync.dma_start(out=dbg["d_x1"][:], in_=x1_sc[:]))
                d_dbg.dinc(sync.dma_start(out=dbg["d_ln2"].rearrange(
                    "(c p) t -> p c t", p=128), in_=x_bf[:]))
                d_outp[0].dinc(sync.dma_start(out=outT[0:128, :], in_=xf[:, 0, :]))
                d_dbg.wait(sync)
                d_outp[0].wait(sync)
            p4s.close()
            return nc, dbg, ctx

        p5s = contextlib.ExitStack()
        dn_st = p5s.enter_context(nc.sbuf_tensor("dn_st", [128, 2, 256], BF16))
        dnt = p5s.enter_context(nc.sbuf_tensor("dnt", [128, 2, T], BF16))
        outst = p5s.enter_context(nc.sbuf_tensor("outst", [128, 2, T], F32))
        x1t5 = p5s.enter_context(nc.sbuf_tensor("x1t5", [128, 2, T], F32))
        p5p = contextlib.ExitStack()
        dn_ps = p5p.enter_context(nc.psum_tensor("dn_ps", [128, 2, 1024], F32))
        st3_ps = p5p.enter_context(nc.psum_tensor("st3_ps", [128, 1024], F32))

        @blk.scalar
        def _(scalar):
            scalar = Drained(scalar)
            bar_wait(scalar, 4)
            for ec in range(NDC):
                v_de.wait(scalar, ec + 1)
                if ec >= 2:
                    t_stq4.wait(scalar, ec - 1)
                s_sq4.inc(scalar.square(out=sqc[:, ec % 2, :], in_=dnt[:, ec % 2, :]))
                mk_dn[ec] = d_dnp[ec % 2].dinc(scalar.dma_start(
                    out=down_sc[ec * 128:(ec + 1) * 128, :], in_=dnt[:, ec % 2, :]))
            v_rowpre.wait(scalar, 4)
            s_sqr.inc(scalar.sqrt(out=rows[0:1, :], in_=rows2[32:33, :]))

        @blk.sync
        def _(sync):
            bar_wait(sync, 4)
            for ep in range(8):
                for ic in range(NIC):
                    un = ep * NIC + ic
                    if un >= 2:
                        t_dnu.wait(sync, un - 1)
                    mk_gw[("dn", un)] = d_gwp[un % 2].dinc(sync.dma_start(
                        out=dn_st[:, un % 2, :],
                        in_=gh_out_d(ep)[ic * 128:(ic + 1) * 128, :]))
            d_dnp[0].wait(sync)
            d_dnp[1].wait(sync)
            d_x1p[0].wait(sync)
            d_x1p[1].wait(sync)
            t_stq4.wait(sync, 16)
            for ec in range(NDC + 1):
                if ec < NDC:
                    if ec >= 2:
                        v_out.wait(sync, ec - 1)
                    d_l5p[ec % 2].dinc(sync.dma_start(
                        out=x1t5[:, ec % 2, :],
                        in_=x1_sc[ec * 128:(ec + 1) * 128, :]))
                    mk_l5[ec] = d_l5p[ec % 2].dinc(sync.dma_start(
                        out=dnt[:, ec % 2, :],
                        in_=down_sc[ec * 128:(ec + 1) * 128, :]))
                if ec >= 1:
                    e1 = ec - 1
                    v_out.wait(sync, e1 + 1)
                    mk_out[e1] = d_outp[e1 % 2].dinc(sync.dma_start(
                        out=outT[e1 * 128:(e1 + 1) * 128, :],
                        in_=outst[:, e1 % 2, :]))
            d_outp[0].wait(sync)
            d_outp[1].wait(sync)

        @blk.tensor
        def _(tensor):
            bar_wait(tensor, 4)
            for ep in range(8):
                if ep >= 1:
                    v_de.wait(tensor, 2 * ep)
                for ic in range(NIC):
                    un = ep * NIC + ic
                    d_gwp[un % 2].wait(tensor, mk_gw[("dn", un)])
                    for ecl in range(2):
                        for half in range(2):
                            i = tensor.matmul(
                                dn_ps[:, ecl, half * 512:(half + 1) * 512],
                                dn_st[:, un % 2, ecl * 128:(ecl + 1) * 128],
                                h_bf[:, ic, half * 512:(half + 1) * 512],
                                start=(ic == 0), stop=(ic == NIC - 1))
                    t_dnu.inc(i)
                for ecl in range(2):
                    ec = 2 * ep + ecl
                    v_de.wait(tensor, ec + 1)
                    for half in range(2):
                        tensor.matmul(st3_ps[0:1, half * 512:(half + 1) * 512],
                                      ones_bf[:],
                                      dnt[:, ec % 2, half * 512:(half + 1) * 512],
                                      start=(ec == 0), stop=(ec == NDC - 1))
                    s_sq4.wait(tensor, ec + 1)
                    for half in range(2):
                        i = tensor.matmul(st3_ps[32:33, half * 512:(half + 1) * 512],
                                          ones_bf[:],
                                          sqc[:, ec % 2, half * 512:(half + 1) * 512],
                                          start=(ec == 0), stop=(ec == NDC - 1))
                    t_stq4.inc(i)
            v_rowb.wait(tensor, 4)
            for half in range(2):
                i = tensor.matmul(st3_ps[:, half * 512:(half + 1) * 512],
                                  bc_lhs[0:2, :], bc_rhs[0:2, half * 512:(half + 1) * 512],
                                  start=True, stop=True)
            t_bc.inc(i, 2)
            mk_bc[("l3", 0)] = t_bc.n
            v_bca.wait(tensor, 4)
            for half in range(2):
                i = tensor.matmul(st3_ps[:, half * 512:(half + 1) * 512],
                                  bc_lhs[32:34, :],
                                  bc_rhs[32:34, half * 512:(half + 1) * 512],
                                  start=True, stop=True)
            t_bc.inc(i, 2)
            mk_bc[("l3", 1)] = t_bc.n

        @blk.vector
        def _(vector):
            vector = Drained(vector)
            bar_wait(vector, 4)
            for ep in range(8):
                t_dnu.wait(vector, (ep + 1) * NIC)
                for ecl in range(2):
                    ec = 2 * ep + ecl
                    if ec >= 2:
                        t_stq4.wait(vector, ec - 1)
                        d_dnp[ec % 2].wait(vector, mk_dn[ec - 2])
                    v_de.inc(vector.tensor_copy(out=dnt[:, ec % 2, :],
                                                in_=dn_ps[:, ecl, :]))
            t_stq4.wait(vector, NDC)
            v_rowb.inc(rows_finalize(vector, st3_ps, v_rowpre, s_sqr, 4))
            t_bc.wait(vector, mk_bc[("l3", 0)])
            v_bca.inc(vector.tensor_copy(out=bc_a[:], in_=st3_ps[:]))
            t_bc.wait(vector, mk_bc[("l3", 1)])
            v_bcb.inc(vector.tensor_copy(out=bc_b[:], in_=st3_ps[:]))
            for ec in range(NDC):
                d_l5p[ec % 2].wait(vector, mk_l5[ec])
                if ec >= 2:
                    d_outp[ec % 2].wait(vector, mk_out[ec - 2])
                vector.tensor_tensor(out=tmp_f[:], in0=dnt[:, ec % 2, :],
                                     in1=bc_a[:], op=OP.mult)
                vector.tensor_tensor(out=tmp_f[:], in0=tmp_f[:],
                                     in1=bc_b[:], op=OP.subtract)
                v_out.inc(vector.tensor_tensor(out=outst[:, ec % 2, :],
                                               in0=tmp_f[:], in1=x1t5[:, ec % 2, :],
                                               op=OP.add))

        p5p.close()
        p5s.close()
        p4s.close()
        return nc, dbg, ctx

    raise AssertionError("unreachable")


_STAGE = 5


def kernel(L0_attn_x, L0_attn_w_qkv, L0_attn_w_proj, L0_ffn_lut,
           L0_ffn_gate_idx, L0_ffn_up_idx, L0_ffn_down_idx):
    x = np.asarray(L0_attn_x)[0]
    xTf = np.ascontiguousarray(x.T.astype(np.float32, copy=False))
    wqkvT = np.ascontiguousarray(np.asarray(L0_attn_w_qkv).T.astype(np.float32, copy=False))
    wprojT = np.ascontiguousarray(np.asarray(L0_attn_w_proj).T.astype(np.float32, copy=False))
    lutb = np.ascontiguousarray(np.broadcast_to(
        np.asarray(L0_ffn_lut).astype(np.float32, copy=False)[None, :], (128, 16)))
    gidxT = np.ascontiguousarray(np.asarray(L0_ffn_gate_idx).T)
    uidxT = np.ascontiguousarray(np.asarray(L0_ffn_up_idx).T)
    didxT = np.ascontiguousarray(np.asarray(L0_ffn_down_idx).T)

    nc, dbg, ctx = build_nc(stage=_STAGE)
    in_maps = []
    for c in range(NCORES):
        in_maps.append({
            "xT": np.ascontiguousarray(xTf[:, c * T:(c + 1) * T]),
            "wqkvT": wqkvT,
            "wprojT": wprojT,
            "lutb": lutb,
            "gidxT": np.ascontiguousarray(gidxT[c * 256:(c + 1) * 256, :]),
            "uidxT": np.ascontiguousarray(uidxT[c * 256:(c + 1) * 256, :]),
            "didxT": np.ascontiguousarray(didxT[:, c * 256:(c + 1) * 256]),
        })
    res = run_bass_kernel_spmd(nc, in_maps, core_ids=list(range(NCORES)))
    kernel.last_results = res
    outTs = [res.results[c]["outT"] for c in range(NCORES)]
    out = np.concatenate(outTs, axis=1).T[None, :, :]
    return np.ascontiguousarray(out.astype(np.float32, copy=False))


# version 04
# speedup vs baseline: 1.3775x; 1.1611x over previous
import contextlib
import math

import numpy as np
import ml_dtypes

import concourse.bass as bass
import concourse.mybir as mybir
from concourse.bass_utils import run_bass_kernel_spmd

F32 = mybir.dt.float32
BF16 = mybir.dt.bfloat16
I32 = mybir.dt.int32
AF = mybir.ActivationFunctionType
OP = mybir.AluOpType

NCORES = 8
T = 1024
D = 2048
NDC = 16
NH = 16
EQKV = 6144
INTER = 5504
NIC = 43
SCALE_ATT = 1.0 / math.sqrt(128.0)
EXP_BIAS = 40.0
EPS = 1e-5

GSH_GATE = 256 * INTER
GSH_DOWN = INTER * 256
GH_BLK = 2 * GSH_GATE + GSH_DOWN
KV_BLK = 2 * D * T + 16

GJOBS_V = ([("gate", s) for s in range(16)] + [("up", s) for s in range(16)]
           + [("down", s) for s in range(11)])
NGV = len(GJOBS_V)


class Ch:
    def __init__(self, ctx, nc, name):
        self.sem = ctx.enter_context(nc.semaphore(name))
        self.n = 0

    def inc(self, instr, amt=1):
        instr.then_inc(self.sem, amt)
        self.n += amt
        return self.n

    def dinc(self, instr):
        return self.inc(instr, 16)

    def wait(self, eng, n=None):
        eng.wait_ge(self.sem, self.n if n is None else n)


class Drained:
    _COMPUTE = {"tensor_copy", "tensor_tensor", "tensor_scalar",
                "scalar_tensor_tensor", "memset", "reciprocal", "activation",
                "square", "sqrt", "copy", "mul", "add", "select", "iota",
                "tensor_reduce", "reduce_max", "reduce_sum", "dma_start"}

    def __init__(self, eng):
        self._e = eng

    def __getattr__(self, name):
        f = getattr(self._e, name)
        if name in Drained._COMPUTE:
            def wrap(*a, **k):
                self._e.drain()
                return f(*a, **k)
            return wrap
        return f


def build_nc(stage=5):
    nc = bass.Bass("TRN2", target_bir_lowering=False,
                   detect_race_conditions=False)

    xT = nc.dram_tensor("xT", [D, T], F32, kind="ExternalInput").ap()
    wqkvT = nc.dram_tensor("wqkvT", [D, EQKV], F32, kind="ExternalInput").ap()
    wprojT = nc.dram_tensor("wprojT", [D, D], F32, kind="ExternalInput").ap()
    lutb = nc.dram_tensor("lutb", [128, 16], F32, kind="ExternalInput").ap()
    gidxT = nc.dram_tensor("gidxT", [256, INTER], I32, kind="ExternalInput").ap()
    uidxT = nc.dram_tensor("uidxT", [256, INTER], I32, kind="ExternalInput").ap()
    didxT = nc.dram_tensor("didxT", [INTER, 256], I32, kind="ExternalInput").ap()
    outT = nc.dram_tensor("outT", [D, T], F32, kind="ExternalOutput").ap()

    dbg = {}
    if stage < 5:
        def _dt(name, shape, dtyp):
            dbg[name] = nc.dram_tensor(name, shape, dtyp, kind="ExternalOutput").ap()
        _dt("d_q", [D, T], BF16)
        _dt("d_k", [D, T], BF16)
        _dt("d_v", [T, D], BF16)
        _dt("d_stats", [2, T], F32)
        _dt("d_o", [D, T], BF16)
        if stage >= 3:
            _dt("d_x1", [D, T], F32)
            _dt("d_ln2", [D, T], BF16)
            _dt("d_proj", [D, T], BF16)
            _dt("d_rows", [2, T], F32)
        if stage >= 4:
            _dt("d_gw", [GH_BLK], BF16)
            _dt("d_h", [INTER, T], BF16)

    KA = D * T + 16
    kv_in = nc.dram_tensor("kv_in", [KV_BLK], BF16, kind="Internal").ap()
    kvA_out = nc.dram_tensor("kvA_out", [NCORES * KA], BF16, kind="Internal").ap()
    kvB_out = nc.dram_tensor("kvB_out", [NCORES * D * T], BF16, kind="Internal").ap()
    gh_in = nc.dram_tensor("gh_in", [GH_BLK], BF16, kind="Internal").ap()
    GA = 2 * GSH_GATE
    ghA_out = nc.dram_tensor("ghA_out", [NCORES * GA], BF16, kind="Internal").ap()
    ghB_out = nc.dram_tensor("ghB_out", [NCORES * GSH_DOWN], BF16, kind="Internal").ap()
    st_bnc = nc.dram_tensor("st_bnc", [2048], F32, kind="Internal").ap()
    x1_sc = nc.dram_tensor("x1_sc", [D, T], F32, kind="Internal").ap()
    proj_sc = nc.dram_tensor("proj_sc", [D, T], BF16, kind="Internal").ap()
    down_sc = nc.dram_tensor("down_sc", [D, T], BF16, kind="Internal").ap()

    kv_in_K = kv_in[0:D * T].rearrange("(e t) -> e t", t=T)
    kv_in_N = kv_in[D * T:D * T + 16]
    kv_in_V = kv_in[KA:KA + D * T].rearrange("(t e) -> t e", e=D)
    kv_out_K = lambda r: kvA_out[r * KA: r * KA + D * T].rearrange("(e t) -> e t", t=T)
    kv_out_V = lambda r: kvB_out[r * D * T: (r + 1) * D * T].rearrange("(t e) -> t e", e=D)
    gh_in_g = gh_in[0:GSH_GATE].rearrange("(d i) -> d i", i=INTER)
    gh_in_u = gh_in[GSH_GATE:2 * GSH_GATE].rearrange("(d i) -> d i", i=INTER)
    gh_in_d = gh_in[2 * GSH_GATE:].rearrange("(i e) -> i e", e=256)
    gh_out_g = lambda r: ghA_out[r * GA: r * GA + GSH_GATE].rearrange("(d i) -> d i", i=INTER)
    gh_out_u = lambda r: ghA_out[r * GA + GSH_GATE: (r + 1) * GA].rearrange("(d i) -> d i", i=INTER)
    gh_out_d = lambda r: ghB_out[r * GSH_DOWN: (r + 1) * GSH_DOWN].rearrange("(i e) -> i e", e=256)

    def ghost_job(kind, sl):
        if kind == "gate":
            return gidxT[:, sl * 344:(sl + 1) * 344], gh_in_g[:, sl * 344:(sl + 1) * 344], 2, 344
        if kind == "up":
            return uidxT[:, sl * 344:(sl + 1) * 344], gh_in_u[:, sl * 344:(sl + 1) * 344], 2, 344
        r0 = sl * 4 * 128
        nrc = 4 if sl < 10 else 3
        return didxT[r0:r0 + nrc * 128, :], gh_in_d[r0:r0 + nrc * 128, :], nrc, 256

    ctx = contextlib.ExitStack()
    with ctx:
        sbuf = lambda name, shape, dtp: ctx.enter_context(nc.sbuf_tensor(name, shape, dtp))
        consts = sbuf("consts", [128, 4], F32)
        ones_bf = sbuf("ones_bf", [128, 1], BF16)
        bc_lhs = sbuf("bc_lhs", [34, 128], F32)
        bc_rhs = sbuf("bc_rhs", [34, 1024], F32)
        lut_s = sbuf("lut_s", [128, 16], F32)
        x_bf = sbuf("x_bf", [128, NDC, T], BF16)
        xf = sbuf("xf", [128, 2, T], F32)
        sqc = sbuf("sqc", [128, 2, T], BF16)
        rows = sbuf("rows", [33, 1024], F32)
        rows2 = sbuf("rows2", [33, 1024], F32)
        bc_a = sbuf("bc_a", [128, T], F32)
        bc_b = sbuf("bc_b", [128, T], F32)
        statsT = sbuf("statsT", [128, 16], F32)
        s_neg = sbuf("s_neg", [128, 48], F32)
        tmp_f = sbuf("tmp_f", [128, T], F32)
        ones2_bf = sbuf("ones2_bf", [2, 128], BF16)
        qnorm_s = sbuf("qnorm_s", [33, 8192], BF16)
        kmax_s = sbuf("kmax_s", [33, 16], F32)
        knrow = sbuf("knrow", [1, 16], BF16)
        kn8 = sbuf("kn8", [1, 128], BF16)

        ch = lambda name: Ch(ctx, nc, name)
        chp = lambda name, n: [Ch(ctx, nc, f"{name}{i}") for i in range(n)]
        d_lut = ch("d_lut")
        d_xp = chp("d_x", 2); mk_x = {}
        d_wp = chp("d_w", 2); mk_w = {}
        d_wvp = chp("d_wv", 2); mk_wv = {}
        d_kvp = chp("d_kv", 2); mk_kv = {}
        d_gip = chp("d_gi", 3); mk_gi = {}
        d_bnc = ch("d_bnc")
        d_ksp = chp("d_ks", 2); mk_ks = {}
        d_vsp = chp("d_vs", 2); mk_vs = {}
        d_knt = ch("d_knt"); d_kn = ch("d_kn")
        d_ghp = chp("d_gh", 3); mk_gh = {}
        d_pjp = chp("d_pj", 2); mk_pj = {}
        d_x1p = chp("d_x1", 2); mk_x1 = {}
        d_dnp = chp("d_dn", 2); mk_dn = {}
        d_ldtp = chp("d_ldt", 2); mk_ldt = {}
        d_gwp = chp("d_gw", 4); mk_gw = {}
        mk_bc = {}
        d_lxp = chp("d_lx", 2); mk_lx = {}
        d_l5p = chp("d_l5", 2); mk_l5 = {}
        d_outp = chp("d_out", 2); mk_out = {}
        d_dbg = ch("d_dbg")
        v_cc = ch("v_cc"); v_xc = ch("v_xc"); v_wq = ch("v_wq"); v_wv = ch("v_wv")
        v_sx = ch("v_sx"); v_rowpre = ch("v_rowpre"); v_rowb = ch("v_rowb")
        v_bca = ch("v_bca"); v_bcb = ch("v_bcb")
        v_svst = ch("v_svst"); v_svbc = ch("v_svbc")
        v_epi = ch("v_epi"); v_vepi = ch("v_vepi")
        v_gc = ch("v_gc"); v_ghd = ch("v_ghd")
        s_k2 = ch("s_k2"); s_q2 = ch("s_q2"); s_qn = ch("s_qn")
        t_kn = ch("t_kn"); t_qn = ch("t_qn"); v_kn = ch("v_kn")
        v_mn = ch("v_mn")
        v_z = ch("v_z"); v_rz = ch("v_rz"); v_no = ch("v_no"); v_o = ch("v_o")
        v_w3 = ch("v_w3"); v_pe = ch("v_pe")
        v_x1 = ch("v_x1"); v_x1b = ch("v_x1b"); v_l2 = ch("v_l2")
        v_h = ch("v_h"); v_de = ch("v_de"); v_out = ch("v_out")
        s_sq = ch("s_sq"); s_exp = ch("s_exp"); s_si = ch("s_si")
        s_sqr = ch("s_sqr")
        s_sq2 = ch("s_sq2"); s_sq3 = ch("s_sq3"); s_sq4 = ch("s_sq4")
        t_stq = ch("t_stq"); t_qk = ch("t_qk"); t_s = ch("t_s")
        t_bc = ch("t_bc"); t_v = ch("t_v"); t_sv = ch("t_sv"); t_svb = ch("t_svb")
        t_smm = ch("t_smm"); t_pv = ch("t_pv"); t_z = ch("t_z"); t_rzb = ch("t_rzb")
        t_pj = ch("t_pj"); t_stq2 = ch("t_stq2")
        t_stq3 = ch("t_stq3")
        t_guu = ch("t_guu")
        t_dnu = ch("t_dnu")
        t_stq4 = ch("t_stq4")
        g_cc = ch("g_cc")
        bar = ch("bar")

        blk = ctx.enter_context(nc.Block())

        def bar_end(eng):
            eng.drain()
            eng.sem_inc(bar.sem, 1)

        def bar_wait(eng, phase):
            eng.wait_ge(bar.sem, 4 * phase)

        scA = contextlib.ExitStack()
        o_bf = scA.enter_context(nc.sbuf_tensor("o_bf", [128, NDC, T], BF16))
        gvi = scA.enter_context(nc.sbuf_tensor("gvi", [128, 3, 1024], I32))
        gvb = scA.enter_context(nc.sbuf_tensor("gvb", [128, 1024], BF16))
        gvm = scA.enter_context(nc.sbuf_tensor("gvm", [128, 1024], BF16))
        gvo = scA.enter_context(nc.sbuf_tensor("gvo", [128, 3, 1024], BF16))
        scQ = contextlib.ExitStack()
        qs = scQ.enter_context(nc.sbuf_tensor("qs", [128, NH, T], BF16))

        def ghost_spill_ap(j):
            kind, sl = GJOBS_V[j]
            _, dst, a, b = ghost_job(kind, sl)
            n = a * b
            oap = gvo[:, j % 3, 0:n].rearrange("p (a b) -> p a b", a=a)
            return dst.rearrange("(c p) i -> p c i", p=128), oap

        def ghost_units(eng, j):
            kind, sl = GJOBS_V[j]
            _, dst, a, b = ghost_job(kind, sl)
            n = a * b
            iap = gvi[:, j % 3, 0:n].rearrange("p (a b) -> p a b", a=a)
            bap = gvb[:, 0:n].rearrange("p (a b) -> p a b", a=a)
            map_ = gvm[:, 0:n].rearrange("p (a b) -> p a b", a=a)
            oap = gvo[:, j % 3, 0:n].rearrange("p (a b) -> p a b", a=a)

            def u_cast():
                d_gip[j % 3].wait(eng, mk_gi[j])
                d_lut.wait(eng, 16)
                v_gc.inc(eng.tensor_copy(out=bap, in_=iap))

            def u_k(k):
                eng.tensor_scalar(out=map_, in0=bap, scalar1=float(k),
                                  scalar2=lut_s[:, k:k + 1], op0=OP.is_equal,
                                  op1=OP.mult)
                if k == 0:
                    if j >= 3:
                        d_ghp[j % 3].wait(eng, mk_gh[j - 3])
                    eng.tensor_copy(out=oap, in_=map_)
                elif k < 15:
                    eng.scalar_tensor_tensor(out=oap, in0=map_, scalar=1.0,
                                             in1=oap, op0=OP.mult, op1=OP.add)
                else:
                    v_ghd.inc(eng.scalar_tensor_tensor(
                        out=oap, in0=map_, scalar=1.0, in1=oap,
                        op0=OP.mult, op1=OP.add))

            return [u_cast] + [(lambda k=k: u_k(k)) for k in range(16)]

        p1q = contextlib.ExitStack()
        wst = p1q.enter_context(nc.sbuf_tensor("wst", [128, 2, NDC, 128], F32))
        ktmp = p1q.enter_context(nc.sbuf_tensor("ktmp", [128, 2, T], BF16))
        wbt = p1q.enter_context(nc.sbuf_tensor("wbt", [128, 2, NDC, 128], BF16))
        p1p = contextlib.ExitStack()
        qk_ps = p1p.enter_context(nc.psum_tensor("qk_ps", [128, 2, 1024], F32))
        st_ps = p1p.enter_context(nc.psum_tensor("st_ps", [128, 1024], F32))
        sA_ps = p1p.enter_context(nc.psum_tensor("sA_ps", [128, 1], F32))
        sB_ps = p1p.enter_context(nc.psum_tensor("sB_ps", [128, 1], F32))

        @blk.sync
        def _(sync):
            d_lut.dinc(sync.dma_start(out=lut_s[:], in_=lutb[:]))
            for dc in range(NDC):
                if dc >= 2:
                    v_xc.wait(sync, dc - 1)
                mk_x[dc] = d_xp[dc % 2].dinc(sync.dma_start(
                    out=xf[:, dc % 2, :],
                    in_=xT[dc * 128:(dc + 1) * 128, :]))
            for ec in range(32):
                if ec >= 2:
                    v_wq.wait(sync, ec - 1)
                for sub in range(4):
                    mk_w[ec] = d_wp[ec % 2].dinc(sync.dma_start(
                        out=wst[:, ec % 2, sub * 4:(sub + 1) * 4, :],
                        in_=wqkvT[sub * 512:(sub + 1) * 512,
                                  ec * 128:(ec + 1) * 128]
                        .rearrange("(c p) e -> p c e", p=128)))

        @blk.scalar
        def _(scalar):
            scalar = Drained(scalar)
            for dc in range(NDC):
                v_xc.wait(scalar, dc + 1)
                if dc >= 2:
                    t_stq.wait(scalar, dc - 1)
                s_sq.inc(scalar.square(out=sqc[:, dc % 2, :], in_=x_bf[:, dc, :]))
            v_rowpre.wait(scalar, 1)
            s_sqr.inc(scalar.sqrt(out=rows[0:1, :], in_=rows2[32:33, :]))
            v_rowb.wait(scalar, 1)
            d_bnc.dinc(scalar.dma_start(
                out=st_bnc[0:1024].rearrange("(a b) -> a b", a=1),
                in_=rows[0:1, :]))
            d_bnc.dinc(scalar.dma_start(
                out=st_bnc[1024:2048].rearrange("(a b) -> a b", a=1),
                in_=rows[32:33, :]))
            d_bnc.wait(scalar, 32)
            d_bnc.dinc(scalar.dma_start(
                out=statsT[:, 0:8],
                in_=st_bnc[0:1024].rearrange("(c p) -> p c", p=128),
                allow_slow_non_contiguous=True))
            d_bnc.dinc(scalar.dma_start(
                out=statsT[:, 8:16],
                in_=st_bnc[1024:2048].rearrange("(c p) -> p c", p=128),
                allow_slow_non_contiguous=True))
            for ec in range(16, 32):
                v_epi.wait(scalar, ec + 1)
                mk_ks[ec] = d_ksp[ec % 2].dinc(scalar.dma_start(
                    out=kv_in_K[(ec - 16) * 128:(ec - 15) * 128, :],
                    in_=ktmp[:, ec % 2, :]))
                if ec == 16:
                    t_stq.wait(scalar, NDC)
                if ec >= 18:
                    t_kn.wait(scalar, ec - 17)
                s_k2.inc(scalar.square(out=sqc[:, ec % 2, :], in_=ktmp[:, ec % 2, :]))
            v_kn.wait(scalar, 16)
            d_knt.dinc(scalar.dma_start(
                out=kv_in_N.rearrange("(a b) -> a b", a=1), in_=knrow[:]))
            for h in range(NH):
                if h == 0:
                    t_kn.wait(scalar, 16)
                if h >= 2:
                    t_qn.wait(scalar, h - 1)
                s_q2.inc(scalar.square(out=sqc[:, h % 2, :], in_=qs[:, h, :]))
                t_qn.wait(scalar, h + 1)
                pr = 32 * (h % 2)
                s_qn.inc(scalar.sqrt(
                    out=qnorm_s[32 * (h // 8):32 * (h // 8) + 1,
                                (h % 8) * 1024:(h % 8) * 1024 + 1024],
                    in_=st_ps[pr:pr + 1, :]))

        @blk.vector
        def _(vector):
            vector = Drained(vector)
            v_cc.inc(vector.memset(consts[:, 0:1], EXP_BIAS))
            v_cc.inc(vector.memset(consts[:, 1:2], SCALE_ATT))
            v_cc.inc(vector.memset(ones_bf[:], 1.0))
            v_cc.inc(vector.memset(bc_lhs[:], 0.0))
            v_cc.inc(vector.memset(bc_lhs[0:1, :], 1.0))
            v_cc.inc(vector.memset(bc_lhs[32:33, :], 1.0))
            v_cc.inc(vector.memset(bc_rhs[:], 0.0))
            v_cc.inc(vector.memset(ones2_bf[:], 0.0))
            v_cc.inc(vector.memset(ones2_bf[0:1, :], 1.0))
            for dc in range(NDC):
                d_xp[dc % 2].wait(vector, mk_x[dc])
                if dc >= 2:
                    t_stq.wait(vector, dc - 1)
                v_xc.inc(vector.tensor_copy(out=x_bf[:, dc, :], in_=xf[:, dc % 2, :]))
            for ec in range(36):
                if ec == 3:
                    t_stq.wait(vector, NDC)
                    inv = 1.0 / D
                    vector.tensor_scalar(out=rows2[0:1, :], in0=st_ps[0:1, :],
                                         scalar1=inv, scalar2=None, op0=OP.mult)
                    vector.tensor_scalar(out=rows[32:33, :], in0=st_ps[32:33, :],
                                         scalar1=inv, scalar2=EPS,
                                         op0=OP.mult, op1=OP.add)
                    vector.tensor_tensor(out=rows2[32:33, :], in0=rows2[0:1, :],
                                         in1=rows2[0:1, :], op=OP.mult)
                    vector.tensor_tensor(out=rows[32:33, :], in0=rows[32:33, :],
                                         in1=rows2[32:33, :], op=OP.subtract)
                    v_rowpre.inc(vector.reciprocal(out=rows2[32:33, :],
                                                   in_=rows[32:33, :]))
                    s_sqr.wait(vector, 1)
                    vector.tensor_tensor(out=rows[32:33, :], in0=rows2[0:1, :],
                                         in1=rows[0:1, :], op=OP.mult)
                    vector.tensor_copy(out=bc_rhs[0:1, :], in_=rows[0:1, :])
                    v_rowb.inc(vector.tensor_copy(out=bc_rhs[32:33, :],
                                                  in_=rows[32:33, :]))
                    t_bc.wait(vector, 2)
                    v_bca.inc(vector.tensor_copy(out=bc_a[:], in_=st_ps[:]))
                    t_bc.wait(vector, 4)
                    v_bcb.inc(vector.tensor_copy(out=bc_b[:], in_=st_ps[:]))
                if ec >= 4:
                    e1 = ec - 4
                    t_qk.wait(vector, e1 + 1)
                    v_sx.wait(vector, e1 + 1)
                    if e1 >= 18:
                        d_ksp[e1 % 2].wait(vector, mk_ks[e1 - 2])
                    vector.tensor_tensor(out=tmp_f[:], in0=qk_ps[:, e1 % 2, :],
                                         in1=bc_a[:], op=OP.mult)
                    dst = qs[:, e1, :] if e1 < 16 else ktmp[:, e1 % 2, :]
                    v_epi.inc(vector.scalar_tensor_tensor(
                        out=dst, in0=bc_b[:], scalar=s_neg[:, e1:e1 + 1],
                        in1=tmp_f[:], op0=OP.mult, op1=OP.add))
                if ec < 32:
                    d_wp[ec % 2].wait(vector, mk_w[ec])
                    if ec >= 2:
                        t_s.wait(vector, ec - 1)
                    v_wq.inc(vector.tensor_copy(out=wbt[:, ec % 2, :, :],
                                                in_=wst[:, ec % 2, :, :]))
                if 2 <= ec < 34:
                    e0 = ec - 2
                    t_s.wait(vector, e0 + 1)
                    sp = sA_ps if e0 % 2 == 0 else sB_ps
                    v_sx.inc(vector.tensor_scalar(
                        out=s_neg[:, e0:e0 + 1], in0=sp[:], scalar1=-1.0,
                        scalar2=None, op0=OP.mult))
                if 20 <= ec < 36:
                    e2 = ec - 4
                    t_kn.wait(vector, e2 - 15)
                    pr = 32 * (e2 % 2)
                    v_kn.inc(vector.reduce_max(
                        out=knrow[:, e2 - 16:e2 - 15], in_=st_ps[pr:pr + 1, :],
                        axis=mybir.AxisListType.X))
        @blk.tensor
        def _(tensor):
            v_cc.wait(tensor, 9)
            for dc in range(NDC):
                v_xc.wait(tensor, dc + 1)
                for half in range(2):
                    tensor.matmul(st_ps[0:1, half * 512:(half + 1) * 512],
                                  ones_bf[:],
                                  x_bf[:, dc, half * 512:(half + 1) * 512],
                                  start=(dc == 0), stop=(dc == NDC - 1))
                s_sq.wait(tensor, dc + 1)
                for half in range(2):
                    i = tensor.matmul(st_ps[32:33, half * 512:(half + 1) * 512],
                                      ones_bf[:],
                                      sqc[:, dc % 2, half * 512:(half + 1) * 512],
                                      start=(dc == 0), stop=(dc == NDC - 1))
                t_stq.inc(i)
            for ec in range(32):
                v_wq.wait(tensor, ec + 1)
                if ec == 2:
                    v_rowb.wait(tensor, 1)
                    for half in range(2):
                        i = tensor.matmul(
                            st_ps[:, half * 512:(half + 1) * 512], bc_lhs[0:2, :],
                            bc_rhs[0:2, half * 512:(half + 1) * 512],
                            start=True, stop=True)
                    t_bc.inc(i, 2)
                    v_bca.wait(tensor, 1)
                    for half in range(2):
                        i = tensor.matmul(
                            st_ps[:, half * 512:(half + 1) * 512], bc_lhs[32:34, :],
                            bc_rhs[32:34, half * 512:(half + 1) * 512],
                            start=True, stop=True)
                    t_bc.inc(i, 2)
                if ec >= 2:
                    v_epi.wait(tensor, ec - 1)
                    v_sx.wait(tensor, ec - 1)
                sp = sA_ps if ec % 2 == 0 else sB_ps
                for dc in range(NDC):
                    for half in range(2):
                        i = tensor.matmul(
                            qk_ps[:, ec % 2, half * 512:(half + 1) * 512],
                            wbt[:, ec % 2, dc, :],
                            x_bf[:, dc, half * 512:(half + 1) * 512],
                            start=(dc == 0), stop=(dc == NDC - 1))
                    j = tensor.matmul(sp[:], wbt[:, ec % 2, dc, :], ones_bf[:],
                                      start=(dc == 0), stop=(dc == NDC - 1))
                t_qk.inc(i)
                t_s.inc(j)
                if ec >= 18:
                    e = ec - 2
                    if e == 16:
                        v_bcb.wait(tensor, 1)
                    s_k2.wait(tensor, e - 15)
                    v_kn.wait(tensor, max(0, e - 17))
                    pr = 32 * (e % 2)
                    for half in range(2):
                        i = tensor.matmul(
                            st_ps[pr:pr + 1, half * 512:(half + 1) * 512],
                            ones_bf[:], sqc[:, e % 2, half * 512:(half + 1) * 512],
                            start=True, stop=True)
                    t_kn.inc(i)
            for e in (30, 31):
                s_k2.wait(tensor, e - 15)
                v_kn.wait(tensor, e - 17)
                pr = 32 * (e % 2)
                for half in range(2):
                    i = tensor.matmul(
                        st_ps[pr:pr + 1, half * 512:(half + 1) * 512],
                        ones_bf[:], sqc[:, e % 2, half * 512:(half + 1) * 512],
                        start=True, stop=True)
                t_kn.inc(i)
            for h in range(NH):
                s_q2.wait(tensor, h + 1)
                if h >= 2:
                    s_qn.wait(tensor, h - 1)
                pr = 32 * (h % 2)
                for half in range(2):
                    i = tensor.matmul(
                        st_ps[pr:pr + 1, half * 512:(half + 1) * 512],
                        ones_bf[:], sqc[:, h % 2, half * 512:(half + 1) * 512],
                        start=True, stop=True)
                t_qn.inc(i)

        p1q.close()

        p1v = contextlib.ExitStack()
        wstv = p1v.enter_context(nc.sbuf_tensor("wstv", [128, 2, 2, 512], F32))
        vtmp = p1v.enter_context(nc.sbuf_tensor("vtmp", [128, 2, 512], BF16))
        wbv = p1v.enter_context(nc.sbuf_tensor("wbv", [128, NDC, 512], BF16))
        s_bc = p1v.enter_context(nc.sbuf_tensor("s_bc", [128, 2, 512], F32))

        @blk.sync
        def _(sync):
            for u in range(32):
                es, sub = divmod(u, 8)
                if u == 0:
                    v_wq.wait(sync, 32)
                if u >= 2:
                    v_wv.wait(sync, u - 1)
                mk_wv[u] = d_wvp[u % 2].dinc(sync.dma_start(
                    out=wstv[:, u % 2, :, :],
                    in_=wqkvT[sub * 256:(sub + 1) * 256,
                              4096 + es * 512:4096 + (es + 1) * 512]
                    .rearrange("(c p) e -> p c e", p=128)))
            bar_end(sync)

        @blk.scalar
        def _(scalar):
            scalar = Drained(scalar)
            for es in range(4):
                for tt in range(8):
                    un = es * 8 + tt
                    v_vepi.wait(scalar, un + 1)
                    mk_vs[un] = d_vsp[un % 2].dinc(scalar.dma_start(
                        out=kv_in_V[tt * 128:(tt + 1) * 128,
                                    es * 512:(es + 1) * 512],
                        in_=vtmp[:, un % 2, :]))
            bar_end(scalar)

        @blk.vector
        def _(vector):
            vector = Drained(vector)
            d_bnc.wait(vector, 64)
            for es in range(4):
                for sub in range(8):
                    u = es * 8 + sub
                    d_wvp[u % 2].wait(vector, mk_wv[u])
                    if u == 0:
                        t_s.wait(vector, 32)
                    if sub == 0 and es >= 1:
                        t_v.wait(vector, es * 8)
                    v_wv.inc(vector.tensor_copy(
                        out=wbv[:, sub * 2:(sub + 1) * 2, :],
                        in_=wstv[:, u % 2, :, :]))
                t_sv.wait(vector, es + 1)
                if es >= 1:
                    t_svb.wait(vector, es)
                srcp = st_ps[64:65, (es % 2) * 512:(es % 2 + 1) * 512]
                v_svst.inc(vector.tensor_copy(out=bc_rhs[0:1, 0:512], in_=srcp))
                t_svb.wait(vector, es + 1)
                dstb = st_ps[:, (es % 2) * 512:(es % 2 + 1) * 512]
                v_svbc.inc(vector.tensor_copy(
                    out=s_bc[:, es % 2, :], in_=dstb))
                for tt in range(8):
                    un = es * 8 + tt
                    t_v.wait(vector, un + 1)
                    if un >= 2:
                        d_vsp[un % 2].wait(vector, mk_vs[un - 2])
                    vector.tensor_scalar(
                        out=tmp_f[:, 0:512], in0=s_bc[:, es % 2, :],
                        scalar1=statsT[:, 8 + tt:9 + tt], scalar2=None, op0=OP.mult)
                    v_vepi.inc(vector.scalar_tensor_tensor(
                        out=vtmp[:, un % 2, :], in0=qk_ps[:, un % 2, 0:512],
                        scalar=statsT[:, tt:tt + 1], in1=tmp_f[:, 0:512],
                        op0=OP.mult, op1=OP.subtract))
            bar_end(vector)

        @blk.tensor
        def _(tensor):
            for es in range(4):
                v_wv.wait(tensor, es * 8 + 8)
                if es == 0:
                    s_qn.wait(tensor, NH)
                if es >= 2:
                    v_svbc.wait(tensor, es - 1)
                dstp = st_ps[64:65, (es % 2) * 512:(es % 2 + 1) * 512]
                for dc in range(NDC):
                    i = tensor.matmul(dstp, ones_bf[:], wbv[:, dc, :],
                                      start=(dc == 0), stop=(dc == NDC - 1))
                t_sv.inc(i)
                v_svst.wait(tensor, es + 1)
                i = tensor.matmul(st_ps[:, (es % 2) * 512:(es % 2 + 1) * 512],
                                  bc_lhs[0:2, :], bc_rhs[0:2, 0:512],
                                  start=True, stop=True)
                t_svb.inc(i)
                for tt in range(8):
                    un = es * 8 + tt
                    if un >= 2:
                        v_vepi.wait(tensor, un - 1)
                    for dc in range(NDC):
                        i = tensor.matmul(qk_ps[:, un % 2, 0:512],
                                          x_bf[:, dc, tt * 128:(tt + 1) * 128],
                                          wbv[:, dc, :],
                                          start=(dc == 0), stop=(dc == NDC - 1))
                    t_v.inc(i)
            bar_end(tensor)

        @blk.gpsimd
        def _(gpsimd):
            d_ksp[0].wait(gpsimd)
            d_ksp[1].wait(gpsimd)
            d_knt.wait(gpsimd)
            g_cc.inc(gpsimd.collective_compute(
                "AllGather", OP.bypass, ins=[kv_in[0:KA]], outs=[kvA_out[:]],
                replica_groups=[list(range(NCORES))]))
            d_vsp[0].wait(gpsimd)
            d_vsp[1].wait(gpsimd)
            g_cc.inc(gpsimd.collective_compute(
                "AllGather", OP.bypass, ins=[kv_in[KA:KA + D * T]],
                outs=[kvB_out[:]],
                replica_groups=[list(range(NCORES))]))

        p1v.close()
        p1p.close()

        p2s = contextlib.ExitStack()
        kst = p2s.enter_context(nc.sbuf_tensor("kst", [128, 2, T], BF16))
        vst = p2s.enter_context(nc.sbuf_tensor("vst", [128, 2, 8, 128], BF16))
        pst = p2s.enter_context(nc.sbuf_tensor("pst", [128, 2, 1024], BF16))
        mneg = p2s.enter_context(nc.sbuf_tensor("mneg", [2, 2, 1024], BF16))
        zacc = p2s.enter_context(nc.sbuf_tensor("zacc", [128, 2, 1024], BF16))
        p2p = contextlib.ExitStack()
        sat_ps = p2p.enter_context(nc.psum_tensor("sat_ps", [128, 2, 1024], F32))
        pv_ps = p2p.enter_context(nc.psum_tensor("pv_ps", [128, 1024], F32))
        z_ps = p2p.enter_context(nc.psum_tensor("z_ps", [128, 1024], F32))

        pop_slots = []
        for _h in range(NH):
            for _j in range(64):
                _busy = ((_j == 2 and _h + 1 < NH) or (_j == 3 and _h >= 1)
                         or (_j == 8 and _h >= 1) or (_j == 63))
                if not _busy:
                    pop_slots.append(_h * 64 + _j)
        first_pop = [pop_slots[17 * j] for j in range(NGV)]
        done_pop = [pop_slots[17 * (j + 1) - 1] for j in range(NGV)]

        def gv_allow(h):
            n = sum(1 for f in first_pop if f <= 64 * h - 1)
            return min(NGV, n + 3)

        @blk.sync
        def _(sync):
            bar_wait(sync, 1)
            gvj = 0
            for h in range(NH):
                for r in range(NCORES):
                    u = h * NCORES + r
                    if u == 0:
                        g_cc.wait(sync, 1)
                        d_kn.dinc(sync.dma_start(
                            out=kn8[:].rearrange("a (r h) -> a r h", r=8),
                            in_=kvA_out[0:8 * KA].rearrange(
                                "(r x) -> r x", r=8)[:, D * T:D * T + 16]))
                    if u % 2 == 0 and gvj < gv_allow(h):
                        if gvj >= 3:
                            v_gc.wait(sync, gvj - 2)
                        kind, sl = GJOBS_V[gvj]
                        srcg, _, a, b = ghost_job(kind, sl)
                        mk_gi[gvj] = d_gip[gvj % 3].dinc(sync.dma_start(
                            out=gvi[:, gvj % 3, 0:a * b].rearrange(
                                "p (a b) -> p a b", a=a),
                            in_=srcg.rearrange("(c p) i -> p c i", p=128)))
                        gvj += 1
                    if u >= 2:
                        t_pv.wait(sync, (u - 1) * 8)
                    d_kvp[u % 2].dinc(sync.dma_start(
                        out=kst[:, u % 2, :],
                        in_=kv_out_K(r)[h * 128:(h + 1) * 128, :]))
                    if u == 0:
                        g_cc.wait(sync, 2)
                    mk_kv[u] = d_kvp[u % 2].dinc(sync.dma_start(
                        out=vst[:, u % 2, :, :],
                        in_=kv_out_V(r)[:, h * 128:(h + 1) * 128]
                        .rearrange("(c p) e -> p c e", p=128)))
            bar_end(sync)

        def _emit_colsums(tensor, hz):
            v_z.wait(tensor, (hz + 1) * 64)
            if hz >= 1:
                v_no.wait(tensor, hz)
            for half in range(2):
                for p in range(2):
                    i = tensor.matmul(
                        z_ps[0:1, half * 512:(half + 1) * 512], ones_bf[:],
                        zacc[:, p, half * 512:(half + 1) * 512],
                        start=(p == 0), stop=(p == 1))
            t_z.inc(i)

        def _emit_rzmm(tensor, hz):
            v_rz.wait(tensor, hz + 1)
            for half in range(2):
                i = tensor.matmul(z_ps[:, half * 512:(half + 1) * 512],
                                  bc_lhs[0:2, :],
                                  bc_rhs[0:2, half * 512:(half + 1) * 512],
                                  start=True, stop=True)
            t_rzb.inc(i)

        @blk.tensor
        def _(tensor):
            bar_wait(tensor, 1)
            for h in range(NH):
                for j in range(65):
                    gj = h * 64 + j
                    if h >= 1 and j == 1:
                        _emit_colsums(tensor, h - 1)
                    if h >= 1 and j == 6:
                        _emit_rzmm(tensor, h - 1)
                    if j < 64:
                        r, jj = divmod(j, 8)
                        u = h * NCORES + r
                        if jj == 0:
                            d_kvp[u % 2].wait(tensor, mk_kv[u])
                        if gj >= 2:
                            s_exp.wait(tensor, gj - 1)
                        for half in range(2):
                            tensor.matmul(
                                sat_ps[:, gj % 2, half * 512:(half + 1) * 512],
                                kst[:, u % 2, jj * 128:(jj + 1) * 128],
                                qs[:, h, half * 512:(half + 1) * 512],
                                start=True, stop=False)
                        if j == 0:
                            v_mn.wait(tensor, h + 1)
                        for half in range(2):
                            i = tensor.matmul(
                                sat_ps[:, gj % 2, half * 512:(half + 1) * 512],
                                ones2_bf[:], mneg[0:2, h % 2, half * 512:(half + 1) * 512],
                                start=False, stop=True)
                        t_smm.inc(i)
                    if j >= 1:
                        jp = j - 1
                        gp = h * 64 + jp
                        rp, jjp = divmod(jp, 8)
                        up = h * NCORES + rp
                        if jp == 0:
                            v_o.wait(tensor, h)
                        s_exp.wait(tensor, gp + 1)
                        for half in range(2):
                            i = tensor.matmul(
                                pv_ps[:, half * 512:(half + 1) * 512],
                                vst[:, up % 2, jjp, :],
                                pst[:, gp % 2, half * 512:(half + 1) * 512],
                                start=(jp == 0), stop=(jp == 63))
                        t_pv.inc(i)
            _emit_colsums(tensor, NH - 1)
            _emit_rzmm(tensor, NH - 1)
            bar_end(tensor)

        @blk.scalar
        def _(scalar):
            scalar = Drained(scalar)
            bar_wait(scalar, 1)
            v_kn.wait(scalar, 17)
            scalar.sqrt(out=kmax_s[0:1, :], in_=kmax_s[0:1, :])
            s_qn.inc(scalar.mul(out=kmax_s[0:1, :], in_=kmax_s[0:1, :], mul=-1.0))
            spilled = 0
            for h in range(NH):
                for j in range(64):
                    gj = h * 64 + j
                    t_smm.wait(scalar, gj + 1)
                    if gj >= 2:
                        t_pv.wait(scalar, gj - 1)
                        v_z.wait(scalar, gj - 1)
                    s_exp.inc(scalar.activation(
                        out=pst[:, gj % 2, :], in_=sat_ps[:, gj % 2, :],
                        func=AF.Exp, bias=consts[:, 0:1], scale=consts[:, 1:2]))
                    while spilled < NGV and done_pop[spilled] + 6 <= gj:
                        v_ghd.wait(scalar, spilled + 1)
                        dsta, oap = ghost_spill_ap(spilled)
                        mk_gh[spilled] = d_ghp[spilled % 3].dinc(
                            scalar.dma_start(out=dsta, in_=oap))
                        spilled += 1
            while spilled < NGV:
                v_ghd.wait(scalar, spilled + 1)
                dsta, oap = ghost_spill_ap(spilled)
                mk_gh[spilled] = d_ghp[spilled % 3].dinc(
                    scalar.dma_start(out=dsta, in_=oap))
                spilled += 1
            bar_end(scalar)

        @blk.vector
        def _(vector):
            vector = Drained(vector)
            bar_wait(vector, 1)
            d_kn.wait(vector, 16)
            for h in range(NH):
                vector.reduce_max(
                    out=kmax_s[0:1, h:h + 1],
                    in_=kn8[:].rearrange("a (r h) -> a r h", r=8)[:, :, h],
                    axis=mybir.AxisListType.X)
            v_kn.inc(vector.tensor_copy(out=kmax_s[0:1, :], in_=kmax_s[0:1, :]))
            s_qn.wait(vector, NH + 1)
            vector.tensor_copy(out=kmax_s[32:33, :], in_=kmax_s[0:1, :])
            vector.memset(mneg[0:2, :, :], 0.0)

            def _emit_mneg(hh):
                v_mn.inc(vector.tensor_scalar(
                    out=mneg[0:1, hh % 2, :],
                    in0=qnorm_s[32 * (hh // 8):32 * (hh // 8) + 1,
                                (hh % 8) * 1024:(hh % 8) * 1024 + 1024],
                    scalar1=kmax_s[32 * (hh // 8):32 * (hh // 8) + 1, hh:hh + 1],
                    scalar2=None, op0=OP.mult))

            _emit_mneg(0)
            gq = []
            gnext = 0

            def gpump():
                nonlocal gnext
                if not gq and gnext < NGV:
                    gq.extend(ghost_units(vector, gnext))
                    gnext += 1
                if gq:
                    gq.pop(0)()

            for h in range(NH):
                for j in range(64):
                    gj = h * 64 + j
                    s_exp.wait(vector, gj + 1)
                    if j < 2:
                        if h >= 1 and j == 0:
                            t_z.wait(vector, h)
                        v_z.inc(vector.tensor_copy(out=zacc[:, j, :],
                                                   in_=pst[:, gj % 2, :]))
                    else:
                        v_z.inc(vector.tensor_tensor(
                            out=zacc[:, j % 2, :], in0=pst[:, gj % 2, :],
                            in1=zacc[:, j % 2, :], op=OP.add))
                    busy = True
                    if j == 2 and h + 1 < NH:
                        t_smm.wait(vector, h * 64)
                        _emit_mneg(h + 1)
                    elif j == 3 and h >= 1:
                        t_z.wait(vector, h)
                        v_rz.inc(vector.reciprocal(out=bc_rhs[0:1, :],
                                                   in_=z_ps[0:1, :]))
                    elif j == 8 and h >= 1:
                        t_rzb.wait(vector, h)
                        v_no.inc(vector.tensor_tensor(
                            out=o_bf[:, h - 1, :], in0=o_bf[:, h - 1, :],
                            in1=z_ps[:], op=OP.mult))
                    elif j == 63:
                        t_pv.wait(vector, (h + 1) * 64)
                        v_o.inc(vector.tensor_copy(out=o_bf[:, h, :],
                                                   in_=pv_ps[:]))
                    else:
                        busy = False
                    if not busy:
                        gpump()
            t_z.wait(vector, NH)
            v_rz.inc(vector.reciprocal(out=bc_rhs[0:1, :], in_=z_ps[0:1, :]))
            t_rzb.wait(vector, NH)
            v_no.inc(vector.tensor_tensor(
                out=o_bf[:, NH - 1, :], in0=o_bf[:, NH - 1, :],
                in1=z_ps[:], op=OP.mult))
            while gq or gnext < NGV:
                gpump()
            bar_end(vector)

        @blk.gpsimd
        def _(gpsimd):
            for c in range(3):
                d_ghp[c].wait(gpsimd, mk_gh[max(j for j in range(32)
                                               if j % 3 == c)])
            g_cc.inc(gpsimd.collective_compute(
                "AllGather", OP.bypass, ins=[gh_in[0:GA]], outs=[ghA_out[:]],
                replica_groups=[list(range(NCORES))]))
            d_ghp[0].wait(gpsimd)
            d_ghp[1].wait(gpsimd)
            d_ghp[2].wait(gpsimd)
            g_cc.inc(gpsimd.collective_compute(
                "AllGather", OP.bypass, ins=[gh_in[GA:GH_BLK]],
                outs=[ghB_out[:]],
                replica_groups=[list(range(NCORES))]))

        p2s.close()
        p2p.close()
        scQ.close()

        if stage <= 2:
            @blk.sync
            def _(sync):
                bar_wait(sync, 2)
                g_cc.wait(sync, 4)
                d_dbg.dinc(sync.dma_start(out=dbg["d_q"].rearrange(
                    "(c p) t -> p c t", p=128), in_=qs[:]))
                d_dbg.dinc(sync.dma_start(out=dbg["d_k"][:], in_=kv_in_K[:]))
                d_dbg.dinc(sync.dma_start(out=dbg["d_v"][:], in_=kv_in_V[:]))
                d_dbg.dinc(sync.dma_start(out=dbg["d_stats"][0:1, :],
                                          in_=rows[0:1, :]))
                d_dbg.dinc(sync.dma_start(out=dbg["d_stats"][1:2, :],
                                          in_=rows[32:33, :]))
                d_dbg.dinc(sync.dma_start(out=dbg["d_o"].rearrange(
                    "(c p) t -> p c t", p=128), in_=o_bf[:]))
                d_outp[0].dinc(sync.dma_start(out=outT[0:128, :], in_=xf[:, 0, :]))
                d_dbg.wait(sync)
                d_outp[0].wait(sync)
            scA.close()
            return nc, dbg, ctx

        p3s = contextlib.ExitStack()
        wst3 = p3s.enter_context(nc.sbuf_tensor("wst3", [128, 2, NDC, 128], F32))
        wbt3 = p3s.enter_context(nc.sbuf_tensor("wbt3", [128, 2, NDC, 128], BF16))
        pjt = p3s.enter_context(nc.sbuf_tensor("pjt", [128, 2, T], BF16))
        x1t = p3s.enter_context(nc.sbuf_tensor("x1t", [128, 2, T], F32))
        x1b = p3s.enter_context(nc.sbuf_tensor("x1b", [128, 2, T], BF16))
        ldt = p3s.enter_context(nc.sbuf_tensor("ldt", [128, 2, T], BF16))
        p3p = contextlib.ExitStack()
        st2_ps = p3p.enter_context(nc.psum_tensor("st2_ps", [128, 1024], F32))
        pj_ps = p3p.enter_context(nc.psum_tensor("pj_ps", [128, 2, 1024], F32))

        @blk.scalar
        def _(scalar):
            scalar = Drained(scalar)
            bar_wait(scalar, 2)
            for ec in range(NDC):
                v_pe.wait(scalar, ec + 1)
                if ec >= 2:
                    t_stq2.wait(scalar, ec - 1)
                s_sq2.inc(scalar.square(out=sqc[:, ec % 2, :], in_=pjt[:, ec % 2, :]))
                mk_pj[ec] = d_pjp[ec % 2].dinc(scalar.dma_start(
                    out=proj_sc[ec * 128:(ec + 1) * 128, :], in_=pjt[:, ec % 2, :]))
            v_rowpre.wait(scalar, 2)
            s_sqr.inc(scalar.sqrt(out=rows[0:1, :], in_=rows2[32:33, :]))
            for ec in range(NDC):
                v_x1.wait(scalar, ec + 1)
                mk_x1[ec] = d_x1p[ec % 2].dinc(scalar.dma_start(
                    out=x1_sc[ec * 128:(ec + 1) * 128, :], in_=x1t[:, ec % 2, :]))
                v_x1b.wait(scalar, ec + 1)
                if ec >= 2:
                    t_stq3.wait(scalar, ec - 1)
                s_sq3.inc(scalar.square(out=sqc[:, ec % 2, :], in_=x1b[:, ec % 2, :]))
            v_rowpre.wait(scalar, 3)
            s_sqr.inc(scalar.sqrt(out=rows[0:1, :], in_=rows2[32:33, :]))
            bar_end(scalar)

        @blk.sync
        def _(sync):
            bar_wait(sync, 2)
            for ec in range(NDC):
                if ec >= 2:
                    v_w3.wait(sync, ec - 1)
                for sub in range(4):
                    mk_w[("p3", ec)] = d_wp[ec % 2].dinc(sync.dma_start(
                        out=wst3[:, ec % 2, sub * 4:(sub + 1) * 4, :],
                        in_=wprojT[sub * 512:(sub + 1) * 512,
                                   ec * 128:(ec + 1) * 128]
                        .rearrange("(c p) e -> p c e", p=128)))
            for ec in range(NDC):
                if ec >= 2:
                    v_x1.wait(sync, ec - 1)
                mk_x[("p3", ec)] = d_xp[ec % 2].dinc(sync.dma_start(
                    out=xf[:, ec % 2, :],
                    in_=xT[ec * 128:(ec + 1) * 128, :]))
                d_pjp[ec % 2].wait(sync, mk_pj[ec])
                mk_ldt[ec] = d_ldtp[ec % 2].dinc(sync.dma_start(
                    out=ldt[:, ec % 2, :],
                    in_=proj_sc[ec * 128:(ec + 1) * 128, :]))
            d_x1p[0].wait(sync)
            d_x1p[1].wait(sync)
            for ec in range(NDC):
                if ec >= 2:
                    v_l2.wait(sync, ec - 1)
                mk_lx[ec] = d_lxp[ec % 2].dinc(sync.dma_start(
                    out=xf[:, ec % 2, :],
                    in_=x1_sc[ec * 128:(ec + 1) * 128, :]))
            bar_end(sync)

        @blk.tensor
        def _(tensor):
            bar_wait(tensor, 2)
            for ec in range(NDC):
                v_w3.wait(tensor, ec + 1)
                if ec >= 2:
                    v_pe.wait(tensor, ec - 1)
                for dc in range(NDC):
                    for half in range(2):
                        i = tensor.matmul(
                            pj_ps[:, ec % 2, half * 512:(half + 1) * 512],
                            wbt3[:, ec % 2, dc, :],
                            o_bf[:, dc, half * 512:(half + 1) * 512],
                            start=(dc == 0), stop=(dc == NDC - 1))
                t_pj.inc(i)
                v_pe.wait(tensor, ec + 1)
                for half in range(2):
                    tensor.matmul(st2_ps[0:1, half * 512:(half + 1) * 512],
                                  ones_bf[:],
                                  pjt[:, ec % 2, half * 512:(half + 1) * 512],
                                  start=(ec == 0), stop=(ec == NDC - 1))
                s_sq2.wait(tensor, ec + 1)
                for half in range(2):
                    i = tensor.matmul(st2_ps[32:33, half * 512:(half + 1) * 512],
                                      ones_bf[:],
                                      sqc[:, ec % 2, half * 512:(half + 1) * 512],
                                      start=(ec == 0), stop=(ec == NDC - 1))
                t_stq2.inc(i)
            v_rowb.wait(tensor, 2)
            for half in range(2):
                i = tensor.matmul(st2_ps[:, half * 512:(half + 1) * 512],
                                  bc_lhs[0:2, :], bc_rhs[0:2, half * 512:(half + 1) * 512],
                                  start=True, stop=True)
            t_bc.inc(i, 2)
            mk_bc[("pa", 0)] = t_bc.n
            v_bca.wait(tensor, 2)
            for half in range(2):
                i = tensor.matmul(st2_ps[:, half * 512:(half + 1) * 512],
                                  bc_lhs[32:34, :],
                                  bc_rhs[32:34, half * 512:(half + 1) * 512],
                                  start=True, stop=True)
            t_bc.inc(i, 2)
            mk_bc[("pa", 1)] = t_bc.n
            for ec in range(NDC):
                v_x1b.wait(tensor, ec + 1)
                for half in range(2):
                    tensor.matmul(st2_ps[0:1, half * 512:(half + 1) * 512],
                                  ones_bf[:],
                                  x1b[:, ec % 2, half * 512:(half + 1) * 512],
                                  start=(ec == 0), stop=(ec == NDC - 1))
                s_sq3.wait(tensor, ec + 1)
                for half in range(2):
                    i = tensor.matmul(st2_ps[32:33, half * 512:(half + 1) * 512],
                                      ones_bf[:],
                                      sqc[:, ec % 2, half * 512:(half + 1) * 512],
                                      start=(ec == 0), stop=(ec == NDC - 1))
                t_stq3.inc(i)
            v_rowb.wait(tensor, 3)
            for half in range(2):
                i = tensor.matmul(st2_ps[:, half * 512:(half + 1) * 512],
                                  bc_lhs[0:2, :], bc_rhs[0:2, half * 512:(half + 1) * 512],
                                  start=True, stop=True)
            t_bc.inc(i, 2)
            mk_bc[("l2", 0)] = t_bc.n
            v_bca.wait(tensor, 3)
            for half in range(2):
                i = tensor.matmul(st2_ps[:, half * 512:(half + 1) * 512],
                                  bc_lhs[32:34, :],
                                  bc_rhs[32:34, half * 512:(half + 1) * 512],
                                  start=True, stop=True)
            t_bc.inc(i, 2)
            mk_bc[("l2", 1)] = t_bc.n
            bar_end(tensor)

        def rows_finalize(vector, st_psrc, pre_ch, sqr_ch, sqr_target):
            inv = 1.0 / D
            vector.tensor_scalar(out=rows2[0:1, :], in0=st_psrc[0:1, :],
                                 scalar1=inv, scalar2=None, op0=OP.mult)
            vector.tensor_scalar(out=rows[32:33, :], in0=st_psrc[32:33, :],
                                 scalar1=inv, scalar2=EPS, op0=OP.mult, op1=OP.add)
            vector.tensor_tensor(out=rows2[32:33, :], in0=rows2[0:1, :],
                                 in1=rows2[0:1, :], op=OP.mult)
            vector.tensor_tensor(out=rows[32:33, :], in0=rows[32:33, :],
                                 in1=rows2[32:33, :], op=OP.subtract)
            pre_ch.inc(vector.reciprocal(out=rows2[32:33, :],
                                         in_=rows[32:33, :]))
            sqr_ch.wait(vector, sqr_target)
            vector.tensor_tensor(out=rows[32:33, :], in0=rows2[0:1, :],
                                 in1=rows[0:1, :], op=OP.mult)
            vector.tensor_copy(out=bc_rhs[0:1, :], in_=rows[0:1, :])
            return vector.tensor_copy(out=bc_rhs[32:33, :],
                                      in_=rows[32:33, :])

        @blk.vector
        def _(vector):
            vector = Drained(vector)
            bar_wait(vector, 2)
            for ec in range(NDC + 2):
                if ec < NDC:
                    d_wp[ec % 2].wait(vector, mk_w[("p3", ec)])
                    if ec >= 2:
                        t_pj.wait(vector, ec - 1)
                    v_w3.inc(vector.tensor_copy(out=wbt3[:, ec % 2, :, :],
                                                in_=wst3[:, ec % 2, :, :]))
                if ec >= 2:
                    e0 = ec - 2
                    t_pj.wait(vector, e0 + 1)
                    if e0 >= 2:
                        d_pjp[e0 % 2].wait(vector, mk_pj[e0 - 2])
                        t_stq2.wait(vector, e0 - 1)
                    v_pe.inc(vector.tensor_copy(out=pjt[:, e0 % 2, :],
                                                in_=pj_ps[:, e0 % 2, :]))
            t_stq2.wait(vector, NDC)
            v_rowb.inc(rows_finalize(vector, st2_ps, v_rowpre, s_sqr, 2))
            t_bc.wait(vector, mk_bc[("pa", 0)])
            v_bca.inc(vector.tensor_copy(out=bc_a[:], in_=st2_ps[:]))
            t_bc.wait(vector, mk_bc[("pa", 1)])
            v_bcb.inc(vector.tensor_copy(out=bc_b[:], in_=st2_ps[:]))
            for ec in range(NDC):
                d_xp[ec % 2].wait(vector, mk_x[("p3", ec)])
                d_ldtp[ec % 2].wait(vector, mk_ldt[ec])
                if ec >= 2:
                    d_x1p[ec % 2].wait(vector, mk_x1[ec - 2])
                    t_stq3.wait(vector, ec - 1)
                vector.tensor_tensor(out=tmp_f[:], in0=ldt[:, ec % 2, :],
                                     in1=bc_a[:], op=OP.mult)
                vector.tensor_tensor(out=tmp_f[:], in0=tmp_f[:],
                                     in1=bc_b[:], op=OP.subtract)
                v_x1.inc(vector.tensor_tensor(out=x1t[:, ec % 2, :], in0=tmp_f[:],
                                              in1=xf[:, ec % 2, :], op=OP.add))
                v_x1b.inc(vector.tensor_copy(out=x1b[:, ec % 2, :],
                                             in_=x1t[:, ec % 2, :]))
            t_stq3.wait(vector, NDC)
            v_rowb.inc(rows_finalize(vector, st2_ps, v_rowpre, s_sqr, 3))
            t_bc.wait(vector, mk_bc[("l2", 0)])
            v_bca.inc(vector.tensor_copy(out=bc_a[:], in_=st2_ps[:]))
            t_bc.wait(vector, mk_bc[("l2", 1)])
            v_bcb.inc(vector.tensor_copy(out=bc_b[:], in_=st2_ps[:]))
            for ec in range(NDC):
                d_lxp[ec % 2].wait(vector, mk_lx[ec])
                vector.tensor_tensor(out=tmp_f[:], in0=xf[:, ec % 2, :],
                                     in1=bc_a[:], op=OP.mult)
                v_l2.inc(vector.tensor_tensor(out=x_bf[:, ec, :], in0=tmp_f[:],
                                              in1=bc_b[:], op=OP.subtract))
            bar_end(vector)

        p3s.close()
        p3p.close()
        scA.close()

        if stage == 3:
            @blk.sync
            def _(sync):
                bar_wait(sync, 3)
                d_dbg.dinc(sync.dma_start(out=dbg["d_x1"][:], in_=x1_sc[:]))
                d_dbg.dinc(sync.dma_start(out=dbg["d_ln2"].rearrange(
                    "(c p) t -> p c t", p=128), in_=x_bf[:]))
                d_dbg.dinc(sync.dma_start(out=dbg["d_proj"][:], in_=proj_sc[:]))
                d_dbg.dinc(sync.dma_start(out=dbg["d_rows"][0:1, :],
                                          in_=bc_a[0:1, :]))
                d_dbg.dinc(sync.dma_start(out=dbg["d_rows"][1:2, :],
                                          in_=bc_b[0:1, :]))
                d_outp[0].dinc(sync.dma_start(out=outT[0:128, :], in_=xf[:, 0, :]))
                d_dbg.wait(sync)
                d_outp[0].wait(sync)
            return nc, dbg, ctx

        p4s = contextlib.ExitStack()
        h_bf = p4s.enter_context(nc.sbuf_tensor("h_bf", [128, NIC, T], BF16))
        gt_st = p4s.enter_context(nc.sbuf_tensor("gt_st", [128, 4, 128], BF16))
        up_st = p4s.enter_context(nc.sbuf_tensor("up_st", [128, 4, 128], BF16))
        sil_t = p4s.enter_context(nc.sbuf_tensor("sil_t", [128, T], BF16))
        p4p = contextlib.ExitStack()
        g_ps = p4p.enter_context(nc.psum_tensor("g_ps", [128, 2, 1024], F32))
        u_ps = p4p.enter_context(nc.psum_tensor("u_ps", [128, 2, 1024], F32))

        @blk.sync
        def _(sync):
            bar_wait(sync, 3)
            g_cc.wait(sync, 3)
            for ic in range(NIC):
                for dc in range(NDC):
                    un = ic * NDC + dc
                    if un >= 4:
                        t_guu.wait(sync, un - 3)
                    r, half = divmod(dc, 2)
                    d_gwp[un % 4].dinc(sync.dma_start(
                        out=gt_st[:, un % 4, :],
                        in_=gh_out_g(r)[half * 128:(half + 1) * 128,
                                        ic * 128:(ic + 1) * 128]))
                    mk_gw[un] = d_gwp[un % 4].dinc(sync.dma_start(
                        out=up_st[:, un % 4, :],
                        in_=gh_out_u(r)[half * 128:(half + 1) * 128,
                                        ic * 128:(ic + 1) * 128]))
            bar_end(sync)

        @blk.tensor
        def _(tensor):
            bar_wait(tensor, 3)
            for ic in range(NIC):
                if ic >= 2:
                    v_h.wait(tensor, ic - 1)
                for dc in range(NDC):
                    un = ic * NDC + dc
                    d_gwp[un % 4].wait(tensor, mk_gw[un])
                    for half in range(2):
                        tensor.matmul(
                            g_ps[:, ic % 2, half * 512:(half + 1) * 512],
                            gt_st[:, un % 4, :],
                            x_bf[:, dc, half * 512:(half + 1) * 512],
                            start=(dc == 0), stop=(dc == NDC - 1))
                    for half in range(2):
                        i = tensor.matmul(
                            u_ps[:, ic % 2, half * 512:(half + 1) * 512],
                            up_st[:, un % 4, :],
                            x_bf[:, dc, half * 512:(half + 1) * 512],
                            start=(dc == 0), stop=(dc == NDC - 1))
                    t_guu.inc(i)
            bar_end(tensor)

        @blk.scalar
        def _(scalar):
            scalar = Drained(scalar)
            bar_wait(scalar, 3)
            for ic in range(NIC):
                t_guu.wait(scalar, (ic + 1) * NDC)
                s_si.inc(scalar.activation(out=sil_t[:], in_=g_ps[:, ic % 2, :],
                                           func=AF.Silu))
            bar_end(scalar)

        @blk.vector
        def _(vector):
            vector = Drained(vector)
            bar_wait(vector, 3)
            for ic in range(NIC):
                s_si.wait(vector, ic + 1)
                v_h.inc(vector.tensor_tensor(out=h_bf[:, ic, :], in0=sil_t[:],
                                             in1=u_ps[:, ic % 2, :], op=OP.mult))
            bar_end(vector)

        p4p.close()

        if stage == 4:
            @blk.sync
            def _(sync):
                bar_wait(sync, 4)
                d_dbg.dinc(sync.dma_start(out=dbg["d_gw"][:], in_=gh_in[:]))
                d_dbg.dinc(sync.dma_start(out=dbg["d_h"].rearrange(
                    "(c p) t -> p c t", p=128), in_=h_bf[:]))
                d_dbg.dinc(sync.dma_start(out=dbg["d_x1"][:], in_=x1_sc[:]))
                d_dbg.dinc(sync.dma_start(out=dbg["d_ln2"].rearrange(
                    "(c p) t -> p c t", p=128), in_=x_bf[:]))
                d_outp[0].dinc(sync.dma_start(out=outT[0:128, :], in_=xf[:, 0, :]))
                d_dbg.wait(sync)
                d_outp[0].wait(sync)
            p4s.close()
            return nc, dbg, ctx

        p5s = contextlib.ExitStack()
        dn_st = p5s.enter_context(nc.sbuf_tensor("dn_st", [128, 4, 256], BF16))
        dnt = p5s.enter_context(nc.sbuf_tensor("dnt", [128, 2, T], BF16))
        outst = p5s.enter_context(nc.sbuf_tensor("outst", [128, 2, T], F32))
        x1t5 = p5s.enter_context(nc.sbuf_tensor("x1t5", [128, 2, T], F32))
        p5p = contextlib.ExitStack()
        dn_ps = p5p.enter_context(nc.psum_tensor("dn_ps", [128, 2, 1024], F32))
        st3_ps = p5p.enter_context(nc.psum_tensor("st3_ps", [128, 1024], F32))

        @blk.scalar
        def _(scalar):
            scalar = Drained(scalar)
            bar_wait(scalar, 4)
            for ec in range(NDC):
                v_de.wait(scalar, ec + 1)
                if ec >= 2:
                    t_stq4.wait(scalar, ec - 1)
                s_sq4.inc(scalar.square(out=sqc[:, ec % 2, :], in_=dnt[:, ec % 2, :]))
                mk_dn[ec] = d_dnp[ec % 2].dinc(scalar.dma_start(
                    out=down_sc[ec * 128:(ec + 1) * 128, :], in_=dnt[:, ec % 2, :]))
            v_rowpre.wait(scalar, 4)
            s_sqr.inc(scalar.sqrt(out=rows[0:1, :], in_=rows2[32:33, :]))

        @blk.sync
        def _(sync):
            bar_wait(sync, 4)
            g_cc.wait(sync, 4)
            for ep in range(8):
                for ic in range(NIC):
                    un = ep * NIC + ic
                    if un >= 4:
                        t_dnu.wait(sync, un - 3)
                    mk_gw[("dn", un)] = d_gwp[un % 4].dinc(sync.dma_start(
                        out=dn_st[:, un % 4, :],
                        in_=gh_out_d(ep)[ic * 128:(ic + 1) * 128, :]))
            d_dnp[0].wait(sync)
            d_dnp[1].wait(sync)
            d_x1p[0].wait(sync)
            d_x1p[1].wait(sync)
            t_stq4.wait(sync, 16)
            for ec in range(NDC + 1):
                if ec < NDC:
                    if ec >= 2:
                        v_out.wait(sync, ec - 1)
                    d_l5p[ec % 2].dinc(sync.dma_start(
                        out=x1t5[:, ec % 2, :],
                        in_=x1_sc[ec * 128:(ec + 1) * 128, :]))
                    mk_l5[ec] = d_l5p[ec % 2].dinc(sync.dma_start(
                        out=dnt[:, ec % 2, :],
                        in_=down_sc[ec * 128:(ec + 1) * 128, :]))
                if ec >= 1:
                    e1 = ec - 1
                    v_out.wait(sync, e1 + 1)
                    mk_out[e1] = d_outp[e1 % 2].dinc(sync.dma_start(
                        out=outT[e1 * 128:(e1 + 1) * 128, :],
                        in_=outst[:, e1 % 2, :]))
            d_outp[0].wait(sync)
            d_outp[1].wait(sync)

        @blk.tensor
        def _(tensor):
            bar_wait(tensor, 4)
            for ep in range(8):
                if ep >= 1:
                    v_de.wait(tensor, 2 * ep)
                for ic in range(NIC):
                    un = ep * NIC + ic
                    d_gwp[un % 4].wait(tensor, mk_gw[("dn", un)])
                    for ecl in range(2):
                        for half in range(2):
                            i = tensor.matmul(
                                dn_ps[:, ecl, half * 512:(half + 1) * 512],
                                dn_st[:, un % 4, ecl * 128:(ecl + 1) * 128],
                                h_bf[:, ic, half * 512:(half + 1) * 512],
                                start=(ic == 0), stop=(ic == NIC - 1))
                    t_dnu.inc(i)
                for ecl in range(2):
                    ec = 2 * ep + ecl
                    v_de.wait(tensor, ec + 1)
                    for half in range(2):
                        tensor.matmul(st3_ps[0:1, half * 512:(half + 1) * 512],
                                      ones_bf[:],
                                      dnt[:, ec % 2, half * 512:(half + 1) * 512],
                                      start=(ec == 0), stop=(ec == NDC - 1))
                    s_sq4.wait(tensor, ec + 1)
                    for half in range(2):
                        i = tensor.matmul(st3_ps[32:33, half * 512:(half + 1) * 512],
                                          ones_bf[:],
                                          sqc[:, ec % 2, half * 512:(half + 1) * 512],
                                          start=(ec == 0), stop=(ec == NDC - 1))
                    t_stq4.inc(i)
            v_rowb.wait(tensor, 4)
            for half in range(2):
                i = tensor.matmul(st3_ps[:, half * 512:(half + 1) * 512],
                                  bc_lhs[0:2, :], bc_rhs[0:2, half * 512:(half + 1) * 512],
                                  start=True, stop=True)
            t_bc.inc(i, 2)
            mk_bc[("l3", 0)] = t_bc.n
            v_bca.wait(tensor, 4)
            for half in range(2):
                i = tensor.matmul(st3_ps[:, half * 512:(half + 1) * 512],
                                  bc_lhs[32:34, :],
                                  bc_rhs[32:34, half * 512:(half + 1) * 512],
                                  start=True, stop=True)
            t_bc.inc(i, 2)
            mk_bc[("l3", 1)] = t_bc.n

        @blk.vector
        def _(vector):
            vector = Drained(vector)
            bar_wait(vector, 4)
            for ep in range(8):
                t_dnu.wait(vector, (ep + 1) * NIC)
                for ecl in range(2):
                    ec = 2 * ep + ecl
                    if ec >= 2:
                        t_stq4.wait(vector, ec - 1)
                        d_dnp[ec % 2].wait(vector, mk_dn[ec - 2])
                    v_de.inc(vector.tensor_copy(out=dnt[:, ec % 2, :],
                                                in_=dn_ps[:, ecl, :]))
            t_stq4.wait(vector, NDC)
            v_rowb.inc(rows_finalize(vector, st3_ps, v_rowpre, s_sqr, 4))
            t_bc.wait(vector, mk_bc[("l3", 0)])
            v_bca.inc(vector.tensor_copy(out=bc_a[:], in_=st3_ps[:]))
            t_bc.wait(vector, mk_bc[("l3", 1)])
            v_bcb.inc(vector.tensor_copy(out=bc_b[:], in_=st3_ps[:]))
            for ec in range(NDC):
                d_l5p[ec % 2].wait(vector, mk_l5[ec])
                if ec >= 2:
                    d_outp[ec % 2].wait(vector, mk_out[ec - 2])
                vector.tensor_tensor(out=tmp_f[:], in0=dnt[:, ec % 2, :],
                                     in1=bc_a[:], op=OP.mult)
                vector.tensor_tensor(out=tmp_f[:], in0=tmp_f[:],
                                     in1=bc_b[:], op=OP.subtract)
                v_out.inc(vector.tensor_tensor(out=outst[:, ec % 2, :],
                                               in0=tmp_f[:], in1=x1t5[:, ec % 2, :],
                                               op=OP.add))

        p5p.close()
        p5s.close()
        p4s.close()
        return nc, dbg, ctx

    raise AssertionError("unreachable")


_STAGE = 5


def kernel(L0_attn_x, L0_attn_w_qkv, L0_attn_w_proj, L0_ffn_lut,
           L0_ffn_gate_idx, L0_ffn_up_idx, L0_ffn_down_idx):
    x = np.asarray(L0_attn_x)[0]
    xTf = np.ascontiguousarray(x.T.astype(np.float32, copy=False))
    wqkvT = np.ascontiguousarray(np.asarray(L0_attn_w_qkv).T.astype(np.float32, copy=False))
    wprojT = np.ascontiguousarray(np.asarray(L0_attn_w_proj).T.astype(np.float32, copy=False))
    lutb = np.ascontiguousarray(np.broadcast_to(
        np.asarray(L0_ffn_lut).astype(np.float32, copy=False)[None, :], (128, 16)))
    gidxT = np.ascontiguousarray(np.asarray(L0_ffn_gate_idx).T)
    uidxT = np.ascontiguousarray(np.asarray(L0_ffn_up_idx).T)
    didxT = np.ascontiguousarray(np.asarray(L0_ffn_down_idx).T)

    nc, dbg, ctx = build_nc(stage=_STAGE)
    in_maps = []
    for c in range(NCORES):
        in_maps.append({
            "xT": np.ascontiguousarray(xTf[:, c * T:(c + 1) * T]),
            "wqkvT": wqkvT,
            "wprojT": wprojT,
            "lutb": lutb,
            "gidxT": np.ascontiguousarray(gidxT[c * 256:(c + 1) * 256, :]),
            "uidxT": np.ascontiguousarray(uidxT[c * 256:(c + 1) * 256, :]),
            "didxT": np.ascontiguousarray(didxT[:, c * 256:(c + 1) * 256]),
        })
    res = run_bass_kernel_spmd(nc, in_maps, core_ids=list(range(NCORES)))
    kernel.last_results = res
    outTs = [res.results[c]["outT"] for c in range(NCORES)]
    out = np.concatenate(outTs, axis=1).T[None, :, :]
    return np.ascontiguousarray(out.astype(np.float32, copy=False))


# version 05
# speedup vs baseline: 1.4035x; 1.0189x over previous
import contextlib
import math

import numpy as np
import ml_dtypes

import concourse.bass as bass
import concourse.mybir as mybir
from concourse.bass_utils import run_bass_kernel_spmd

F32 = mybir.dt.float32
BF16 = mybir.dt.bfloat16
I32 = mybir.dt.int32
AF = mybir.ActivationFunctionType
OP = mybir.AluOpType

NCORES = 8
T = 1024
D = 2048
NDC = 16
NH = 16
EQKV = 6144
INTER = 5504
NIC = 43
SCALE_ATT = 1.0 / math.sqrt(128.0)
EXP_BIAS = 40.0
EPS = 1e-5

GSH_GATE = 256 * INTER
GSH_DOWN = INTER * 256
GH_BLK = 2 * GSH_GATE + GSH_DOWN
KV_BLK = 2 * D * T + 16

GJOBS_V = ([("gate", s) for s in range(16)] + [("up", s) for s in range(16)]
           + [("down", s) for s in range(11)])
NGV = len(GJOBS_V)


class Ch:
    def __init__(self, ctx, nc, name):
        self.sem = ctx.enter_context(nc.semaphore(name))
        self.n = 0

    def inc(self, instr, amt=1):
        instr.then_inc(self.sem, amt)
        self.n += amt
        return self.n

    def dinc(self, instr):
        return self.inc(instr, 16)

    def wait(self, eng, n=None):
        eng.wait_ge(self.sem, self.n if n is None else n)


class Drained:
    _COMPUTE = {"tensor_copy", "tensor_tensor", "tensor_scalar",
                "scalar_tensor_tensor", "memset", "reciprocal", "activation",
                "square", "sqrt", "copy", "mul", "add", "select", "iota",
                "tensor_reduce", "reduce_max", "reduce_sum", "dma_start"}

    def __init__(self, eng):
        self._e = eng

    def __getattr__(self, name):
        f = getattr(self._e, name)
        if name in Drained._COMPUTE:
            def wrap(*a, **k):
                self._e.drain()
                return f(*a, **k)
            return wrap
        return f


def build_nc(stage=5):
    nc = bass.Bass("TRN2", target_bir_lowering=False,
                   detect_race_conditions=False)

    xT = nc.dram_tensor("xT", [D, T], F32, kind="ExternalInput").ap()
    wqkvT = nc.dram_tensor("wqkvT", [D, EQKV], F32, kind="ExternalInput").ap()
    wprojT = nc.dram_tensor("wprojT", [D, D], F32, kind="ExternalInput").ap()
    lutb = nc.dram_tensor("lutb", [128, 16], F32, kind="ExternalInput").ap()
    gidxT = nc.dram_tensor("gidxT", [256, INTER], I32, kind="ExternalInput").ap()
    uidxT = nc.dram_tensor("uidxT", [256, INTER], I32, kind="ExternalInput").ap()
    didxT = nc.dram_tensor("didxT", [INTER, 256], I32, kind="ExternalInput").ap()
    outT = nc.dram_tensor("outT", [D, T], F32, kind="ExternalOutput").ap()

    dbg = {}
    if stage < 5:
        def _dt(name, shape, dtyp):
            dbg[name] = nc.dram_tensor(name, shape, dtyp, kind="ExternalOutput").ap()
        _dt("d_q", [D, T], BF16)
        _dt("d_k", [D, T], BF16)
        _dt("d_v", [T, D], BF16)
        _dt("d_stats", [2, T], F32)
        _dt("d_o", [D, T], BF16)
        if stage >= 3:
            _dt("d_x1", [D, T], F32)
            _dt("d_ln2", [D, T], BF16)
            _dt("d_proj", [D, T], BF16)
            _dt("d_rows", [2, T], F32)
        if stage >= 4:
            _dt("d_gw", [GH_BLK], BF16)
            _dt("d_h", [INTER, T], BF16)

    KA = D * T + 16
    kv_in = nc.dram_tensor("kv_in", [KV_BLK], BF16, kind="Internal").ap()
    kvA_out = nc.dram_tensor("kvA_out", [NCORES * KA], BF16, kind="Internal").ap()
    kvB_out = nc.dram_tensor("kvB_out", [NCORES * D * T], BF16, kind="Internal").ap()
    gh_in = nc.dram_tensor("gh_in", [GH_BLK], BF16, kind="Internal").ap()
    GA = 2 * GSH_GATE
    ghA_out = nc.dram_tensor("ghA_out", [NCORES * GA], BF16, kind="Internal").ap()
    ghB_out = nc.dram_tensor("ghB_out", [NCORES * GSH_DOWN], BF16, kind="Internal").ap()
    st_bnc = nc.dram_tensor("st_bnc", [2048], F32, kind="Internal").ap()
    x1_sc = nc.dram_tensor("x1_sc", [D, T], F32, kind="Internal").ap()
    proj_sc = nc.dram_tensor("proj_sc", [D, T], BF16, kind="Internal").ap()
    down_sc = nc.dram_tensor("down_sc", [D, T], BF16, kind="Internal").ap()

    kv_in_K = kv_in[0:D * T].rearrange("(e t) -> e t", t=T)
    kv_in_N = kv_in[D * T:D * T + 16]
    kv_in_V = kv_in[KA:KA + D * T].rearrange("(t e) -> t e", e=D)
    kv_out_K = lambda r: kvA_out[r * KA: r * KA + D * T].rearrange("(e t) -> e t", t=T)
    kv_out_V = lambda r: kvB_out[r * D * T: (r + 1) * D * T].rearrange("(t e) -> t e", e=D)
    gh_in_g = gh_in[0:GSH_GATE].rearrange("(d i) -> d i", i=INTER)
    gh_in_u = gh_in[GSH_GATE:2 * GSH_GATE].rearrange("(d i) -> d i", i=INTER)
    gh_in_d = gh_in[2 * GSH_GATE:].rearrange("(i e) -> i e", e=256)
    gh_out_g = lambda r: ghA_out[r * GA: r * GA + GSH_GATE].rearrange("(d i) -> d i", i=INTER)
    gh_out_u = lambda r: ghA_out[r * GA + GSH_GATE: (r + 1) * GA].rearrange("(d i) -> d i", i=INTER)
    gh_out_d = lambda r: ghB_out[r * GSH_DOWN: (r + 1) * GSH_DOWN].rearrange("(i e) -> i e", e=256)

    def ghost_job(kind, sl):
        if kind == "gate":
            return gidxT[:, sl * 344:(sl + 1) * 344], gh_in_g[:, sl * 344:(sl + 1) * 344], 2, 344
        if kind == "up":
            return uidxT[:, sl * 344:(sl + 1) * 344], gh_in_u[:, sl * 344:(sl + 1) * 344], 2, 344
        r0 = sl * 4 * 128
        nrc = 4 if sl < 10 else 3
        return didxT[r0:r0 + nrc * 128, :], gh_in_d[r0:r0 + nrc * 128, :], nrc, 256

    ctx = contextlib.ExitStack()
    with ctx:
        sbuf = lambda name, shape, dtp: ctx.enter_context(nc.sbuf_tensor(name, shape, dtp))
        consts = sbuf("consts", [128, 4], F32)
        ones_bf = sbuf("ones_bf", [128, 1], BF16)
        bc_lhs = sbuf("bc_lhs", [34, 128], F32)
        bc_rhs = sbuf("bc_rhs", [34, 1024], F32)
        lut_s = sbuf("lut_s", [128, 16], F32)
        x_bf = sbuf("x_bf", [128, NDC, T], BF16)
        xf = sbuf("xf", [128, 2, T], F32)
        sqc = sbuf("sqc", [128, 2, T], BF16)
        rows = sbuf("rows", [33, 1024], F32)
        rows2 = sbuf("rows2", [33, 1024], F32)
        bc_a = sbuf("bc_a", [128, T], F32)
        bc_b = sbuf("bc_b", [128, T], F32)
        statsT = sbuf("statsT", [128, 16], F32)
        s_neg = sbuf("s_neg", [128, 48], F32)
        tmp_f = sbuf("tmp_f", [128, T], F32)
        ones2_bf = sbuf("ones2_bf", [2, 128], BF16)
        qnorm_s = sbuf("qnorm_s", [33, 8192], BF16)
        kmax_s = sbuf("kmax_s", [33, 16], F32)
        knrow = sbuf("knrow", [1, 16], BF16)
        kn8 = sbuf("kn8", [1, 128], BF16)

        ch = lambda name: Ch(ctx, nc, name)
        chp = lambda name, n: [Ch(ctx, nc, f"{name}{i}") for i in range(n)]
        d_lut = ch("d_lut")
        d_xp = chp("d_x", 2); mk_x = {}
        d_wp = chp("d_w", 2); mk_w = {}
        d_wvp = chp("d_wv", 2); mk_wv = {}
        d_kvp = chp("d_kv", 2); mk_kv = {}
        d_gip = chp("d_gi", 3); mk_gi = {}
        d_bnc = ch("d_bnc")
        d_ksp = chp("d_ks", 2); mk_ks = {}
        d_vsp = chp("d_vs", 2); mk_vs = {}
        d_knt = ch("d_knt"); d_kn = ch("d_kn")
        d_ghp = chp("d_gh", 3); mk_gh = {}
        d_pjp = chp("d_pj", 2); mk_pj = {}
        d_x1p = chp("d_x1", 2); mk_x1 = {}
        d_dnp = chp("d_dn", 2); mk_dn = {}
        d_ldtp = chp("d_ldt", 2); mk_ldt = {}
        d_gwp = chp("d_gw", 4); mk_gw = {}
        mk_bc = {}
        d_lxp = chp("d_lx", 2); mk_lx = {}
        d_l5p = chp("d_l5", 2); mk_l5 = {}
        d_outp = chp("d_out", 2); mk_out = {}
        d_dbg = ch("d_dbg")
        v_cc = ch("v_cc"); v_xc = ch("v_xc"); v_wq = ch("v_wq"); v_wv = ch("v_wv")
        v_sx = ch("v_sx"); v_rowpre = ch("v_rowpre"); v_rowb = ch("v_rowb")
        v_bca = ch("v_bca"); v_bcb = ch("v_bcb")
        v_svst = ch("v_svst"); v_svbc = ch("v_svbc")
        v_epi = ch("v_epi"); v_vepi = ch("v_vepi")
        v_gc = ch("v_gc"); v_ghd = ch("v_ghd")
        s_k2 = ch("s_k2"); s_q2 = ch("s_q2"); s_qn = ch("s_qn")
        t_kn = ch("t_kn"); t_qn = ch("t_qn"); v_kn = ch("v_kn")
        v_mn = ch("v_mn")
        v_z = ch("v_z"); v_rz = ch("v_rz"); v_no = ch("v_no"); v_o = ch("v_o")
        v_w3 = ch("v_w3"); v_pe = ch("v_pe")
        v_x1 = ch("v_x1"); v_x1b = ch("v_x1b"); v_l2 = ch("v_l2")
        v_h = ch("v_h"); v_de = ch("v_de"); v_out = ch("v_out")
        s_sq = ch("s_sq"); s_exp = ch("s_exp"); s_si = ch("s_si")
        s_sqr = ch("s_sqr")
        s_sq2 = ch("s_sq2"); s_sq3 = ch("s_sq3"); s_sq4 = ch("s_sq4")
        t_stq = ch("t_stq"); t_qk = ch("t_qk"); t_s = ch("t_s")
        t_bc = ch("t_bc"); t_v = ch("t_v"); t_sv = ch("t_sv"); t_svb = ch("t_svb")
        t_smm = ch("t_smm"); t_pv = ch("t_pv"); t_z = ch("t_z"); t_rzb = ch("t_rzb")
        t_pj = ch("t_pj"); t_stq2 = ch("t_stq2")
        t_stq3 = ch("t_stq3")
        t_guu = ch("t_guu")
        t_dnu = ch("t_dnu")
        t_stq4 = ch("t_stq4")
        g_cc = ch("g_cc")
        bar = ch("bar")

        blk = ctx.enter_context(nc.Block())

        def bar_end(eng):
            eng.drain()
            eng.sem_inc(bar.sem, 1)

        def bar_wait(eng, phase):
            eng.wait_ge(bar.sem, 4 * phase)

        scA = contextlib.ExitStack()
        o_bf = scA.enter_context(nc.sbuf_tensor("o_bf", [128, NDC, T], BF16))
        gvi = scA.enter_context(nc.sbuf_tensor("gvi", [128, 3, 1024], I32))
        gvb = scA.enter_context(nc.sbuf_tensor("gvb", [128, 1024], BF16))
        gvm = scA.enter_context(nc.sbuf_tensor("gvm", [128, 1024], BF16))
        gvo = scA.enter_context(nc.sbuf_tensor("gvo", [128, 3, 1024], BF16))
        scQ = contextlib.ExitStack()
        qs = scQ.enter_context(nc.sbuf_tensor("qs", [128, NH, T], BF16))

        def ghost_spill_ap(j):
            kind, sl = GJOBS_V[j]
            _, dst, a, b = ghost_job(kind, sl)
            n = a * b
            oap = gvo[:, j % 3, 0:n].rearrange("p (a b) -> p a b", a=a)
            return dst.rearrange("(c p) i -> p c i", p=128), oap

        def ghost_units(eng, j):
            kind, sl = GJOBS_V[j]
            _, dst, a, b = ghost_job(kind, sl)
            n = a * b
            iap = gvi[:, j % 3, 0:n].rearrange("p (a b) -> p a b", a=a)
            bap = gvb[:, 0:n].rearrange("p (a b) -> p a b", a=a)
            map_ = gvm[:, 0:n].rearrange("p (a b) -> p a b", a=a)
            oap = gvo[:, j % 3, 0:n].rearrange("p (a b) -> p a b", a=a)

            def u_cast():
                d_gip[j % 3].wait(eng, mk_gi[j])
                d_lut.wait(eng, 16)
                v_gc.inc(eng.tensor_copy(out=bap, in_=iap))

            def u_k(k):
                eng.tensor_scalar(out=map_, in0=bap, scalar1=float(k),
                                  scalar2=lut_s[:, k:k + 1], op0=OP.is_equal,
                                  op1=OP.mult)
                if k == 0:
                    if j >= 3:
                        d_ghp[j % 3].wait(eng, mk_gh[j - 3])
                    eng.tensor_copy(out=oap, in_=map_)
                elif k < 15:
                    eng.scalar_tensor_tensor(out=oap, in0=map_, scalar=1.0,
                                             in1=oap, op0=OP.mult, op1=OP.add)
                else:
                    v_ghd.inc(eng.scalar_tensor_tensor(
                        out=oap, in0=map_, scalar=1.0, in1=oap,
                        op0=OP.mult, op1=OP.add))

            return [u_cast] + [(lambda k=k: u_k(k)) for k in range(16)]

        p1q = contextlib.ExitStack()
        wst = p1q.enter_context(nc.sbuf_tensor("wst", [128, 2, NDC, 128], F32))
        ktmp = p1q.enter_context(nc.sbuf_tensor("ktmp", [128, 2, T], BF16))
        wbt = p1q.enter_context(nc.sbuf_tensor("wbt", [128, 2, NDC, 128], BF16))
        p1p = contextlib.ExitStack()
        qk_ps = p1p.enter_context(nc.psum_tensor("qk_ps", [128, 2, 1024], F32))
        st_ps = p1p.enter_context(nc.psum_tensor("st_ps", [128, 1024], F32))
        sA_ps = p1p.enter_context(nc.psum_tensor("sA_ps", [128, 1], F32))
        sB_ps = p1p.enter_context(nc.psum_tensor("sB_ps", [128, 1], F32))

        @blk.sync
        def _(sync):
            d_lut.dinc(sync.dma_start(out=lut_s[:], in_=lutb[:]))
            for dc in range(NDC):
                if dc >= 2:
                    v_xc.wait(sync, dc - 1)
                mk_x[dc] = d_xp[dc % 2].dinc(sync.dma_start(
                    out=xf[:, dc % 2, :],
                    in_=xT[dc * 128:(dc + 1) * 128, :]))
            for ec in range(32):
                if ec >= 2:
                    v_wq.wait(sync, ec - 1)
                for sub in range(4):
                    mk_w[ec] = d_wp[ec % 2].dinc(sync.dma_start(
                        out=wst[:, ec % 2, sub * 4:(sub + 1) * 4, :],
                        in_=wqkvT[sub * 512:(sub + 1) * 512,
                                  ec * 128:(ec + 1) * 128]
                        .rearrange("(c p) e -> p c e", p=128)))

        @blk.scalar
        def _(scalar):
            scalar = Drained(scalar)
            for dc in range(NDC):
                v_xc.wait(scalar, dc + 1)
                if dc >= 2:
                    t_stq.wait(scalar, dc - 1)
                s_sq.inc(scalar.square(out=sqc[:, dc % 2, :], in_=x_bf[:, dc, :]))
            v_rowpre.wait(scalar, 1)
            s_sqr.inc(scalar.sqrt(out=rows[0:1, :], in_=rows2[32:33, :]))
            v_rowb.wait(scalar, 1)
            d_bnc.dinc(scalar.dma_start(
                out=st_bnc[0:1024].rearrange("(a b) -> a b", a=1),
                in_=rows[0:1, :]))
            d_bnc.dinc(scalar.dma_start(
                out=st_bnc[1024:2048].rearrange("(a b) -> a b", a=1),
                in_=rows[32:33, :]))
            d_bnc.wait(scalar, 32)
            d_bnc.dinc(scalar.dma_start(
                out=statsT[:, 0:8],
                in_=st_bnc[0:1024].rearrange("(c p) -> p c", p=128),
                allow_slow_non_contiguous=True))
            d_bnc.dinc(scalar.dma_start(
                out=statsT[:, 8:16],
                in_=st_bnc[1024:2048].rearrange("(c p) -> p c", p=128),
                allow_slow_non_contiguous=True))
            for ec in range(16, 32):
                v_epi.wait(scalar, ec + 1)
                mk_ks[ec] = d_ksp[ec % 2].dinc(scalar.dma_start(
                    out=kv_in_K[(ec - 16) * 128:(ec - 15) * 128, :],
                    in_=ktmp[:, ec % 2, :]))
                if ec == 16:
                    t_stq.wait(scalar, NDC)
                if ec >= 18:
                    t_kn.wait(scalar, ec - 17)
                s_k2.inc(scalar.square(out=sqc[:, ec % 2, :], in_=ktmp[:, ec % 2, :]))
            v_kn.wait(scalar, 16)
            d_knt.dinc(scalar.dma_start(
                out=kv_in_N.rearrange("(a b) -> a b", a=1), in_=knrow[:]))
            for h in range(NH):
                if h == 0:
                    t_kn.wait(scalar, 16)
                if h >= 2:
                    t_qn.wait(scalar, h - 1)
                s_q2.inc(scalar.square(out=sqc[:, h % 2, :], in_=qs[:, h, :]))
                t_qn.wait(scalar, h + 1)
                pr = 32 * (h % 2)
                s_qn.inc(scalar.sqrt(
                    out=qnorm_s[32 * (h // 8):32 * (h // 8) + 1,
                                (h % 8) * 1024:(h % 8) * 1024 + 1024],
                    in_=st_ps[pr:pr + 1, :]))

        @blk.vector
        def _(vector):
            vector = Drained(vector)
            v_cc.inc(vector.memset(consts[:, 0:1], EXP_BIAS))
            v_cc.inc(vector.memset(consts[:, 1:2], SCALE_ATT))
            v_cc.inc(vector.memset(ones_bf[:], 1.0))
            v_cc.inc(vector.memset(bc_lhs[:], 0.0))
            v_cc.inc(vector.memset(bc_lhs[0:1, :], 1.0))
            v_cc.inc(vector.memset(bc_lhs[32:33, :], 1.0))
            v_cc.inc(vector.memset(bc_rhs[:], 0.0))
            v_cc.inc(vector.memset(ones2_bf[:], 0.0))
            v_cc.inc(vector.memset(ones2_bf[0:1, :], 1.0))
            for dc in range(NDC):
                d_xp[dc % 2].wait(vector, mk_x[dc])
                if dc >= 2:
                    t_stq.wait(vector, dc - 1)
                v_xc.inc(vector.tensor_copy(out=x_bf[:, dc, :], in_=xf[:, dc % 2, :]))
            for ec in range(36):
                if ec == 3:
                    t_stq.wait(vector, NDC)
                    inv = 1.0 / D
                    vector.tensor_scalar(out=rows2[0:1, :], in0=st_ps[0:1, :],
                                         scalar1=inv, scalar2=None, op0=OP.mult)
                    vector.tensor_scalar(out=rows[32:33, :], in0=st_ps[32:33, :],
                                         scalar1=inv, scalar2=EPS,
                                         op0=OP.mult, op1=OP.add)
                    vector.tensor_tensor(out=rows2[32:33, :], in0=rows2[0:1, :],
                                         in1=rows2[0:1, :], op=OP.mult)
                    vector.tensor_tensor(out=rows[32:33, :], in0=rows[32:33, :],
                                         in1=rows2[32:33, :], op=OP.subtract)
                    v_rowpre.inc(vector.reciprocal(out=rows2[32:33, :],
                                                   in_=rows[32:33, :]))
                    s_sqr.wait(vector, 1)
                    vector.tensor_tensor(out=rows[32:33, :], in0=rows2[0:1, :],
                                         in1=rows[0:1, :], op=OP.mult)
                    vector.tensor_copy(out=bc_rhs[0:1, :], in_=rows[0:1, :])
                    v_rowb.inc(vector.tensor_copy(out=bc_rhs[32:33, :],
                                                  in_=rows[32:33, :]))
                    t_bc.wait(vector, 2)
                    v_bca.inc(vector.tensor_copy(out=bc_a[:], in_=st_ps[:]))
                    t_bc.wait(vector, 4)
                    v_bcb.inc(vector.tensor_copy(out=bc_b[:], in_=st_ps[:]))
                if ec >= 4:
                    e1 = ec - 4
                    t_qk.wait(vector, e1 + 1)
                    v_sx.wait(vector, e1 + 1)
                    if e1 >= 18:
                        d_ksp[e1 % 2].wait(vector, mk_ks[e1 - 2])
                    vector.tensor_tensor(out=tmp_f[:], in0=qk_ps[:, e1 % 2, :],
                                         in1=bc_a[:], op=OP.mult)
                    dst = qs[:, e1, :] if e1 < 16 else ktmp[:, e1 % 2, :]
                    v_epi.inc(vector.scalar_tensor_tensor(
                        out=dst, in0=bc_b[:], scalar=s_neg[:, e1:e1 + 1],
                        in1=tmp_f[:], op0=OP.mult, op1=OP.add))
                if ec < 32:
                    d_wp[ec % 2].wait(vector, mk_w[ec])
                    if ec >= 2:
                        t_s.wait(vector, ec - 1)
                    v_wq.inc(vector.tensor_copy(out=wbt[:, ec % 2, :, :],
                                                in_=wst[:, ec % 2, :, :]))
                if 2 <= ec < 34:
                    e0 = ec - 2
                    t_s.wait(vector, e0 + 1)
                    sp = sA_ps if e0 % 2 == 0 else sB_ps
                    v_sx.inc(vector.tensor_scalar(
                        out=s_neg[:, e0:e0 + 1], in0=sp[:], scalar1=-1.0,
                        scalar2=None, op0=OP.mult))
                if 20 <= ec < 36:
                    e2 = ec - 4
                    t_kn.wait(vector, e2 - 15)
                    pr = 32 * (e2 % 2)
                    v_kn.inc(vector.reduce_max(
                        out=knrow[:, e2 - 16:e2 - 15], in_=st_ps[pr:pr + 1, :],
                        axis=mybir.AxisListType.X))
        @blk.tensor
        def _(tensor):
            v_cc.wait(tensor, 9)
            for dc in range(NDC):
                v_xc.wait(tensor, dc + 1)
                for half in range(2):
                    tensor.matmul(st_ps[0:1, half * 512:(half + 1) * 512],
                                  ones_bf[:],
                                  x_bf[:, dc, half * 512:(half + 1) * 512],
                                  start=(dc == 0), stop=(dc == NDC - 1))
                s_sq.wait(tensor, dc + 1)
                for half in range(2):
                    i = tensor.matmul(st_ps[32:33, half * 512:(half + 1) * 512],
                                      ones_bf[:],
                                      sqc[:, dc % 2, half * 512:(half + 1) * 512],
                                      start=(dc == 0), stop=(dc == NDC - 1))
                t_stq.inc(i)
            for ec in range(32):
                v_wq.wait(tensor, ec + 1)
                if ec == 2:
                    v_rowb.wait(tensor, 1)
                    for half in range(2):
                        i = tensor.matmul(
                            st_ps[:, half * 512:(half + 1) * 512], bc_lhs[0:2, :],
                            bc_rhs[0:2, half * 512:(half + 1) * 512],
                            start=True, stop=True)
                    t_bc.inc(i, 2)
                    v_bca.wait(tensor, 1)
                    for half in range(2):
                        i = tensor.matmul(
                            st_ps[:, half * 512:(half + 1) * 512], bc_lhs[32:34, :],
                            bc_rhs[32:34, half * 512:(half + 1) * 512],
                            start=True, stop=True)
                    t_bc.inc(i, 2)
                if ec >= 2:
                    v_epi.wait(tensor, ec - 1)
                    v_sx.wait(tensor, ec - 1)
                sp = sA_ps if ec % 2 == 0 else sB_ps
                for dc in range(NDC):
                    for half in range(2):
                        i = tensor.matmul(
                            qk_ps[:, ec % 2, half * 512:(half + 1) * 512],
                            wbt[:, ec % 2, dc, :],
                            x_bf[:, dc, half * 512:(half + 1) * 512],
                            start=(dc == 0), stop=(dc == NDC - 1))
                    j = tensor.matmul(sp[:], wbt[:, ec % 2, dc, :], ones_bf[:],
                                      start=(dc == 0), stop=(dc == NDC - 1))
                t_qk.inc(i)
                t_s.inc(j)
                if ec >= 18:
                    e = ec - 2
                    if e == 16:
                        v_bcb.wait(tensor, 1)
                    s_k2.wait(tensor, e - 15)
                    v_kn.wait(tensor, max(0, e - 17))
                    pr = 32 * (e % 2)
                    for half in range(2):
                        i = tensor.matmul(
                            st_ps[pr:pr + 1, half * 512:(half + 1) * 512],
                            ones_bf[:], sqc[:, e % 2, half * 512:(half + 1) * 512],
                            start=True, stop=True)
                    t_kn.inc(i)
            for e in (30, 31):
                s_k2.wait(tensor, e - 15)
                v_kn.wait(tensor, e - 17)
                pr = 32 * (e % 2)
                for half in range(2):
                    i = tensor.matmul(
                        st_ps[pr:pr + 1, half * 512:(half + 1) * 512],
                        ones_bf[:], sqc[:, e % 2, half * 512:(half + 1) * 512],
                        start=True, stop=True)
                t_kn.inc(i)
            for h in range(NH):
                s_q2.wait(tensor, h + 1)
                if h >= 2:
                    s_qn.wait(tensor, h - 1)
                pr = 32 * (h % 2)
                for half in range(2):
                    i = tensor.matmul(
                        st_ps[pr:pr + 1, half * 512:(half + 1) * 512],
                        ones_bf[:], sqc[:, h % 2, half * 512:(half + 1) * 512],
                        start=True, stop=True)
                t_qn.inc(i)

        p1q.close()

        p1v = contextlib.ExitStack()
        wstv = p1v.enter_context(nc.sbuf_tensor("wstv", [128, 2, 2, 512], F32))
        vtmp = p1v.enter_context(nc.sbuf_tensor("vtmp", [128, 2, 512], BF16))
        wbv = p1v.enter_context(nc.sbuf_tensor("wbv", [128, NDC, 512], BF16))
        s_bc = p1v.enter_context(nc.sbuf_tensor("s_bc", [128, 2, 512], F32))

        @blk.sync
        def _(sync):
            for u in range(32):
                es, sub = divmod(u, 8)
                if u == 0:
                    v_wq.wait(sync, 32)
                if u >= 2:
                    v_wv.wait(sync, u - 1)
                mk_wv[u] = d_wvp[u % 2].dinc(sync.dma_start(
                    out=wstv[:, u % 2, :, :],
                    in_=wqkvT[sub * 256:(sub + 1) * 256,
                              4096 + es * 512:4096 + (es + 1) * 512]
                    .rearrange("(c p) e -> p c e", p=128)))
            bar_end(sync)

        @blk.scalar
        def _(scalar):
            scalar = Drained(scalar)
            for es in range(4):
                for tt in range(8):
                    un = es * 8 + tt
                    v_vepi.wait(scalar, un + 1)
                    mk_vs[un] = d_vsp[un % 2].dinc(scalar.dma_start(
                        out=kv_in_V[tt * 128:(tt + 1) * 128,
                                    es * 512:(es + 1) * 512],
                        in_=vtmp[:, un % 2, :]))
            bar_end(scalar)

        @blk.vector
        def _(vector):
            vector = Drained(vector)
            d_bnc.wait(vector, 64)
            for es in range(4):
                for sub in range(8):
                    u = es * 8 + sub
                    d_wvp[u % 2].wait(vector, mk_wv[u])
                    if u == 0:
                        t_s.wait(vector, 32)
                    if sub == 0 and es >= 1:
                        t_v.wait(vector, es * 8)
                    v_wv.inc(vector.tensor_copy(
                        out=wbv[:, sub * 2:(sub + 1) * 2, :],
                        in_=wstv[:, u % 2, :, :]))
                t_sv.wait(vector, es + 1)
                if es >= 1:
                    t_svb.wait(vector, es)
                srcp = st_ps[64:65, (es % 2) * 512:(es % 2 + 1) * 512]
                v_svst.inc(vector.tensor_copy(out=bc_rhs[0:1, 0:512], in_=srcp))
                t_svb.wait(vector, es + 1)
                dstb = st_ps[:, (es % 2) * 512:(es % 2 + 1) * 512]
                v_svbc.inc(vector.tensor_copy(
                    out=s_bc[:, es % 2, :], in_=dstb))
                for tt in range(8):
                    un = es * 8 + tt
                    t_v.wait(vector, un + 1)
                    if un >= 2:
                        d_vsp[un % 2].wait(vector, mk_vs[un - 2])
                    vector.tensor_scalar(
                        out=tmp_f[:, 0:512], in0=s_bc[:, es % 2, :],
                        scalar1=statsT[:, 8 + tt:9 + tt], scalar2=None, op0=OP.mult)
                    v_vepi.inc(vector.scalar_tensor_tensor(
                        out=vtmp[:, un % 2, :], in0=qk_ps[:, un % 2, 0:512],
                        scalar=statsT[:, tt:tt + 1], in1=tmp_f[:, 0:512],
                        op0=OP.mult, op1=OP.subtract))
            bar_end(vector)

        @blk.tensor
        def _(tensor):
            for es in range(4):
                v_wv.wait(tensor, es * 8 + 8)
                if es == 0:
                    s_qn.wait(tensor, NH)
                if es >= 2:
                    v_svbc.wait(tensor, es - 1)
                dstp = st_ps[64:65, (es % 2) * 512:(es % 2 + 1) * 512]
                for dc in range(NDC):
                    i = tensor.matmul(dstp, ones_bf[:], wbv[:, dc, :],
                                      start=(dc == 0), stop=(dc == NDC - 1))
                t_sv.inc(i)
                v_svst.wait(tensor, es + 1)
                i = tensor.matmul(st_ps[:, (es % 2) * 512:(es % 2 + 1) * 512],
                                  bc_lhs[0:2, :], bc_rhs[0:2, 0:512],
                                  start=True, stop=True)
                t_svb.inc(i)
                for tt in range(8):
                    un = es * 8 + tt
                    if un >= 2:
                        v_vepi.wait(tensor, un - 1)
                    for dc in range(NDC):
                        i = tensor.matmul(qk_ps[:, un % 2, 0:512],
                                          x_bf[:, dc, tt * 128:(tt + 1) * 128],
                                          wbv[:, dc, :],
                                          start=(dc == 0), stop=(dc == NDC - 1))
                    t_v.inc(i)
            bar_end(tensor)

        @blk.gpsimd
        def _(gpsimd):
            d_ksp[0].wait(gpsimd)
            d_ksp[1].wait(gpsimd)
            d_knt.wait(gpsimd)
            g_cc.inc(gpsimd.collective_compute(
                "AllGather", OP.bypass, ins=[kv_in[0:KA]], outs=[kvA_out[:]],
                replica_groups=[list(range(NCORES))]))
            d_vsp[0].wait(gpsimd)
            d_vsp[1].wait(gpsimd)
            g_cc.inc(gpsimd.collective_compute(
                "AllGather", OP.bypass, ins=[kv_in[KA:KA + D * T]],
                outs=[kvB_out[:]],
                replica_groups=[list(range(NCORES))]))

        p1v.close()
        p1p.close()

        p2s = contextlib.ExitStack()
        kst = p2s.enter_context(nc.sbuf_tensor("kst", [128, 2, T], BF16))
        vst = p2s.enter_context(nc.sbuf_tensor("vst", [128, 2, 8, 128], BF16))
        pst = p2s.enter_context(nc.sbuf_tensor("pst", [128, 2, 1024], BF16))
        mneg = p2s.enter_context(nc.sbuf_tensor("mneg", [2, 2, 1024], BF16))
        zacc = p2s.enter_context(nc.sbuf_tensor("zacc", [128, 2, 1024], BF16))
        p2p = contextlib.ExitStack()
        sat_ps = p2p.enter_context(nc.psum_tensor("sat_ps", [128, 2, 1024], F32))
        pv_ps = p2p.enter_context(nc.psum_tensor("pv_ps", [128, 1024], F32))
        z_ps = p2p.enter_context(nc.psum_tensor("z_ps", [128, 1024], F32))

        pop_slots = []
        for _h in range(NH):
            for _j in range(64):
                _busy = ((_j == 2 and _h + 1 < NH) or (_j == 3 and _h >= 1)
                         or (_j == 8 and _h >= 1) or (_j == 63))
                if not _busy:
                    pop_slots.append(_h * 64 + _j)
        first_pop = [pop_slots[17 * j] for j in range(NGV)]
        done_pop = [pop_slots[17 * (j + 1) - 1] for j in range(NGV)]

        def gv_allow(h):
            n = sum(1 for f in first_pop if f <= 64 * h - 1)
            return min(NGV, n + 3)

        @blk.sync
        def _(sync):
            bar_wait(sync, 1)
            gvj = 0
            for h in range(NH):
                for r in range(NCORES):
                    u = h * NCORES + r
                    if u == 0:
                        g_cc.wait(sync, 1)
                        d_kn.dinc(sync.dma_start(
                            out=kn8[:].rearrange("a (r h) -> a r h", r=8),
                            in_=kvA_out[0:8 * KA].rearrange(
                                "(r x) -> r x", r=8)[:, D * T:D * T + 16]))
                    if u % 2 == 0 and gvj < gv_allow(h):
                        if gvj >= 3:
                            v_gc.wait(sync, gvj - 2)
                        kind, sl = GJOBS_V[gvj]
                        srcg, _, a, b = ghost_job(kind, sl)
                        mk_gi[gvj] = d_gip[gvj % 3].dinc(sync.dma_start(
                            out=gvi[:, gvj % 3, 0:a * b].rearrange(
                                "p (a b) -> p a b", a=a),
                            in_=srcg.rearrange("(c p) i -> p c i", p=128)))
                        gvj += 1
                    if u >= 2:
                        t_pv.wait(sync, (u - 1) * 8)
                    d_kvp[u % 2].dinc(sync.dma_start(
                        out=kst[:, u % 2, :],
                        in_=kv_out_K(r)[h * 128:(h + 1) * 128, :]))
                    if u == 0:
                        g_cc.wait(sync, 2)
                    mk_kv[u] = d_kvp[u % 2].dinc(sync.dma_start(
                        out=vst[:, u % 2, :, :],
                        in_=kv_out_V(r)[:, h * 128:(h + 1) * 128]
                        .rearrange("(c p) e -> p c e", p=128)))
            bar_end(sync)

        def _emit_colsums(tensor, hz):
            v_z.wait(tensor, (hz + 1) * 64)
            if hz >= 1:
                v_no.wait(tensor, hz)
            for half in range(2):
                for p in range(2):
                    i = tensor.matmul(
                        z_ps[0:1, half * 512:(half + 1) * 512], ones_bf[:],
                        zacc[:, p, half * 512:(half + 1) * 512],
                        start=(p == 0), stop=(p == 1))
            t_z.inc(i)

        def _emit_rzmm(tensor, hz):
            v_rz.wait(tensor, hz + 1)
            for half in range(2):
                i = tensor.matmul(z_ps[:, half * 512:(half + 1) * 512],
                                  bc_lhs[0:2, :],
                                  bc_rhs[0:2, half * 512:(half + 1) * 512],
                                  start=True, stop=True)
            t_rzb.inc(i)

        @blk.tensor
        def _(tensor):
            bar_wait(tensor, 1)
            for h in range(NH):
                for j in range(65):
                    gj = h * 64 + j
                    if h >= 1 and j == 1:
                        _emit_colsums(tensor, h - 1)
                    if h >= 1 and j == 6:
                        _emit_rzmm(tensor, h - 1)
                    if j < 64:
                        r, jj = divmod(j, 8)
                        u = h * NCORES + r
                        if jj == 0:
                            d_kvp[u % 2].wait(tensor, mk_kv[u])
                        if gj >= 2:
                            s_exp.wait(tensor, gj - 1)
                        for half in range(2):
                            tensor.matmul(
                                sat_ps[:, gj % 2, half * 512:(half + 1) * 512],
                                kst[:, u % 2, jj * 128:(jj + 1) * 128],
                                qs[:, h, half * 512:(half + 1) * 512],
                                start=True, stop=False)
                        if j == 0:
                            v_mn.wait(tensor, h + 1)
                        for half in range(2):
                            i = tensor.matmul(
                                sat_ps[:, gj % 2, half * 512:(half + 1) * 512],
                                ones2_bf[:], mneg[0:2, h % 2, half * 512:(half + 1) * 512],
                                start=False, stop=True)
                        t_smm.inc(i)
                    if j >= 1:
                        jp = j - 1
                        gp = h * 64 + jp
                        rp, jjp = divmod(jp, 8)
                        up = h * NCORES + rp
                        if jp == 0:
                            v_o.wait(tensor, h)
                        s_exp.wait(tensor, gp + 1)
                        for half in range(2):
                            i = tensor.matmul(
                                pv_ps[:, half * 512:(half + 1) * 512],
                                vst[:, up % 2, jjp, :],
                                pst[:, gp % 2, half * 512:(half + 1) * 512],
                                start=(jp == 0), stop=(jp == 63))
                        t_pv.inc(i)
            _emit_colsums(tensor, NH - 1)
            _emit_rzmm(tensor, NH - 1)
            bar_end(tensor)

        @blk.scalar
        def _(scalar):
            scalar = Drained(scalar)
            bar_wait(scalar, 1)
            v_kn.wait(scalar, 17)
            scalar.sqrt(out=kmax_s[0:1, :], in_=kmax_s[0:1, :])
            s_qn.inc(scalar.mul(out=kmax_s[0:1, :], in_=kmax_s[0:1, :], mul=-1.0))
            spilled = 0
            for h in range(NH):
                for j in range(64):
                    gj = h * 64 + j
                    t_smm.wait(scalar, gj + 1)
                    if gj >= 2:
                        t_pv.wait(scalar, gj - 1)
                        v_z.wait(scalar, gj - 1)
                    s_exp.inc(scalar.activation(
                        out=pst[:, gj % 2, :], in_=sat_ps[:, gj % 2, :],
                        func=AF.Exp, bias=consts[:, 0:1], scale=consts[:, 1:2]))
                    while spilled < NGV and done_pop[spilled] + 6 <= gj:
                        v_ghd.wait(scalar, spilled + 1)
                        dsta, oap = ghost_spill_ap(spilled)
                        mk_gh[spilled] = d_ghp[spilled % 3].dinc(
                            scalar.dma_start(out=dsta, in_=oap))
                        spilled += 1
            while spilled < NGV:
                v_ghd.wait(scalar, spilled + 1)
                dsta, oap = ghost_spill_ap(spilled)
                mk_gh[spilled] = d_ghp[spilled % 3].dinc(
                    scalar.dma_start(out=dsta, in_=oap))
                spilled += 1
            bar_end(scalar)

        @blk.vector
        def _(vector):
            vector = Drained(vector)
            bar_wait(vector, 1)
            d_kn.wait(vector, 16)
            for h in range(NH):
                vector.reduce_max(
                    out=kmax_s[0:1, h:h + 1],
                    in_=kn8[:].rearrange("a (r h) -> a r h", r=8)[:, :, h],
                    axis=mybir.AxisListType.X)
            v_kn.inc(vector.tensor_copy(out=kmax_s[0:1, :], in_=kmax_s[0:1, :]))
            s_qn.wait(vector, NH + 1)
            vector.tensor_copy(out=kmax_s[32:33, :], in_=kmax_s[0:1, :])
            vector.memset(mneg[0:2, :, :], 0.0)

            def _emit_mneg(hh):
                v_mn.inc(vector.tensor_scalar(
                    out=mneg[0:1, hh % 2, :],
                    in0=qnorm_s[32 * (hh // 8):32 * (hh // 8) + 1,
                                (hh % 8) * 1024:(hh % 8) * 1024 + 1024],
                    scalar1=kmax_s[32 * (hh // 8):32 * (hh // 8) + 1, hh:hh + 1],
                    scalar2=None, op0=OP.mult))

            _emit_mneg(0)
            gq = []
            gnext = 0

            def gpump():
                nonlocal gnext
                if not gq and gnext < NGV:
                    gq.extend(ghost_units(vector, gnext))
                    gnext += 1
                if gq:
                    gq.pop(0)()

            for h in range(NH):
                for j in range(64):
                    gj = h * 64 + j
                    s_exp.wait(vector, gj + 1)
                    if j < 2:
                        if h >= 1 and j == 0:
                            t_z.wait(vector, h)
                        v_z.inc(vector.tensor_copy(out=zacc[:, j, :],
                                                   in_=pst[:, gj % 2, :]))
                    else:
                        v_z.inc(vector.tensor_tensor(
                            out=zacc[:, j % 2, :], in0=pst[:, gj % 2, :],
                            in1=zacc[:, j % 2, :], op=OP.add))
                    busy = True
                    if j == 2 and h + 1 < NH:
                        t_smm.wait(vector, h * 64)
                        _emit_mneg(h + 1)
                    elif j == 3 and h >= 1:
                        t_z.wait(vector, h)
                        v_rz.inc(vector.reciprocal(out=bc_rhs[0:1, :],
                                                   in_=z_ps[0:1, :]))
                    elif j == 8 and h >= 1:
                        t_rzb.wait(vector, h)
                        v_no.inc(vector.tensor_tensor(
                            out=o_bf[:, h - 1, :], in0=o_bf[:, h - 1, :],
                            in1=z_ps[:], op=OP.mult))
                    elif j == 63:
                        t_pv.wait(vector, (h + 1) * 64)
                        v_o.inc(vector.tensor_copy(out=o_bf[:, h, :],
                                                   in_=pv_ps[:]))
                    else:
                        busy = False
                    if not busy:
                        gpump()
            t_z.wait(vector, NH)
            v_rz.inc(vector.reciprocal(out=bc_rhs[0:1, :], in_=z_ps[0:1, :]))
            t_rzb.wait(vector, NH)
            v_no.inc(vector.tensor_tensor(
                out=o_bf[:, NH - 1, :], in0=o_bf[:, NH - 1, :],
                in1=z_ps[:], op=OP.mult))
            while gq or gnext < NGV:
                gpump()
            bar_end(vector)

        @blk.gpsimd
        def _(gpsimd):
            for c in range(3):
                d_ghp[c].wait(gpsimd, mk_gh[max(j for j in range(32)
                                               if j % 3 == c)])
            g_cc.inc(gpsimd.collective_compute(
                "AllGather", OP.bypass, ins=[gh_in[0:GA]], outs=[ghA_out[:]],
                replica_groups=[list(range(NCORES))]))
            d_ghp[0].wait(gpsimd)
            d_ghp[1].wait(gpsimd)
            d_ghp[2].wait(gpsimd)
            g_cc.inc(gpsimd.collective_compute(
                "AllGather", OP.bypass, ins=[gh_in[GA:GH_BLK]],
                outs=[ghB_out[:]],
                replica_groups=[list(range(NCORES))]))

        p2s.close()
        p2p.close()
        scQ.close()

        if stage <= 2:
            @blk.sync
            def _(sync):
                bar_wait(sync, 2)
                g_cc.wait(sync, 4)
                d_dbg.dinc(sync.dma_start(out=dbg["d_q"].rearrange(
                    "(c p) t -> p c t", p=128), in_=qs[:]))
                d_dbg.dinc(sync.dma_start(out=dbg["d_k"][:], in_=kv_in_K[:]))
                d_dbg.dinc(sync.dma_start(out=dbg["d_v"][:], in_=kv_in_V[:]))
                d_dbg.dinc(sync.dma_start(out=dbg["d_stats"][0:1, :],
                                          in_=rows[0:1, :]))
                d_dbg.dinc(sync.dma_start(out=dbg["d_stats"][1:2, :],
                                          in_=rows[32:33, :]))
                d_dbg.dinc(sync.dma_start(out=dbg["d_o"].rearrange(
                    "(c p) t -> p c t", p=128), in_=o_bf[:]))
                d_outp[0].dinc(sync.dma_start(out=outT[0:128, :], in_=xf[:, 0, :]))
                d_dbg.wait(sync)
                d_outp[0].wait(sync)
            scA.close()
            return nc, dbg, ctx

        p3s = contextlib.ExitStack()
        wst3 = p3s.enter_context(nc.sbuf_tensor("wst3", [128, 2, NDC, 128], F32))
        wbt3 = p3s.enter_context(nc.sbuf_tensor("wbt3", [128, 2, NDC, 128], BF16))
        pjt = p3s.enter_context(nc.sbuf_tensor("pjt", [128, 2, T], BF16))
        x1t = p3s.enter_context(nc.sbuf_tensor("x1t", [128, 2, T], F32))
        x1b = p3s.enter_context(nc.sbuf_tensor("x1b", [128, 2, T], BF16))
        ldt = p3s.enter_context(nc.sbuf_tensor("ldt", [128, 2, T], BF16))
        p3p = contextlib.ExitStack()
        st2_ps = p3p.enter_context(nc.psum_tensor("st2_ps", [128, 1024], F32))
        pj_ps = p3p.enter_context(nc.psum_tensor("pj_ps", [128, 2, 1024], F32))

        @blk.scalar
        def _(scalar):
            scalar = Drained(scalar)
            bar_wait(scalar, 2)
            for ec in range(NDC):
                v_pe.wait(scalar, ec + 1)
                if ec >= 2:
                    t_stq2.wait(scalar, ec - 1)
                s_sq2.inc(scalar.square(out=sqc[:, ec % 2, :], in_=pjt[:, ec % 2, :]))
                mk_pj[ec] = d_pjp[ec % 2].dinc(scalar.dma_start(
                    out=proj_sc[ec * 128:(ec + 1) * 128, :], in_=pjt[:, ec % 2, :]))
            v_rowpre.wait(scalar, 2)
            s_sqr.inc(scalar.sqrt(out=rows[0:1, :], in_=rows2[32:33, :]))
            for ec in range(NDC):
                v_x1.wait(scalar, ec + 1)
                mk_x1[ec] = d_x1p[ec % 2].dinc(scalar.dma_start(
                    out=x1_sc[ec * 128:(ec + 1) * 128, :], in_=x1t[:, ec % 2, :]))
                v_x1b.wait(scalar, ec + 1)
                if ec >= 2:
                    t_stq3.wait(scalar, ec - 1)
                s_sq3.inc(scalar.square(out=sqc[:, ec % 2, :], in_=x1b[:, ec % 2, :]))
            v_rowpre.wait(scalar, 3)
            s_sqr.inc(scalar.sqrt(out=rows[0:1, :], in_=rows2[32:33, :]))
            bar_end(scalar)

        @blk.sync
        def _(sync):
            bar_wait(sync, 2)
            for ec in range(NDC):
                if ec >= 2:
                    v_w3.wait(sync, ec - 1)
                for sub in range(4):
                    mk_w[("p3", ec)] = d_wp[ec % 2].dinc(sync.dma_start(
                        out=wst3[:, ec % 2, sub * 4:(sub + 1) * 4, :],
                        in_=wprojT[sub * 512:(sub + 1) * 512,
                                   ec * 128:(ec + 1) * 128]
                        .rearrange("(c p) e -> p c e", p=128)))
            for ec in range(NDC):
                if ec >= 2:
                    v_x1.wait(sync, ec - 1)
                mk_x[("p3", ec)] = d_xp[ec % 2].dinc(sync.dma_start(
                    out=xf[:, ec % 2, :],
                    in_=xT[ec * 128:(ec + 1) * 128, :]))
                d_pjp[ec % 2].wait(sync, mk_pj[ec])
                mk_ldt[ec] = d_ldtp[ec % 2].dinc(sync.dma_start(
                    out=ldt[:, ec % 2, :],
                    in_=proj_sc[ec * 128:(ec + 1) * 128, :]))
            d_x1p[0].wait(sync)
            d_x1p[1].wait(sync)
            for ec in range(NDC):
                if ec >= 2:
                    v_l2.wait(sync, ec - 1)
                mk_lx[ec] = d_lxp[ec % 2].dinc(sync.dma_start(
                    out=xf[:, ec % 2, :],
                    in_=x1_sc[ec * 128:(ec + 1) * 128, :]))
            bar_end(sync)

        @blk.tensor
        def _(tensor):
            bar_wait(tensor, 2)
            for ec in range(NDC):
                v_w3.wait(tensor, ec + 1)
                if ec >= 2:
                    v_pe.wait(tensor, ec - 1)
                for dc in range(NDC):
                    for half in range(2):
                        i = tensor.matmul(
                            pj_ps[:, ec % 2, half * 512:(half + 1) * 512],
                            wbt3[:, ec % 2, dc, :],
                            o_bf[:, dc, half * 512:(half + 1) * 512],
                            start=(dc == 0), stop=(dc == NDC - 1))
                t_pj.inc(i)
                v_pe.wait(tensor, ec + 1)
                for half in range(2):
                    tensor.matmul(st2_ps[0:1, half * 512:(half + 1) * 512],
                                  ones_bf[:],
                                  pjt[:, ec % 2, half * 512:(half + 1) * 512],
                                  start=(ec == 0), stop=(ec == NDC - 1))
                s_sq2.wait(tensor, ec + 1)
                for half in range(2):
                    i = tensor.matmul(st2_ps[32:33, half * 512:(half + 1) * 512],
                                      ones_bf[:],
                                      sqc[:, ec % 2, half * 512:(half + 1) * 512],
                                      start=(ec == 0), stop=(ec == NDC - 1))
                t_stq2.inc(i)
            v_rowb.wait(tensor, 2)
            for half in range(2):
                i = tensor.matmul(st2_ps[:, half * 512:(half + 1) * 512],
                                  bc_lhs[0:2, :], bc_rhs[0:2, half * 512:(half + 1) * 512],
                                  start=True, stop=True)
            t_bc.inc(i, 2)
            mk_bc[("pa", 0)] = t_bc.n
            v_bca.wait(tensor, 2)
            for half in range(2):
                i = tensor.matmul(st2_ps[:, half * 512:(half + 1) * 512],
                                  bc_lhs[32:34, :],
                                  bc_rhs[32:34, half * 512:(half + 1) * 512],
                                  start=True, stop=True)
            t_bc.inc(i, 2)
            mk_bc[("pa", 1)] = t_bc.n
            for ec in range(NDC):
                v_x1b.wait(tensor, ec + 1)
                for half in range(2):
                    tensor.matmul(st2_ps[0:1, half * 512:(half + 1) * 512],
                                  ones_bf[:],
                                  x1b[:, ec % 2, half * 512:(half + 1) * 512],
                                  start=(ec == 0), stop=(ec == NDC - 1))
                s_sq3.wait(tensor, ec + 1)
                for half in range(2):
                    i = tensor.matmul(st2_ps[32:33, half * 512:(half + 1) * 512],
                                      ones_bf[:],
                                      sqc[:, ec % 2, half * 512:(half + 1) * 512],
                                      start=(ec == 0), stop=(ec == NDC - 1))
                t_stq3.inc(i)
            v_rowb.wait(tensor, 3)
            for half in range(2):
                i = tensor.matmul(st2_ps[:, half * 512:(half + 1) * 512],
                                  bc_lhs[0:2, :], bc_rhs[0:2, half * 512:(half + 1) * 512],
                                  start=True, stop=True)
            t_bc.inc(i, 2)
            mk_bc[("l2", 0)] = t_bc.n
            v_bca.wait(tensor, 3)
            for half in range(2):
                i = tensor.matmul(st2_ps[:, half * 512:(half + 1) * 512],
                                  bc_lhs[32:34, :],
                                  bc_rhs[32:34, half * 512:(half + 1) * 512],
                                  start=True, stop=True)
            t_bc.inc(i, 2)
            mk_bc[("l2", 1)] = t_bc.n
            bar_end(tensor)

        def rows_finalize(vector, st_psrc, pre_ch, sqr_ch, sqr_target):
            inv = 1.0 / D
            vector.tensor_scalar(out=rows2[0:1, :], in0=st_psrc[0:1, :],
                                 scalar1=inv, scalar2=None, op0=OP.mult)
            vector.tensor_scalar(out=rows[32:33, :], in0=st_psrc[32:33, :],
                                 scalar1=inv, scalar2=EPS, op0=OP.mult, op1=OP.add)
            vector.tensor_tensor(out=rows2[32:33, :], in0=rows2[0:1, :],
                                 in1=rows2[0:1, :], op=OP.mult)
            vector.tensor_tensor(out=rows[32:33, :], in0=rows[32:33, :],
                                 in1=rows2[32:33, :], op=OP.subtract)
            pre_ch.inc(vector.reciprocal(out=rows2[32:33, :],
                                         in_=rows[32:33, :]))
            sqr_ch.wait(vector, sqr_target)
            vector.tensor_tensor(out=rows[32:33, :], in0=rows2[0:1, :],
                                 in1=rows[0:1, :], op=OP.mult)
            vector.tensor_copy(out=bc_rhs[0:1, :], in_=rows[0:1, :])
            return vector.tensor_copy(out=bc_rhs[32:33, :],
                                      in_=rows[32:33, :])

        @blk.vector
        def _(vector):
            vector = Drained(vector)
            bar_wait(vector, 2)
            for ec in range(NDC + 2):
                if ec < NDC:
                    d_wp[ec % 2].wait(vector, mk_w[("p3", ec)])
                    if ec >= 2:
                        t_pj.wait(vector, ec - 1)
                    v_w3.inc(vector.tensor_copy(out=wbt3[:, ec % 2, :, :],
                                                in_=wst3[:, ec % 2, :, :]))
                if ec >= 2:
                    e0 = ec - 2
                    t_pj.wait(vector, e0 + 1)
                    if e0 >= 2:
                        d_pjp[e0 % 2].wait(vector, mk_pj[e0 - 2])
                        t_stq2.wait(vector, e0 - 1)
                    v_pe.inc(vector.tensor_copy(out=pjt[:, e0 % 2, :],
                                                in_=pj_ps[:, e0 % 2, :]))
            t_stq2.wait(vector, NDC)
            v_rowb.inc(rows_finalize(vector, st2_ps, v_rowpre, s_sqr, 2))
            t_bc.wait(vector, mk_bc[("pa", 0)])
            v_bca.inc(vector.tensor_copy(out=bc_a[:], in_=st2_ps[:]))
            t_bc.wait(vector, mk_bc[("pa", 1)])
            v_bcb.inc(vector.tensor_copy(out=bc_b[:], in_=st2_ps[:]))
            for ec in range(NDC):
                d_xp[ec % 2].wait(vector, mk_x[("p3", ec)])
                d_ldtp[ec % 2].wait(vector, mk_ldt[ec])
                if ec >= 2:
                    d_x1p[ec % 2].wait(vector, mk_x1[ec - 2])
                    t_stq3.wait(vector, ec - 1)
                vector.tensor_tensor(out=tmp_f[:], in0=ldt[:, ec % 2, :],
                                     in1=bc_a[:], op=OP.mult)
                vector.tensor_tensor(out=tmp_f[:], in0=tmp_f[:],
                                     in1=bc_b[:], op=OP.subtract)
                v_x1.inc(vector.tensor_tensor(out=x1t[:, ec % 2, :], in0=tmp_f[:],
                                              in1=xf[:, ec % 2, :], op=OP.add))
                v_x1b.inc(vector.tensor_copy(out=x1b[:, ec % 2, :],
                                             in_=x1t[:, ec % 2, :]))
            t_stq3.wait(vector, NDC)
            v_rowb.inc(rows_finalize(vector, st2_ps, v_rowpre, s_sqr, 3))
            t_bc.wait(vector, mk_bc[("l2", 0)])
            v_bca.inc(vector.tensor_copy(out=bc_a[:], in_=st2_ps[:]))
            t_bc.wait(vector, mk_bc[("l2", 1)])
            v_bcb.inc(vector.tensor_copy(out=bc_b[:], in_=st2_ps[:]))
            for ec in range(NDC):
                d_lxp[ec % 2].wait(vector, mk_lx[ec])
                vector.tensor_tensor(out=tmp_f[:], in0=xf[:, ec % 2, :],
                                     in1=bc_a[:], op=OP.mult)
                v_l2.inc(vector.tensor_tensor(out=x_bf[:, ec, :], in0=tmp_f[:],
                                              in1=bc_b[:], op=OP.subtract))
            bar_end(vector)

        p3s.close()
        p3p.close()
        scA.close()

        if stage == 3:
            @blk.sync
            def _(sync):
                bar_wait(sync, 3)
                d_dbg.dinc(sync.dma_start(out=dbg["d_x1"][:], in_=x1_sc[:]))
                d_dbg.dinc(sync.dma_start(out=dbg["d_ln2"].rearrange(
                    "(c p) t -> p c t", p=128), in_=x_bf[:]))
                d_dbg.dinc(sync.dma_start(out=dbg["d_proj"][:], in_=proj_sc[:]))
                d_dbg.dinc(sync.dma_start(out=dbg["d_rows"][0:1, :],
                                          in_=bc_a[0:1, :]))
                d_dbg.dinc(sync.dma_start(out=dbg["d_rows"][1:2, :],
                                          in_=bc_b[0:1, :]))
                d_outp[0].dinc(sync.dma_start(out=outT[0:128, :], in_=xf[:, 0, :]))
                d_dbg.wait(sync)
                d_outp[0].wait(sync)
            return nc, dbg, ctx

        p4s = contextlib.ExitStack()
        h_bf = p4s.enter_context(nc.sbuf_tensor("h_bf", [128, NIC, T], BF16))
        gt_st = p4s.enter_context(nc.sbuf_tensor("gt_st", [128, 4, 128], BF16))
        up_st = p4s.enter_context(nc.sbuf_tensor("up_st", [128, 4, 128], BF16))
        sil_t = p4s.enter_context(nc.sbuf_tensor("sil_t", [128, T], BF16))
        p4p = contextlib.ExitStack()
        g_ps = p4p.enter_context(nc.psum_tensor("g_ps", [128, 2, 1024], F32))
        u_ps = p4p.enter_context(nc.psum_tensor("u_ps", [128, 2, 1024], F32))

        d_gup = [d_xp[0], d_xp[1], d_lxp[0], d_lxp[1]]
        mk_gu = {}

        @blk.sync
        def _(sync):
            bar_wait(sync, 3)
            g_cc.wait(sync, 3)
            for ic in range(NIC):
                for dc in range(NDC):
                    un = ic * NDC + dc
                    if un >= 4:
                        t_guu.wait(sync, un - 3)
                    r, half = divmod(dc, 2)
                    mk_gw[un] = d_gwp[un % 4].dinc(sync.dma_start(
                        out=gt_st[:, un % 4, :],
                        in_=gh_out_g(r)[half * 128:(half + 1) * 128,
                                        ic * 128:(ic + 1) * 128]))
            bar_end(sync)

        @blk.scalar
        def _(scalar):
            scalar = Drained(scalar)
            bar_wait(scalar, 3)
            g_cc.wait(scalar, 3)
            for ic in range(NIC + 1):
                if ic < NIC:
                    for dc in range(NDC):
                        un = ic * NDC + dc
                        if un >= 4:
                            t_guu.wait(scalar, un - 3)
                        r, half = divmod(dc, 2)
                        mk_gu[un] = d_gup[un % 4].dinc(scalar.dma_start(
                            out=up_st[:, un % 4, :],
                            in_=gh_out_u(r)[half * 128:(half + 1) * 128,
                                            ic * 128:(ic + 1) * 128]))
                if ic >= 1:
                    t_guu.wait(scalar, ic * NDC)
                    if ic >= 2:
                        v_h.wait(scalar, ic - 1)
                    s_si.inc(scalar.activation(out=sil_t[:],
                                               in_=g_ps[:, (ic - 1) % 2, :],
                                               func=AF.Silu))
            bar_end(scalar)

        @blk.tensor
        def _(tensor):
            bar_wait(tensor, 3)
            for ic in range(NIC):
                if ic >= 2:
                    v_h.wait(tensor, ic - 1)
                for dc in range(NDC):
                    un = ic * NDC + dc
                    d_gwp[un % 4].wait(tensor, mk_gw[un])
                    d_gup[un % 4].wait(tensor, mk_gu[un])
                    for half in range(2):
                        tensor.matmul(
                            g_ps[:, ic % 2, half * 512:(half + 1) * 512],
                            gt_st[:, un % 4, :],
                            x_bf[:, dc, half * 512:(half + 1) * 512],
                            start=(dc == 0), stop=(dc == NDC - 1))
                    for half in range(2):
                        i = tensor.matmul(
                            u_ps[:, ic % 2, half * 512:(half + 1) * 512],
                            up_st[:, un % 4, :],
                            x_bf[:, dc, half * 512:(half + 1) * 512],
                            start=(dc == 0), stop=(dc == NDC - 1))
                    t_guu.inc(i)
            bar_end(tensor)

        @blk.vector
        def _(vector):
            vector = Drained(vector)
            bar_wait(vector, 3)
            for ic in range(NIC):
                s_si.wait(vector, ic + 1)
                v_h.inc(vector.tensor_tensor(out=h_bf[:, ic, :], in0=sil_t[:],
                                             in1=u_ps[:, ic % 2, :], op=OP.mult))
            bar_end(vector)

        p4p.close()

        if stage == 4:
            @blk.sync
            def _(sync):
                bar_wait(sync, 4)
                d_dbg.dinc(sync.dma_start(out=dbg["d_gw"][:], in_=gh_in[:]))
                d_dbg.dinc(sync.dma_start(out=dbg["d_h"].rearrange(
                    "(c p) t -> p c t", p=128), in_=h_bf[:]))
                d_dbg.dinc(sync.dma_start(out=dbg["d_x1"][:], in_=x1_sc[:]))
                d_dbg.dinc(sync.dma_start(out=dbg["d_ln2"].rearrange(
                    "(c p) t -> p c t", p=128), in_=x_bf[:]))
                d_outp[0].dinc(sync.dma_start(out=outT[0:128, :], in_=xf[:, 0, :]))
                d_dbg.wait(sync)
                d_outp[0].wait(sync)
            p4s.close()
            return nc, dbg, ctx

        p5s = contextlib.ExitStack()
        dn_st = p5s.enter_context(nc.sbuf_tensor("dn_st", [128, 4, 256], BF16))
        dnt = p5s.enter_context(nc.sbuf_tensor("dnt", [128, 2, T], BF16))
        outst = p5s.enter_context(nc.sbuf_tensor("outst", [128, 2, T], F32))
        x1t5 = p5s.enter_context(nc.sbuf_tensor("x1t5", [128, 2, T], F32))
        p5p = contextlib.ExitStack()
        dn_ps = p5p.enter_context(nc.psum_tensor("dn_ps", [128, 2, 1024], F32))
        st3_ps = p5p.enter_context(nc.psum_tensor("st3_ps", [128, 1024], F32))

        @blk.scalar
        def _(scalar):
            scalar = Drained(scalar)
            bar_wait(scalar, 4)
            for ec in range(NDC):
                v_de.wait(scalar, ec + 1)
                if ec >= 2:
                    t_stq4.wait(scalar, ec - 1)
                s_sq4.inc(scalar.square(out=sqc[:, ec % 2, :], in_=dnt[:, ec % 2, :]))
                mk_dn[ec] = d_dnp[ec % 2].dinc(scalar.dma_start(
                    out=down_sc[ec * 128:(ec + 1) * 128, :], in_=dnt[:, ec % 2, :]))
            v_rowpre.wait(scalar, 4)
            s_sqr.inc(scalar.sqrt(out=rows[0:1, :], in_=rows2[32:33, :]))

        @blk.sync
        def _(sync):
            bar_wait(sync, 4)
            g_cc.wait(sync, 4)
            for ep in range(8):
                for ic in range(NIC):
                    un = ep * NIC + ic
                    if un >= 4:
                        t_dnu.wait(sync, un - 3)
                    mk_gw[("dn", un)] = d_gwp[un % 4].dinc(sync.dma_start(
                        out=dn_st[:, un % 4, :],
                        in_=gh_out_d(ep)[ic * 128:(ic + 1) * 128, :]))
            d_dnp[0].wait(sync)
            d_dnp[1].wait(sync)
            d_x1p[0].wait(sync)
            d_x1p[1].wait(sync)
            t_stq4.wait(sync, 16)
            for ec in range(NDC + 1):
                if ec < NDC:
                    if ec >= 2:
                        v_out.wait(sync, ec - 1)
                    d_l5p[ec % 2].dinc(sync.dma_start(
                        out=x1t5[:, ec % 2, :],
                        in_=x1_sc[ec * 128:(ec + 1) * 128, :]))
                    mk_l5[ec] = d_l5p[ec % 2].dinc(sync.dma_start(
                        out=dnt[:, ec % 2, :],
                        in_=down_sc[ec * 128:(ec + 1) * 128, :]))
                if ec >= 1:
                    e1 = ec - 1
                    v_out.wait(sync, e1 + 1)
                    mk_out[e1] = d_outp[e1 % 2].dinc(sync.dma_start(
                        out=outT[e1 * 128:(e1 + 1) * 128, :],
                        in_=outst[:, e1 % 2, :]))
            d_outp[0].wait(sync)
            d_outp[1].wait(sync)

        @blk.tensor
        def _(tensor):
            bar_wait(tensor, 4)
            for ep in range(8):
                if ep >= 1:
                    v_de.wait(tensor, 2 * ep)
                for ic in range(NIC):
                    un = ep * NIC + ic
                    d_gwp[un % 4].wait(tensor, mk_gw[("dn", un)])
                    for ecl in range(2):
                        for half in range(2):
                            i = tensor.matmul(
                                dn_ps[:, ecl, half * 512:(half + 1) * 512],
                                dn_st[:, un % 4, ecl * 128:(ecl + 1) * 128],
                                h_bf[:, ic, half * 512:(half + 1) * 512],
                                start=(ic == 0), stop=(ic == NIC - 1))
                    t_dnu.inc(i)
                for ecl in range(2):
                    ec = 2 * ep + ecl
                    v_de.wait(tensor, ec + 1)
                    for half in range(2):
                        tensor.matmul(st3_ps[0:1, half * 512:(half + 1) * 512],
                                      ones_bf[:],
                                      dnt[:, ec % 2, half * 512:(half + 1) * 512],
                                      start=(ec == 0), stop=(ec == NDC - 1))
                    s_sq4.wait(tensor, ec + 1)
                    for half in range(2):
                        i = tensor.matmul(st3_ps[32:33, half * 512:(half + 1) * 512],
                                          ones_bf[:],
                                          sqc[:, ec % 2, half * 512:(half + 1) * 512],
                                          start=(ec == 0), stop=(ec == NDC - 1))
                    t_stq4.inc(i)
            v_rowb.wait(tensor, 4)
            for half in range(2):
                i = tensor.matmul(st3_ps[:, half * 512:(half + 1) * 512],
                                  bc_lhs[0:2, :], bc_rhs[0:2, half * 512:(half + 1) * 512],
                                  start=True, stop=True)
            t_bc.inc(i, 2)
            mk_bc[("l3", 0)] = t_bc.n
            v_bca.wait(tensor, 4)
            for half in range(2):
                i = tensor.matmul(st3_ps[:, half * 512:(half + 1) * 512],
                                  bc_lhs[32:34, :],
                                  bc_rhs[32:34, half * 512:(half + 1) * 512],
                                  start=True, stop=True)
            t_bc.inc(i, 2)
            mk_bc[("l3", 1)] = t_bc.n

        @blk.vector
        def _(vector):
            vector = Drained(vector)
            bar_wait(vector, 4)
            for ep in range(8):
                t_dnu.wait(vector, (ep + 1) * NIC)
                for ecl in range(2):
                    ec = 2 * ep + ecl
                    if ec >= 2:
                        t_stq4.wait(vector, ec - 1)
                        d_dnp[ec % 2].wait(vector, mk_dn[ec - 2])
                    v_de.inc(vector.tensor_copy(out=dnt[:, ec % 2, :],
                                                in_=dn_ps[:, ecl, :]))
            t_stq4.wait(vector, NDC)
            v_rowb.inc(rows_finalize(vector, st3_ps, v_rowpre, s_sqr, 4))
            t_bc.wait(vector, mk_bc[("l3", 0)])
            v_bca.inc(vector.tensor_copy(out=bc_a[:], in_=st3_ps[:]))
            t_bc.wait(vector, mk_bc[("l3", 1)])
            v_bcb.inc(vector.tensor_copy(out=bc_b[:], in_=st3_ps[:]))
            for ec in range(NDC):
                d_l5p[ec % 2].wait(vector, mk_l5[ec])
                if ec >= 2:
                    d_outp[ec % 2].wait(vector, mk_out[ec - 2])
                vector.tensor_tensor(out=tmp_f[:], in0=dnt[:, ec % 2, :],
                                     in1=bc_a[:], op=OP.mult)
                vector.tensor_tensor(out=tmp_f[:], in0=tmp_f[:],
                                     in1=bc_b[:], op=OP.subtract)
                v_out.inc(vector.tensor_tensor(out=outst[:, ec % 2, :],
                                               in0=tmp_f[:], in1=x1t5[:, ec % 2, :],
                                               op=OP.add))

        p5p.close()
        p5s.close()
        p4s.close()
        return nc, dbg, ctx

    raise AssertionError("unreachable")


_STAGE = 5


def kernel(L0_attn_x, L0_attn_w_qkv, L0_attn_w_proj, L0_ffn_lut,
           L0_ffn_gate_idx, L0_ffn_up_idx, L0_ffn_down_idx):
    x = np.asarray(L0_attn_x)[0]
    xTf = np.ascontiguousarray(x.T.astype(np.float32, copy=False))
    wqkvT = np.ascontiguousarray(np.asarray(L0_attn_w_qkv).T.astype(np.float32, copy=False))
    wprojT = np.ascontiguousarray(np.asarray(L0_attn_w_proj).T.astype(np.float32, copy=False))
    lutb = np.ascontiguousarray(np.broadcast_to(
        np.asarray(L0_ffn_lut).astype(np.float32, copy=False)[None, :], (128, 16)))
    gidxT = np.ascontiguousarray(np.asarray(L0_ffn_gate_idx).T)
    uidxT = np.ascontiguousarray(np.asarray(L0_ffn_up_idx).T)
    didxT = np.ascontiguousarray(np.asarray(L0_ffn_down_idx).T)

    nc, dbg, ctx = build_nc(stage=_STAGE)
    in_maps = []
    for c in range(NCORES):
        in_maps.append({
            "xT": np.ascontiguousarray(xTf[:, c * T:(c + 1) * T]),
            "wqkvT": wqkvT,
            "wprojT": wprojT,
            "lutb": lutb,
            "gidxT": np.ascontiguousarray(gidxT[c * 256:(c + 1) * 256, :]),
            "uidxT": np.ascontiguousarray(uidxT[c * 256:(c + 1) * 256, :]),
            "didxT": np.ascontiguousarray(didxT[:, c * 256:(c + 1) * 256]),
        })
    res = run_bass_kernel_spmd(nc, in_maps, core_ids=list(range(NCORES)))
    kernel.last_results = res
    outTs = [res.results[c]["outT"] for c in range(NCORES)]
    out = np.concatenate(outTs, axis=1).T[None, :, :]
    return np.ascontiguousarray(out.astype(np.float32, copy=False))
